# version 15
# baseline (speedup 1.0000x reference)
import contextlib
import numpy as np
import concourse.bass as bass
import concourse.mybir as mybir
from concourse.bass_utils import run_bass_kernel_spmd

F32 = mybir.dt.float32
BF16 = mybir.dt.bfloat16
I32 = mybir.dt.int32
AF = mybir.ActivationFunctionType
ALU = mybir.AluOpType
AXX = mybir.AxisListType.X

P = 128
D = 1024
ND = 8
NSEQ = 4
TSEQ = 2048
TB = 1024
NB = 8
NSUB = 2
SUB = 512
NTT = 4
NM_G = 16
NM_MLP = 32
NM_FFN = 24
EPS = 1e-5
QS = 1.0 / 127.0
INV_D = 1.0 / 1024.0


class Prog:
    def __init__(self):
        self.h = {}
        self.c = {}
        self.waited = {}

    def add_sem(self, name, handle):
        self.h[name] = handle
        self.c[name] = 0

    def inc(self, instr, sem, n=1):
        instr.then_inc(self.h[sem], n)
        self.c[sem] += n
        return (sem, self.c[sem])

    def mark(self, sem):
        return (sem, self.c[sem])

    def wait(self, eng, engname, m):
        if m is None:
            return
        sem, v = m
        if v <= 0:
            return
        key = (engname, sem)
        if self.waited.get(key, -1) >= v:
            return
        eng.wait_ge(self.h[sem], v)
        self.waited[key] = v

    def waits(self, eng, engname, marks):
        for m in marks:
            self.wait(eng, engname, m)

    def init_dmaq(self, qname, semnames):
        if not hasattr(self, "q"):
            self.q = {}
        self.q[qname] = {"sems": list(semnames), "rr": 0}

    def dma(self, eng, engname, qname, out, in_):
        q = self.q[qname]
        i = q["rr"]
        q["rr"] = (i + 1) % len(q["sems"])
        s = q["sems"][i]
        self.wait(eng, engname, (s, self.c[s]))
        return self.inc(eng.dma_start(out=out, in_=in_), s, 16)


def build_program(n_blocks=NB):
    nc = bass.Bass("TRN2", target_bir_lowering=False, debug=False)

    x_in = nc.dram_tensor("x_in", [NB, NSUB, P, NTT, D], F32, kind="ExternalInput").ap()
    y_out = nc.dram_tensor("y_out", [NB, NSUB, P, NTT, D], F32, kind="ExternalOutput").ap()

    w_gate = nc.dram_tensor("w_gate", [P, NM_G, ND, P], F32, kind="ExternalInput").ap()
    w_state = nc.dram_tensor("w_state", [P, ND, ND, P], F32, kind="ExternalInput").ap()
    w_outp = nc.dram_tensor("w_outp", [P, ND, ND, P], F32, kind="ExternalInput").ap()
    w_mlp1 = nc.dram_tensor("w_mlp1", [P, NM_MLP, ND, P], F32, kind="ExternalInput").ap()
    w_mlp2 = nc.dram_tensor("w_mlp2", [P, ND, NM_MLP, P], F32, kind="ExternalInput").ap()
    w_fg = nc.dram_tensor("w_fg", [P, NM_FFN, ND, P], I32, kind="ExternalInput").ap()
    w_fu = nc.dram_tensor("w_fu", [P, NM_FFN, ND, P], I32, kind="ExternalInput").ap()
    w_fd = nc.dram_tensor("w_fd", [P, ND, NM_FFN, P], I32, kind="ExternalInput").ap()

    c_gate_b = nc.dram_tensor("c_gate_b", [P, NM_G], F32, kind="ExternalInput").ap()
    c_state_b = nc.dram_tensor("c_state_b", [P, ND], F32, kind="ExternalInput").ap()
    c_out_b = nc.dram_tensor("c_out_b", [P, ND], F32, kind="ExternalInput").ap()
    c_mlp_b1 = nc.dram_tensor("c_mlp_b1", [P, NM_MLP], F32, kind="ExternalInput").ap()
    c_mlp_b2 = nc.dram_tensor("c_mlp_b2", [P, ND], F32, kind="ExternalInput").ap()
    c_n1w = nc.dram_tensor("c_n1w", [P, ND], F32, kind="ExternalInput").ap()
    c_n1b = nc.dram_tensor("c_n1b", [P, ND], F32, kind="ExternalInput").ap()
    c_n2w = nc.dram_tensor("c_n2w", [P, ND], F32, kind="ExternalInput").ap()
    c_n2b = nc.dram_tensor("c_n2b", [P, ND], F32, kind="ExternalInput").ap()
    r_state_b = nc.dram_tensor("r_state_b", [1, D], F32, kind="ExternalInput").ap()
    c_ident = nc.dram_tensor("c_ident", [P, P], F32, kind="ExternalInput").ap()
    c_ones = nc.dram_tensor("c_ones", [1, P], F32, kind="ExternalInput").ap()

    s_gate = nc.dram_tensor("s_gate", [P, NM_G, ND, P], BF16, kind="Internal").ap()
    s_state = nc.dram_tensor("s_state", [P, ND, ND, P], BF16, kind="Internal").ap()
    s_outp = nc.dram_tensor("s_outp", [P, ND, ND, P], BF16, kind="Internal").ap()
    s_mlp1 = nc.dram_tensor("s_mlp1", [P, NM_MLP, ND, P], BF16, kind="Internal").ap()
    s_mlp2 = nc.dram_tensor("s_mlp2", [P, ND, NM_MLP, P], BF16, kind="Internal").ap()
    s_fg = nc.dram_tensor("s_fg", [P, NM_FFN, ND, P], BF16, kind="Internal").ap()
    s_fu = nc.dram_tensor("s_fu", [P, NM_FFN, ND, P], BF16, kind="Internal").ap()
    s_fd = nc.dram_tensor("s_fd", [P, ND, NM_FFN, P], BF16, kind="Internal").ap()

    pr = Prog()

    with contextlib.ExitStack() as es:
        E = es.enter_context

        X1 = E(nc.sbuf_tensor("X1", [P, 8192], F32))
        FMB = E(nc.sbuf_tensor("FMB", [P, 8192], BF16))
        TTR = E(nc.sbuf_tensor("TTR", [P, 8192], F32))
        T1 = TTR[:, 0:4096]
        T2 = TTR[:, 4096:8192]
        M1R = E(nc.sbuf_tensor("M1R", [P, 16384], BF16))
        M1 = M1R
        W_MIX = M1R
        REG = E(nc.sbuf_tensor("REG", [P, 20480], F32))
        Z = REG[:, 0:4096].bitcast(BF16)
        B1 = REG[:, 4096:8192].bitcast(BF16)
        B2 = REG[:, 8192:12288].bitcast(BF16)
        B3 = REG[:, 12288:16384].bitcast(BF16)
        SCBC = REG[:, 16384:20480]
        W_BIG = REG[:, :].bitcast(BF16)
        A_PRO = REG[:, 0:1024]
        A_FS = REG[:, 9216:10240]
        A_FFNO = REG[:, 10240:14336]
        A_RING = TTR[:, 0:4096].bitcast(BF16)

        def small(name, shape, dtype):
            return E(nc.sbuf_tensor(name, shape, dtype))

        IDF = small("IDF", [P, P], F32)
        ONES = small("ONES", [1, P], F32)
        CB_G = small("CB_G", [P, NM_G], F32)
        CB_S = small("CB_S", [P, ND], F32)
        CB_O = small("CB_O", [P, ND], F32)
        CB_M1 = small("CB_M1", [P, NM_MLP], F32)
        CB_M2 = small("CB_M2", [P, ND], F32)
        CN1W = small("CN1W", [P, ND], F32)
        CN1B = small("CN1B", [P, ND], F32)
        CN2W = small("CN2W", [P, ND], F32)
        CN2B = small("CN2B", [P, ND], F32)
        WMS = small("WMS", [P, ND], BF16)
        HCAR = small("HCAR", [P, ND], BF16)
        BN6 = small("BN6", [P, 96], F32)
        BN2 = small("BN2", [P, 16], F32)
        MV = small("MV", [P, 32], F32)
        ZS = small("ZS", [P, 16], F32)
        RS = small("RS", [P, 8], F32)
        WSA = small("WSA", [P, 8], F32)
        BSB = small("BSB", [1, 4], F32)
        MAGIC = small("MAGIC", [P, 4], I32)
        NEWT = small("NEWT", [P, 8], F32)
        BN6A = small("BN6A", [P, 96], F32)
        BN2A = small("BN2A", [P, 16], F32)
        MVA = small("MVA", [P, 32], F32)
        R0 = small("R0", [1, 1024], F32)
        R1 = small("R1", [1, 1024], F32)
        R2 = small("R2", [1, 1024], F32)

        A_BC = SCBC[:, 0:1024]
        D_BC = SCBC[:, 2048:3072]
        BCH = [SCBC[:, 1024:2048], SCBC[:, 3072:4096]]

        psum = [E(nc.psum_tensor(f"pb{i}", [P, 512], F32)) for i in range(8)]
        bank_fill = [None] * 8
        bank_free = [None] * 8

        for s_ in (["pe", "act", "dve"]
                   + [f"ql{i}" for i in range(8)] + [f"qs{i}" for i in range(4)]):
            pr.add_sem(s_, E(nc.semaphore(s_)))
        pr.init_dmaq("ql", [f"ql{i}" for i in range(8)])
        pr.init_dmaq("qs", [f"qs{i}" for i in range(4)])
        block = E(nc.Block())

        def sync_sec(f):
            @block.sync
            def _(e):
                f(e)

        def gp_sec(f):
            @block.gpsimd
            def _(e):
                f(e)

        def pe_sec(f):
            @block.tensor
            def _(e):
                f(e)

        def act_sec(f):
            @block.scalar
            def _(e):
                f(e)

        def dve_sec(f):
            @block.vector
            def _(e):
                f(e)

        EV = {"act": act_sec, "dve": dve_sec}
        CM = {}

        def prologue():
            def f(sy):
                for name, dst, src in [
                    ("ident", IDF[:], c_ident[:, :]), ("ones", ONES[:], c_ones[:, :]),
                    ("gb", CB_G[:], c_gate_b[:, :]), ("sb", CB_S[:], c_state_b[:, :]),
                    ("ob", CB_O[:], c_out_b[:, :]), ("m1b", CB_M1[:], c_mlp_b1[:, :]),
                    ("m2b", CB_M2[:], c_mlp_b2[:, :]),
                    ("n1w", CN1W[:], c_n1w[:, :]), ("n1b", CN1B[:], c_n1b[:, :]),
                    ("n2w", CN2W[:], c_n2w[:, :]), ("n2b", CN2B[:], c_n2b[:, :]),
                    ("sbr", R2[0:1, :], r_state_b[:, :]),
                ]:
                    CM[name] = pr.dma(sy, "sync", "ql", dst, src)
            sync_sec(f)

            def g(gp):
                prev = None
                for name, dst, src in [
                    ("gate", s_gate, w_gate), ("state", s_state, w_state),
                    ("outp", s_outp, w_outp), ("mlp1", s_mlp1, w_mlp1),
                    ("mlp2", s_mlp2, w_mlp2), ("fg", s_fg, w_fg),
                    ("fu", s_fu, w_fu), ("fd", s_fd, w_fd),
                ]:
                    if prev is not None:
                        pr.wait(gp, "pool", prev)
                    CM["cast_" + name] = pr.dma(
                        gp, "pool", "qs",
                        dst.rearrange("p m k f -> p (m k f)"),
                        src.rearrange("p m k f -> p (m k f)"))
                    prev = CM["cast_" + name]
            gp_sec(g)

            SW = REG[:, 8192:16384]
            SWR = REG[:, 16384:16448]

            def fl(sy):
                CM["wsm"] = pr.dma(sy, "sync", "ql", SW,
                                   w_state.rearrange("p m k f -> p (m k f)"))
            sync_sec(fl)

            def vl(dv):
                pr.wait(dv, "dve", CM["wsm"])
                rm = pr.inc(dv.tensor_reduce(
                    SWR[:, 0:64],
                    SW.rearrange("p (mk f) -> p mk f", f=P),
                    AXX, ALU.add), "dve", 1)
                pr.wait(dv, "dve", rm)
                mm = pr.inc(dv.tensor_copy(WSA[:], SWR[:, 0:ND]), "dve", 1)
                for m in range(1, ND):
                    pr.wait(dv, "dve", mm)
                    mm = pr.inc(dv.tensor_add(WSA[:], WSA[:],
                                              SWR[:, m * ND:(m + 1) * ND]),
                                "dve", 1)
                pr.wait(dv, "dve", mm)
                CM["wsum"] = mm
            dve_sec(vl)

            def vf(dv):
                mgm = pr.inc(dv.memset(MAGIC[:], 0x5f3759df), "dve", 1)
                pr.wait(dv, "dve", mgm)
                CM["wms"] = pr.inc(
                    dv.tensor_scalar(WMS[:], WSA[:], INV_D, None, ALU.mult), "dve", 1)
                pr.wait(dv, "dve", CM["sbr"])
                mm = pr.inc(dv.tensor_reduce(BSB[0:1, 0:1], R2[0:1, :], AXX, ALU.add),
                            "dve", 1)
                pr.wait(dv, "dve", mm)
                CM["bsb"] = pr.inc(
                    dv.tensor_scalar(BSB[0:1, 1:2], BSB[0:1, 0:1], INV_D, None, ALU.mult),
                    "dve", 1)
            dve_sec(vf)

        prologue()

        reg_free = {
            "T1": [], "T2": [], "Z": [], "B1": [],
            "B2": [CM["wsum"]], "B3": [CM["wsum"]],
            "FMB": [], "M1": [], "SCBC": [CM["wms"]], "X1": [],
        }
        CONST_MARKS = [CM[n] for n in (
            "ident", "ones", "gb", "sb", "ob", "m1b", "m2b",
            "n1w", "n1b", "n2w", "n2b", "sbr")]

        def rsqrt_newton(dv, dst, src):
            dsti = dst.bitcast(I32)
            srci = src.bitcast(I32)
            n = dst.shape[1]
            sh = NEWT[:, 0:n].bitcast(I32)
            tq = NEWT[:, 4:4 + n]
            m0 = pr.inc(dv.tensor_scalar(sh, srci, 1, None,
                                         ALU.logical_shift_right), "dve", 1)
            pr.wait(dv, "dve", m0)
            m0 = pr.inc(dv.tensor_tensor(dsti, MAGIC[:, 0:n], sh, ALU.subtract),
                        "dve", 1)
            pr.wait(dv, "dve", m0)
            for _ in range(3):
                m0 = pr.inc(dv.tensor_mul(tq, dst, dst), "dve", 1)
                pr.wait(dv, "dve", m0)
                m0 = pr.inc(dv.tensor_mul(tq, tq, src), "dve", 1)
                pr.wait(dv, "dve", m0)
                m0 = pr.inc(dv.tensor_scalar(tq, tq, -0.5, 1.5, ALU.mult, ALU.add),
                            "dve", 1)
                pr.wait(dv, "dve", m0)
                m0 = pr.inc(dv.tensor_mul(dst, dst, tq), "dve", 1)
                pr.wait(dv, "dve", m0)
            return m0

        def mm_stage(wsb, w_tile, n_m, n_k, rhs_fn, eng, dst_fn, w_ready,
                     func=AF.Identity, bias_col=None, scale=1.0, pre_waits=(),
                     extra_pe_waits=(), banks=(0, 1, 2, 3), subs=(0, 1),
                     group_waits=None, m_order=None):
            sem = eng
            evac_base = pr.c[sem]
            nbk = len(banks)
            idx = [(s, m) for s in subs
                   for m in (m_order if m_order is not None else range(n_m))]
            fills = []

            def fpe(t):
                if not callable(w_ready):
                    pr.wait(t, "pe", w_ready)
                for mk in extra_pe_waits:
                    pr.wait(t, "pe", mk)
                for i, (s, m) in enumerate(idx):
                    if callable(w_ready):
                        pr.wait(t, "pe", w_ready(m))
                    if group_waits is not None:
                        pr.waits(t, "pe", group_waits(s, m))
                    bank = banks[i % nbk]
                    if i >= nbk:
                        pr.wait(t, "pe", (sem, evac_base + i - nbk + 1))
                    else:
                        pr.wait(t, "pe", bank_free[bank])
                    ins = None
                    for k in range(n_k):
                        ins = t.matmul(
                            psum[bank][:, 0:512],
                            wsb[:, w_tile(m, k) * P:(w_tile(m, k) + 1) * P],
                            rhs_fn(s, k), start=(k == 0), stop=(k == n_k - 1))
                    fills.append(pr.inc(ins, "pe", 1))
            pe_sec(fpe)

            marks = []

            def fev(e):
                pr.waits(e, eng, CONST_MARKS)
                for mk in pre_waits:
                    pr.wait(e, eng, mk)
                for i, (s, m) in enumerate(idx):
                    bank = banks[i % nbk]
                    pr.wait(e, eng, fills[i])
                    if eng == "act":
                        ins = e.activation(dst_fn(s, m), psum[bank][:, 0:512], func,
                                           bias=(bias_col(m) if bias_col else 0.0),
                                           scale=scale)
                    else:
                        if bias_col is not None:
                            ins = e.tensor_scalar(dst_fn(s, m), psum[bank][:, 0:512],
                                                  bias_col(m), None, ALU.add)
                        elif scale != 1.0:
                            ins = e.tensor_scalar(dst_fn(s, m), psum[bank][:, 0:512],
                                                  scale, None, ALU.mult)
                        else:
                            ins = e.tensor_copy(dst_fn(s, m), psum[bank][:, 0:512])
                    mk = pr.inc(ins, eng, 1)
                    bank_free[bank] = mk
                    marks.append(mk)
            EV[eng](fev)
            assert pr.c[sem] == evac_base + len(idx)
            return marks

        def tm_to_fm_pe(src, src_ready, eng, banks=(4, 5)):
            sem = eng
            base = pr.c[sem]
            fills = []

            def fpe(t):
                for mk in src_ready:
                    pr.wait(t, "pe", mk)
                pr.wait(t, "pe", CM["ident"])
                for c in range(ND):
                    bank = banks[c % 2]
                    if c >= 2:
                        pr.wait(t, "pe", (sem, base + c - 1))
                    else:
                        pr.wait(t, "pe", bank_free[bank])
                    ins = None
                    for tau in range(NTT):
                        ins = t.transpose(
                            psum[bank][:, tau * P:(tau + 1) * P],
                            src[:, tau * 1024 + c * P: tau * 1024 + (c + 1) * P],
                            IDF[:])
                    fills.append(pr.inc(ins, "pe", 1))
            pe_sec(fpe)
            return fills

        def fm_to_tm_pe(src, src_ready, eng, incs_per_group, banks=(4, 5)):
            sem = eng
            base = pr.c[sem]
            fills = []

            def fpe(t):
                for mk in src_ready:
                    pr.wait(t, "pe", mk)
                pr.wait(t, "pe", CM["ident"])
                for i in range(NTT * 2):
                    tau, cg = divmod(i, 2)
                    bank = banks[cg]
                    if i >= 2:
                        pr.wait(t, "pe", (sem, base + (i - 1) * incs_per_group))
                    else:
                        pr.wait(t, "pe", bank_free[bank])
                    ins = None
                    for j in range(4):
                        c = cg * 4 + j
                        ins = t.transpose(
                            psum[bank][:, j * P:(j + 1) * P],
                            src[:, c * 512 + tau * P: c * 512 + (tau + 1) * P],
                            IDF[:])
                    fills.append(pr.inc(ins, "pe", 1))
            pe_sec(fpe)
            return fills

        x1_store_marks = [None] * NSUB
        wmix_free = None
        scan_h_mark = None

        def load_w(name, dst2, src2, cast_mark, prior_use):
            def f(sy):
                pr.wait(sy, "sync", cast_mark)
                pr.wait(sy, "sync", prior_use)
                CM[name] = pr.dma(sy, "sync", "ql", dst2, src2)
            sync_sec(f)

        def emit_A_chain(b, s, defer_x1gen):
            def fx(sy):
                if s == 0:
                    for mk in reg_free["T1"]:
                        pr.wait(sy, "sync", mk)
                else:
                    pr.wait(sy, "sync", CM[f"x1g{b}_0"])
                CM[f"x{b}_{s}"] = pr.dma(
                    sy, "sync", "ql",
                    T1[:, :], x_in[b, s].rearrange("p tt d -> p (tt d)"))
            sync_sec(fx)

            def fv(dv):
                pr.wait(dv, "dve", CM[f"x{b}_{s}"])
                ins = None
                for tau in range(NTT):
                    g = s * NTT + tau
                    dv.bn_stats(BN6A[:, g * 12:g * 12 + 6],
                                T1[:, tau * 1024:tau * 1024 + 512])
                    ins = dv.bn_stats(BN6A[:, g * 12 + 6:g * 12 + 12],
                                      T1[:, tau * 1024 + 512:(tau + 1) * 1024])
                m0 = pr.inc(ins, "dve", 1)
                pr.wait(dv, "dve", m0)
                for tau in range(NTT):
                    g = s * NTT + tau
                    ins = dv.bn_aggr(BN2A[:, g * 2:g * 2 + 2],
                                     BN6A[:, g * 12:(g + 1) * 12])
                m1_ = pr.inc(ins, "dve", 1)
                pr.wait(dv, "dve", m1_)
                m2_ = pr.inc(dv.tensor_scalar(MVA[:, 24 + s * 4:24 + s * 4 + 4],
                                              BN2A[:, s * 8 + 1:s * 8 + 8:2],
                                              EPS, None, ALU.add), "dve", 1)
                pr.wait(dv, "dve", m2_)
                rsqrt_newton(dv, MVA[:, s * 4:s * 4 + 4],
                             MVA[:, 24 + s * 4:24 + s * 4 + 4])
                m3_ = pr.inc(dv.scalar_tensor_tensor(
                    MVA[:, 8 + s * 4:8 + s * 4 + 4], BN2A[:, s * 8:s * 8 + 8:2],
                    -1.0, MVA[:, s * 4:s * 4 + 4], ALU.mult, ALU.mult), "dve", 1)
                CM[f"ln1c{b}_{s}"] = pr.inc(
                    dv.tensor_scalar(MVA[:, 16 + s * 4:16 + s * 4 + 4],
                                     MVA[:, s * 4:s * 4 + 4], 1.0, None, ALU.add),
                    "dve", 1)
                pr.wait(dv, "dve", CM[f"ln1c{b}_{s}"])
            dve_sec(fv)

            def fa2(dv):
                for mk in reg_free["Z"]:
                    pr.wait(dv, "dve", mk)
                ins = None
                for tau in range(NTT):
                    g = s * NTT + tau
                    m_c = BN2A[:, s * 8 + 2 * tau:s * 8 + 2 * tau + 1]
                    rst = MVA[:, s * 4 + tau:s * 4 + tau + 1]
                    ins = dv.tensor_scalar(
                        Z[:, g * 1024:(g + 1) * 1024],
                        T1[:, tau * 1024:(tau + 1) * 1024], m_c, rst,
                        ALU.subtract, ALU.mult)
                CM[f"zx{b}_{s}"] = pr.inc(ins, "dve", 1)
                pr.wait(dv, "dve", CM[f"zx{b}_{s}"])
            dve_sec(fa2)

            def fv3(dv):
                pr.wait(dv, "dve", CM[f"zx{b}_{s}"])
                m0 = pr.inc(dv.tensor_mul(RS[:, 0:4], MVA[:, s * 4:s * 4 + 4],
                                          MVA[:, s * 4:s * 4 + 4]), "dve", 1)
                pr.wait(dv, "dve", m0)
                m1_ = pr.inc(dv.tensor_mul(RS[:, 0:4], RS[:, 0:4],
                                           BN2A[:, s * 8 + 1:s * 8 + 8:2]), "dve", 1)
                pr.wait(dv, "dve", m1_)
                m2_ = pr.inc(dv.tensor_scalar(RS[:, 0:4], RS[:, 0:4], EPS, None,
                                              ALU.add), "dve", 1)
                pr.wait(dv, "dve", m2_)
                rsqrt_newton(dv, RS[:, 4:8], RS[:, 0:4])
                for mk in reg_free["T2"]:
                    pr.wait(dv, "dve", mk)
                ins = None
                for tau in range(NTT):
                    g = s * NTT + tau
                    ins = dv.tensor_scalar(T2[:, tau * 1024:(tau + 1) * 1024],
                                           Z[:, g * 1024:(g + 1) * 1024],
                                           RS[:, 4 + tau:4 + tau + 1], None,
                                           ALU.mult)
                CM[f"t1{b}_{s}"] = pr.inc(ins, "dve", 1)
            dve_sec(fv3)
            reg_free["T2"] = [CM[f"t1{b}_{s}"]]
            if not defer_x1gen:
                emit_X1gen(b, s)

        def emit_X1gen(b, s):
            def fg(dv):
                if x1_store_marks[s] is not None:
                    pr.wait(dv, "dve", x1_store_marks[s])
                pr.wait(dv, "dve", CM[f"zx{b}_{s}"])
                ins = None
                for tau in range(NTT):
                    g = s * NTT + tau
                    nb = MVA[:, 8 + s * 4 + tau:8 + s * 4 + tau + 1]
                    rp1 = MVA[:, 16 + s * 4 + tau:16 + s * 4 + tau + 1]
                    ins = dv.tensor_scalar(X1[:, g * 1024:(g + 1) * 1024],
                                           T1[:, tau * 1024:(tau + 1) * 1024],
                                           rp1, nb, ALU.mult, ALU.add)
                CM[f"x1g{b}_{s}"] = pr.inc(ins, "dve", 1)
            dve_sec(fg)
            if s == 1:
                reg_free["T1"] = [CM[f"x1g{b}_{s}"]]

        def emit_A_tail_sub(b, s, xn_marks):
            fills = tm_to_fm_pe(T2, [CM[f"t1{b}_{s}"]], "act")

            def fev_xn(ac):
                pr.waits(ac, "act", CONST_MARKS)
                for mk in reg_free["FMB"]:
                    pr.wait(ac, "act", mk)
                for c in range(ND):
                    bank = (4, 5)[c % 2]
                    pr.wait(ac, "act", fills[c])
                    ins = ac.activation(
                        FMB[:, c * 1024 + s * 512: c * 1024 + (s + 1) * 512],
                        psum[bank][:, 0:512], AF.Identity,
                        scale=CN1W[:, c:c + 1], bias=CN1B[:, c:c + 1])
                    mk = pr.inc(ins, "act", 1)
                    bank_free[bank] = mk
                    xn_marks.append(mk)
            act_sec(fev_xn)
            reg_free["T2"] = [pr.mark("pe")]

        def emit_A_full(b, xn_bysub):
            emit_A_chain(b, 0, defer_x1gen=False)
            emit_A_tail_sub(b, 0, xn_bysub[0])
            emit_A_chain(b, 1, defer_x1gen=False)
            emit_A_tail_sub(b, 1, xn_bysub[1])

        pending_xn = [[], []]
        emit_A_full(0, pending_xn)

        for b in range(n_blocks):
            xn_by_sub = pending_xn
            xn_marks = xn_by_sub[0] + xn_by_sub[1]
            pending_xn = [[], []]

            load_w(f"gate{b}", W_MIX[:, 0:16384],
                   s_gate.rearrange("p m k f -> p (m k f)"),
                   CM["cast_gate"], wmix_free)

            def gate_dst(s, m):
                if m < 8:
                    return B1[:, m * 1024 + s * 512: m * 1024 + (s + 1) * 512]
                return B2[:, (m - 8) * 1024 + s * 512: (m - 8) * 1024 + (s + 1) * 512]

            gate_marks = mm_stage(
                W_MIX, lambda m, k: m * ND + k, NM_G, ND,
                lambda s, k: FMB[:, k * 1024 + s * 512: k * 1024 + (s + 1) * 512],
                "act", gate_dst, CM[f"gate{b}"], func=AF.Silu,
                pre_waits=list(reg_free["B1"]) + list(reg_free["B2"]),
                group_waits=lambda s, m: xn_by_sub[s])
            gidx = [(s, m) for s in range(NSUB) for m in range(NM_G)]
            g1s_marks = [gate_marks[i] for i, (s, m) in enumerate(gidx) if m < 8]
            g2s_marks = [gate_marks[i] for i, (s, m) in enumerate(gidx) if m >= 8]
            gate_last_mm = pr.mark("pe")

            load_w(f"state{b}", W_MIX[:, 0:8192],
                   s_state.rearrange("p m k f -> p (m k f)"),
                   CM["cast_state"], gate_last_mm)

            def fv1(dv):
                pr.waits(dv, "dve", g1s_marks)
                pr.waits(dv, "dve", xn_marks)
                for mk in reg_free["B3"]:
                    pr.wait(dv, "dve", mk)
                CM[f"v1{b}"] = pr.inc(dv.tensor_mul(B3, FMB[:], B1), "dve", 1)
            dve_sec(fv1)

            dpre_base = pr.c["dve"]

            def fdpre_pe(t):
                pr.wait(t, "pe", CM["wms"])
                pr.waits(t, "pe", xn_marks)
                for s in range(NSUB):
                    if s > 0:
                        pr.wait(t, "pe", ("dve", dpre_base + s))
                    else:
                        pr.wait(t, "pe", bank_free[6])
                    ins = None
                    for k in range(ND):
                        ins = t.matmul(
                            psum[6][0:1, 0:512], WMS[:, k:k + 1],
                            FMB[:, k * 1024 + s * 512: k * 1024 + (s + 1) * 512],
                            start=(k == 0), stop=(k == ND - 1))
                    CM[f"dpref{b}_{s}"] = pr.inc(ins, "pe", 1)
            pe_sec(fdpre_pe)

            def fdpre_dv(dv):
                pr.wait(dv, "dve", CM["bsb"])
                for s in range(NSUB):
                    pr.wait(dv, "dve", CM[f"dpref{b}_{s}"])
                    mk = pr.inc(dv.tensor_scalar(R0[0:1, s * 512:(s + 1) * 512],
                                                 psum[6][0:1, 0:512], BSB[0:1, 1:2],
                                                 None, ALU.add), "dve", 1)
                    bank_free[6] = mk
                CM[f"dpre{b}"] = pr.mark("dve")
            dve_sec(fdpre_dv)
            assert pr.c["dve"] == dpre_base + NSUB

            def fda(ac):
                pr.wait(ac, "act", CM[f"dpre{b}"])
                m0 = pr.inc(ac.activation(R1[0:1, :], R0[0:1, :], AF.Abs), "act", 1)
                pr.wait(ac, "act", m0)
                m1_ = pr.inc(ac.activation(R2[0:1, :], R1[0:1, :], AF.Exp, scale=-1.0),
                             "act", 1)
                pr.wait(ac, "act", m1_)
                CM[f"sp{b}"] = pr.inc(
                    ac.activation(R1[0:1, :], R2[0:1, :], AF.Ln, bias=1.0), "act", 1)
                pr.wait(ac, "act", ("dve", pr.c["dve"] + 2))
                CM[f"arow{b}"] = pr.inc(
                    ac.activation(R1[0:1, :], R0[0:1, :], AF.Exp, scale=-1.0), "act", 1)
            act_sec(fda)

            def fdv2(dv):
                pr.wait(dv, "dve", CM[f"sp{b}"])
                m0 = pr.inc(dv.tensor_scalar_max(R2[0:1, :], R0[0:1, :], 0.0), "dve", 1)
                pr.wait(dv, "dve", m0)
                CM[f"delta{b}"] = pr.inc(
                    dv.tensor_add(R0[0:1, :], R2[0:1, :], R1[0:1, :]), "dve", 1)
            dve_sec(fdv2)

            bc_base = pr.c["dve"]

            def fbc_pe(t):
                pr.wait(t, "pe", CM["ones"])
                i = 0
                for bank, rm, row in [(6, f"delta{b}", R0), (7, f"arow{b}", R1)]:
                    pr.wait(t, "pe", CM[rm])
                    for s in range(NSUB):
                        if i % 2 == 1:
                            pr.wait(t, "pe", ("dve", bc_base + i))
                        else:
                            pr.wait(t, "pe", bank_free[bank])
                        ins = t.matmul(psum[bank][:, 0:512], ONES[:],
                                       row[0:1, s * 512:(s + 1) * 512],
                                       start=True, stop=True)
                        CM[f"bcf{b}_{i}"] = pr.inc(ins, "pe", 1)
                        i += 1
            pe_sec(fbc_pe)

            bc_marks = []

            def fbc_dv(dv):
                for mk in reg_free["SCBC"]:
                    pr.wait(dv, "dve", mk)
                i = 0
                for bank, dstv in [(6, D_BC), (7, A_BC)]:
                    for s in range(NSUB):
                        pr.wait(dv, "dve", CM[f"bcf{b}_{i}"])
                        mk = pr.inc(dv.tensor_copy(dstv[:, s * 512:(s + 1) * 512],
                                                   psum[bank][:, 0:512]), "dve", 1)
                        bank_free[bank] = mk
                        bc_marks.append(mk)
                        i += 1
            dve_sec(fbc_dv)
            assert pr.c["dve"] == bc_base + 4
            reg_free["SCBC"] = []

            u_marks = mm_stage(
                W_MIX, lambda m, k: m * ND + k, ND, ND,
                lambda s, k: B3[:, k * 1024 + s * 512: k * 1024 + (s + 1) * 512],
                "dve",
                lambda s, m: B1[:, m * 1024 + s * 512: m * 1024 + (s + 1) * 512],
                CM[f"state{b}"], bias_col=lambda m: CB_S[:, m:m + 1],
                pre_waits=[CM[f"v1{b}"]],
                extra_pe_waits=[CM[f"v1{b}"]] + g1s_marks)

            g2s_by_sub = [g2s_marks[0:8], g2s_marks[8:16]]
            bm_marks = mm_stage(
                W_MIX, lambda m, k: m * ND + k, ND, ND,
                lambda s, k: B2[:, k * 1024 + s * 512: k * 1024 + (s + 1) * 512],
                "act",
                lambda s, m: B3[:, m * 1024 + s * 512: m * 1024 + (s + 1) * 512],
                CM[f"state{b}"], bias_col=lambda m: CB_S[:, m:m + 1],
                pre_waits=[CM[f"v1{b}"]],
                extra_pe_waits=[CM[f"v1{b}"]],
                group_waits=lambda s, m: g2s_by_sub[s])
            state_last_mm = pr.mark("pe")

            load_w(f"outp{b}", W_MIX[:, 0:8192],
                   s_outp.rearrange("p m k f -> p (m k f)"),
                   CM["cast_outp"], state_last_mm)

            load_w(f"mlp1a{b}", W_BIG[:, 16384:24576],
                   s_mlp1[:, 8:16].rearrange("p m k f -> p (m k f)"),
                   CM["cast_mlp1"], state_last_mm)

            scan_marks = []

            def fscan(dv):
                pr.waits(dv, "dve", bc_marks)
                pr.waits(dv, "dve", u_marks)
                udm = pr.inc(dv.tensor_tensor(
                    B1[:, :].rearrange("p (c f) -> p c f", c=ND),
                    B1[:, :].rearrange("p (c f) -> p c f", c=ND),
                    D_BC[:, None, :].broadcast_to([P, ND, 1024]), ALU.mult),
                    "dve", 1)
                pr.wait(dv, "dve", udm)
                for c in range(ND):
                    pr.wait(dv, "dve", bm_marks[8 + c])
                    m0 = pr.inc(dv.tensor_tensor(BCH[c % 2],
                                                 B1[:, c * 1024:(c + 1) * 1024],
                                                 B3[:, c * 1024:(c + 1) * 1024],
                                                 ALU.mult), "dve", 1)
                    pr.wait(dv, "dve", m0)
                    init = 0.0 if b % 2 == 0 else HCAR[:, c:c + 1]
                    ins = dv.tensor_tensor_scan(B1[:, c * 1024:(c + 1) * 1024],
                                                A_BC, BCH[c % 2], init,
                                                ALU.mult, ALU.add)
                    scan_marks.append(pr.inc(ins, "dve", 1))
                if b % 2 == 0:
                    pr.waits(dv, "dve", scan_marks)
                    ins = None
                    for c in range(ND):
                        ins = dv.tensor_copy(
                            HCAR[:, c:c + 1],
                            B1[:, (c + 1) * 1024 - 1:(c + 1) * 1024])
                    CM[f"hcar{b}"] = pr.inc(ins, "dve", 1)
            dve_sec(fscan)
            if b % 2 == 0:
                scan_h_mark = CM[f"hcar{b}"]

            scan_done_dve = pr.mark("dve")
            load_w(f"mlp1b{b}", W_BIG[:, 24576:40960],
                   s_mlp1[:, 16:32].rearrange("p m k f -> p (m k f)"),
                   CM["cast_mlp1"], scan_done_dve)

            x2n_by_sub = [[], []]
            xb_last = None
            t1_free = list(reg_free["T1"])

            def emit_out_mm(s, pre):
                return mm_stage(
                    W_MIX, lambda m, k: m * ND + k, ND, ND,
                    lambda s_, k: B1[:, k * 1024 + s * 512: k * 1024 + (s + 1) * 512],
                    "act", lambda s_, m: T1[:, m * 512:(m + 1) * 512],
                    CM[f"outp{b}"], bias_col=lambda m: CB_O[:, m:m + 1],
                    pre_waits=pre, extra_pe_waits=list(scan_marks), subs=(s,))

            def emit_fcons(s, fills):
                xb_marks = []

                def fcons(dv, s=s, fills=fills, xb_marks=xb_marks):
                    pr.wait(dv, "dve", CM[f"x1g{b}_{s}"])
                    for i in range(NTT * 2):
                        tau, cg = divmod(i, 2)
                        bank = (4, 5)[cg]
                        g = s * NTT + tau
                        pr.wait(dv, "dve", fills[i])
                        sl = slice(g * 1024 + cg * 512, g * 1024 + (cg + 1) * 512)
                        sl2 = slice(tau * 1024 + cg * 512, tau * 1024 + (cg + 1) * 512)
                        pr.inc(dv.tensor_add(X1[:, sl], X1[:, sl],
                                             psum[bank][:, 0:512]), "dve", 1)
                        ins = dv.tensor_add(T2[:, sl2], Z[:, sl], psum[bank][:, 0:512])
                        mk = pr.inc(ins, "dve", 1)
                        bank_free[bank] = mk
                        xb_marks.append(mk)
                dve_sec(fcons)
                return xb_marks

            def emit_ln2_t2(s, xb_marks):
                def fv5(dv, s=s, xb_marks=xb_marks):
                    pr.waits(dv, "dve", xb_marks)
                    ins = None
                    for tau in range(NTT):
                        g = s * NTT + tau
                        dv.bn_stats(BN6[:, g * 12:g * 12 + 6],
                                    T2[:, tau * 1024:tau * 1024 + 512])
                        ins = dv.bn_stats(BN6[:, g * 12 + 6:g * 12 + 12],
                                          T2[:, tau * 1024 + 512:(tau + 1) * 1024])
                    m0 = pr.inc(ins, "dve", 1)
                    pr.wait(dv, "dve", m0)
                    for tau in range(NTT):
                        g = s * NTT + tau
                        ins = dv.bn_aggr(BN2[:, g * 2:g * 2 + 2],
                                         BN6[:, g * 12:(g + 1) * 12])
                    m1_ = pr.inc(ins, "dve", 1)
                    pr.wait(dv, "dve", m1_)
                    m2_ = pr.inc(dv.tensor_scalar(MV[:, 24 + s * 4:24 + s * 4 + 4],
                                                  BN2[:, s * 8 + 1:s * 8 + 8:2],
                                                  EPS, None, ALU.add), "dve", 1)
                    pr.wait(dv, "dve", m2_)
                    rsqrt_newton(dv, MV[:, s * 4:s * 4 + 4],
                                 MV[:, 24 + s * 4:24 + s * 4 + 4])
                    ins = None
                    for tau in range(NTT):
                        g = s * NTT + tau
                        ins = dv.tensor_scalar(
                            T2[:, tau * 1024:(tau + 1) * 1024],
                            T2[:, tau * 1024:(tau + 1) * 1024],
                            BN2[:, g * 2:g * 2 + 1],
                            MV[:, s * 4 + tau:s * 4 + tau + 1],
                            ALU.subtract, ALU.mult)
                    CM[f"t2{b}_{s}"] = pr.inc(ins, "dve", 1)
                dve_sec(fv5)

            def emit_x2n(s, fills2):
                def fev_x2n(ac, s=s, fills2=fills2):
                    pr.waits(ac, "act", CONST_MARKS)
                    for c in range(ND):
                        bank = (4, 5)[c % 2]
                        pr.wait(ac, "act", fills2[c])
                        ins = ac.activation(
                            FMB[:, c * 1024 + s * 512: c * 1024 + (s + 1) * 512],
                            psum[bank][:, 0:512], AF.Identity,
                            scale=CN2W[:, c:c + 1], bias=CN2B[:, c:c + 1])
                        mk = pr.inc(ins, "act", 1)
                        bank_free[bank] = mk
                        x2n_by_sub[s].append(mk)
                act_sec(fev_x2n)

            ssm0 = emit_out_mm(0, t1_free)
            fills_s0 = fm_to_tm_pe(T1, ssm0, "dve", 2)
            ssmT0_done = pr.mark("pe")
            xb0 = emit_fcons(0, fills_s0)
            emit_ln2_t2(0, xb0)
            xb_last = xb0[-1]

            ssm1 = emit_out_mm(1, [ssmT0_done])
            outmm1_pe = pr.mark("pe")
            load_w(f"mlp1c{b}", W_BIG[:, 8192:16384],
                   s_mlp1[:, 0:8].rearrange("p m k f -> p (m k f)"),
                   CM["cast_mlp1"], outmm1_pe)

            def mlp1_ready(m, b=b):
                if m < 8:
                    return CM[f"mlp1c{b}"]
                if m < 16:
                    return CM[f"mlp1a{b}"]
                return CM[f"mlp1b{b}"]
            mlp1_m_order = list(range(8, 32)) + list(range(0, 8))
            fills_t20 = tm_to_fm_pe(T2, [CM[f"t2{b}_0"]], "act")
            emit_x2n(0, fills_t20)
            t2T0_done = pr.mark("pe")
            fills_s1 = fm_to_tm_pe(T1, ssm1, "dve", 2)
            ssmT1_done = pr.mark("pe")
            xb1 = emit_fcons(1, fills_s1)
            emit_ln2_t2(1, xb1)
            xb_last = xb1[-1]

            m1_0 = mm_stage(
                W_BIG, lambda m, k: 64 + m * ND + k, NM_MLP, ND,
                lambda s_, k: FMB[:, k * 1024 + 0 * 512: k * 1024 + 1 * 512],
                "act", lambda s_, m: M1[:, m * 512:(m + 1) * 512],
                mlp1_ready, func=AF.Gelu,
                bias_col=lambda m: CB_M1[:, m:m + 1],
                pre_waits=list(reg_free["M1"]),
                extra_pe_waits=list(x2n_by_sub[0]) + list(reg_free["M1"]),
                subs=(0,), m_order=mlp1_m_order)

            fills_t21 = tm_to_fm_pe(T2, [CM[f"t2{b}_1"]], "act")
            emit_x2n(1, fills_t21)
            t2T1_done = pr.mark("pe")

            prev_ring = [None, None]
            ringslots = [0, 4096]

            def emit_mlp2(s, ring_free_mark, m1_marks):
                pe_base2 = pr.c["pe"]
                dve_base2 = pr.c["dve"]
                ring_marks = {}

                def fr(sy, s=s):
                    pr.wait(sy, "sync", CM["cast_mlp2"])
                    pr.wait(sy, "sync", ring_free_mark)
                    for m in range(ND):
                        slot = ringslots[m % 2]
                        if m < 2:
                            pr.wait(sy, "sync", prev_ring[m % 2])
                        else:
                            pr.wait(sy, "sync", ("pe", pe_base2 + m - 1))
                        ring_marks[m] = pr.dma(
                            sy, "sync", "ql", A_RING[:, slot:slot + 4096],
                            s_mlp2[:, m].rearrange("p k f -> p (k f)"))
                sync_sec(fr)

                def fpe2(t, s=s):
                    pr.waits(t, "pe", m1_marks)
                    for m in range(ND):
                        slot = ringslots[m % 2]
                        bank = m % 4
                        pr.wait(t, "pe", ring_marks[m])
                        if m >= 4:
                            pr.wait(t, "pe", ("dve", dve_base2 + m - 3))
                        else:
                            pr.wait(t, "pe", bank_free[bank])
                        ins = None
                        for k in range(NM_MLP):
                            ins = t.matmul(psum[bank][:, 0:512],
                                           A_RING[:, slot + k * P:slot + (k + 1) * P],
                                           M1[:, k * 512:(k + 1) * 512],
                                           start=(k == 0), stop=(k == NM_MLP - 1))
                        CM[f"m2f{b}_{s}_{m}"] = pr.inc(ins, "pe", 1)
                        prev_ring[m % 2] = CM[f"m2f{b}_{s}_{m}"]
                pe_sec(fpe2)

                mlpo_marks = []

                def fev2(dv, s=s):
                    for mk in reg_free["T2"]:
                        pr.wait(dv, "dve", mk)
                    for m in range(ND):
                        bank = m % 4
                        pr.wait(dv, "dve", CM[f"m2f{b}_{s}_{m}"])
                        ins = dv.tensor_scalar(T2[:, m * 512:(m + 1) * 512],
                                               psum[bank][:, 0:512],
                                               CB_M2[:, m:m + 1], None, ALU.add)
                        mk = pr.inc(ins, "dve", 1)
                        bank_free[bank] = mk
                        mlpo_marks.append(mk)
                dve_sec(fev2)
                assert pr.c["dve"] == dve_base2 + ND

                fills3 = fm_to_tm_pe(T2, mlpo_marks, "dve", 1)

                def fcons2(dv, s=s, fills3=fills3):
                    for i in range(NTT * 2):
                        tau, cg = divmod(i, 2)
                        bank = (4, 5)[cg]
                        g = s * NTT + tau
                        pr.wait(dv, "dve", fills3[i])
                        sl = slice(g * 1024 + cg * 512, g * 1024 + (cg + 1) * 512)
                        ins = dv.tensor_add(X1[:, sl], X1[:, sl], psum[bank][:, 0:512])
                        mk = pr.inc(ins, "dve", 1)
                        bank_free[bank] = mk
                dve_sec(fcons2)
                reg_free["T2"] = [pr.mark("pe")]
                CM[f"xmid{b}_{s}"] = pr.mark("dve")

            reg_free["T2"] = [t2T1_done]
            emit_mlp2(0, ssmT1_done, m1_0)
            mlp2_0_pe = pr.mark("pe")

            m1_1 = mm_stage(
                W_BIG, lambda m, k: 64 + m * ND + k, NM_MLP, ND,
                lambda s_, k: FMB[:, k * 1024 + 1 * 512: k * 1024 + 2 * 512],
                "act", lambda s_, m: M1[:, m * 512:(m + 1) * 512],
                mlp1_ready, func=AF.Gelu,
                bias_col=lambda m: CB_M1[:, m:m + 1],
                pre_waits=[mlp2_0_pe],
                extra_pe_waits=list(x2n_by_sub[1]) + [mlp2_0_pe],
                subs=(1,), m_order=mlp1_m_order)

            emit_mlp2(1, ssmT1_done, m1_1)
            reg_free["M1"] = [pr.mark("pe")]
            reg_free["Z"] = [xb_last]
            mlp_last_pe = pr.mark("pe")
            reg_free["T1"] = [CM[f"m2f{b}_1_{ND - 1}"]]

            prev_g = [None, None]
            prev_u = [None, None]
            prev_d = [None, None]
            rgs = [8192, 9216]
            rus = [10240, 11264]
            rds = [12288, 15360]
            ringslots_d = rds

            def emit_ln3(s, t2_free):
                def fv7(dv, s=s):
                    pr.wait(dv, "dve", CM[f"xmid{b}_{s}"])
                    ins = None
                    for tau in range(NTT):
                        g = s * NTT + tau
                        dv.bn_stats(BN6[:, g * 12:g * 12 + 6],
                                    X1[:, g * 1024:g * 1024 + 512])
                        ins = dv.bn_stats(BN6[:, g * 12 + 6:g * 12 + 12],
                                          X1[:, g * 1024 + 512:(g + 1) * 1024])
                    m0 = pr.inc(ins, "dve", 1)
                    pr.wait(dv, "dve", m0)
                    for tau in range(NTT):
                        g = s * NTT + tau
                        ins = dv.bn_aggr(BN2[:, g * 2:g * 2 + 2],
                                         BN6[:, g * 12:(g + 1) * 12])
                    m1_ = pr.inc(ins, "dve", 1)
                    pr.wait(dv, "dve", m1_)
                    m2_ = pr.inc(dv.tensor_scalar(MV[:, 24 + s * 4:24 + s * 4 + 4],
                                                  BN2[:, s * 8 + 1:s * 8 + 8:2],
                                                  EPS, None, ALU.add), "dve", 1)
                    pr.wait(dv, "dve", m2_)
                    rsqrt_newton(dv, MV[:, s * 4:s * 4 + 4],
                                 MV[:, 24 + s * 4:24 + s * 4 + 4])
                    CM[f"ln3c{b}_{s}"] = pr.inc(dv.scalar_tensor_tensor(
                        MV[:, 8 + s * 4:8 + s * 4 + 4], BN2[:, s * 8:s * 8 + 8:2],
                        -1.0, MV[:, s * 4:s * 4 + 4], ALU.mult, ALU.mult), "dve", 1)
                dve_sec(fv7)

                def fa8(ac, s=s, t2_free=t2_free):
                    pr.wait(ac, "act", CM[f"ln3c{b}_{s}"])
                    for mk in t2_free:
                        pr.wait(ac, "act", mk)
                    ins = None
                    for tau in range(NTT):
                        g = s * NTT + tau
                        ins = ac.activation(
                            T2[:, tau * 1024:(tau + 1) * 1024],
                            X1[:, g * 1024:(g + 1) * 1024], AF.Identity,
                            scale=MV[:, s * 4 + tau:s * 4 + tau + 1],
                            bias=MV[:, 8 + s * 4 + tau:8 + s * 4 + tau + 1])
                    CM[f"t3{b}_{s}"] = pr.inc(ins, "act", 1)
                act_sec(fa8)

            def emit_zn(s):
                fills4 = tm_to_fm_pe(T2, [CM[f"t3{b}_{s}"]], "dve")
                zn_marks = []

                def fev_zn(dv, s=s, fills4=fills4):
                    for c in range(ND):
                        bank = (4, 5)[c % 2]
                        pr.wait(dv, "dve", fills4[c])
                        ins = dv.tensor_copy(
                            FMB[:, c * 1024 + s * 512: c * 1024 + (s + 1) * 512],
                            psum[bank][:, 0:512])
                        mk = pr.inc(ins, "dve", 1)
                        bank_free[bank] = mk
                        zn_marks.append(mk)
                dve_sec(fev_zn)
                return zn_marks, pr.mark("pe")

            def emit_gu(s, zn_marks, m1_free):
                pe_base3 = pr.c["pe"]
                act_base3 = pr.c["act"]
                dve_base3 = pr.c["dve"]

                def frg(sy, s=s):
                    pr.wait(sy, "sync", CM["cast_fg"])
                    pr.wait(sy, "sync", CM["cast_fu"])
                    pr.wait(sy, "sync", mlp_last_pe)
                    for m in range(NM_FFN):
                        sg_slot = rgs[m % 2]
                        su_slot = rus[m % 2]
                        if m < 2:
                            pr.wait(sy, "sync", prev_g[m % 2])
                        else:
                            pr.wait(sy, "sync", ("pe", pe_base3 + (m - 2) * 2 + 1))
                        CM[f"rg{b}_{s}_{m}"] = pr.dma(
                            sy, "sync", "ql", W_BIG[:, sg_slot:sg_slot + 1024],
                            s_fg[:, m].rearrange("p k f -> p (k f)"))
                        if m < 2:
                            pr.wait(sy, "sync", prev_u[m % 2])
                        else:
                            pr.wait(sy, "sync", ("pe", pe_base3 + (m - 2) * 2 + 2))
                        CM[f"ru{b}_{s}_{m}"] = pr.dma(
                            sy, "sync", "ql", W_BIG[:, su_slot:su_slot + 1024],
                            s_fu[:, m].rearrange("p k f -> p (k f)"))
                sync_sec(frg)

                def fpe3(t, s=s, act_base3=act_base3, dve_base3=dve_base3):
                    pr.waits(t, "pe", m1_free)
                    for m in range(NM_FFN):
                        sg_slot = rgs[m % 2]
                        su_slot = rus[m % 2]
                        bg = m % 2
                        bu = 2 + m % 2
                        pr.wait(t, "pe", CM[f"rg{b}_{s}_{m}"])
                        if m < 2:
                            pr.wait(t, "pe", bank_free[bg])
                            pr.waits(t, "pe", zn_marks)
                        else:
                            pr.wait(t, "pe", ("act", act_base3 + m - 1))
                        ins = None
                        for k in range(ND):
                            ins = t.matmul(
                                psum[bg][:, 0:512],
                                W_BIG[:, sg_slot + k * P:sg_slot + (k + 1) * P],
                                FMB[:, k * 1024 + s * 512: k * 1024 + (s + 1) * 512],
                                start=(k == 0), stop=(k == ND - 1))
                        CM[f"fgf{b}_{s}_{m}"] = pr.inc(ins, "pe", 1)
                        pr.wait(t, "pe", CM[f"ru{b}_{s}_{m}"])
                        if m < 2:
                            pr.wait(t, "pe", bank_free[bu])
                        else:
                            pr.wait(t, "pe", ("dve", dve_base3 + (m - 2) * 2 + 1))
                        for k in range(ND):
                            ins = t.matmul(
                                psum[bu][:, 0:512],
                                W_BIG[:, su_slot + k * P:su_slot + (k + 1) * P],
                                FMB[:, k * 1024 + s * 512: k * 1024 + (s + 1) * 512],
                                start=(k == 0), stop=(k == ND - 1))
                        CM[f"fuf{b}_{s}_{m}"] = pr.inc(ins, "pe", 1)
                pe_sec(fpe3)

                def fact3(ac, s=s, dve_base3=dve_base3):
                    for m in range(NM_FFN):
                        bg = m % 2
                        pr.wait(ac, "act", CM[f"fgf{b}_{s}_{m}"])
                        if m > 0:
                            pr.wait(ac, "act", ("dve", dve_base3 + (m - 1) * 2 + 2))
                        elif CM.get(f"hlast{b}_{1 - s}") is not None and s == 1:
                            pr.wait(ac, "act", CM[f"hlast{b}_0"])
                        ins = ac.activation(A_FS[:, 0:512], psum[bg][:, 0:512],
                                            AF.Silu, scale=QS)
                        CM[f"fsg{b}_{s}_{m}"] = pr.inc(ins, "act", 1)
                act_sec(fact3)
                assert pr.c["act"] == act_base3 + NM_FFN

                h_marks = []

                def fdve3(dv, s=s):
                    for m in range(NM_FFN):
                        bu = 2 + m % 2
                        pr.wait(dv, "dve", CM[f"fuf{b}_{s}_{m}"])
                        if s == 1 and m == 0:
                            pr.wait(dv, "dve", CM[f"fdf{b}_0_{ND - 1}"])
                        if h_marks:
                            pr.wait(dv, "dve", h_marks[-1])
                        m0 = pr.inc(dv.tensor_scalar(A_FS[:, 512:1024],
                                                     psum[bu][:, 0:512], QS, None,
                                                     ALU.mult), "dve", 1)
                        pr.wait(dv, "dve", m0)
                        pr.wait(dv, "dve", CM[f"fsg{b}_{s}_{m}"])
                        ins = dv.tensor_mul(M1[:, m * 512:(m + 1) * 512],
                                            A_FS[:, 0:512], A_FS[:, 512:1024])
                        mk = pr.inc(ins, "dve", 1)
                        h_marks.append(mk)
                dve_sec(fdve3)
                assert pr.c["dve"] == dve_base3 + 2 * NM_FFN
                CM[f"hlast{b}_{s}"] = h_marks[-1]
                prev_g[0] = CM[f"fgf{b}_{s}_{NM_FFN - 2}"]
                prev_g[1] = CM[f"fgf{b}_{s}_{NM_FFN - 1}"]
                prev_u[0] = CM[f"fuf{b}_{s}_{NM_FFN - 2}"]
                prev_u[1] = CM[f"fuf{b}_{s}_{NM_FFN - 1}"]
                return h_marks

            def emit_down(s, h_marks, extra_pe=()):
                pe_base4 = pr.c["pe"]

                def frd(sy, s=s):
                    pr.wait(sy, "sync", CM["cast_fd"])
                    pr.wait(sy, "sync", mlp_last_pe)
                    for m in range(ND):
                        slot = rds[m % 2]
                        if m < 2:
                            pr.wait(sy, "sync", prev_d[m % 2])
                        else:
                            pr.wait(sy, "sync", ("pe", pe_base4 + m - 1))
                        CM[f"rd{b}_{s}_{m}"] = pr.dma(
                            sy, "sync", "ql", W_BIG[:, slot:slot + 3072],
                            s_fd[:, m].rearrange("p k f -> p (k f)"))
                sync_sec(frd)

                act_base4 = pr.c["act"]

                def fpe4(t, s=s, act_base4=act_base4):
                    pr.waits(t, "pe", h_marks)
                    for mk in extra_pe:
                        pr.wait(t, "pe", mk)
                    for m in range(ND):
                        slot = rds[m % 2]
                        bank = m % 4
                        pr.wait(t, "pe", CM[f"rd{b}_{s}_{m}"])
                        if m >= 4:
                            pr.wait(t, "pe", ("act", act_base4 + m - 3))
                        else:
                            pr.wait(t, "pe", bank_free[bank])
                        ins = None
                        for k in range(NM_FFN):
                            ins = t.matmul(psum[bank][:, 0:512],
                                           W_BIG[:, slot + k * P:slot + (k + 1) * P],
                                           M1[:, k * 512:(k + 1) * 512],
                                           start=(k == 0), stop=(k == NM_FFN - 1))
                        CM[f"fdf{b}_{s}_{m}"] = pr.inc(ins, "pe", 1)
                pe_sec(fpe4)
                prev_d[0] = CM[f"fdf{b}_{s}_{ND - 2}"]
                prev_d[1] = CM[f"fdf{b}_{s}_{ND - 1}"]
                return act_base4

            def emit_fev4(s, act_base4):
                ffno_marks = []

                def fev4(ac, s=s):
                    for m in range(ND):
                        bank = m % 4
                        pr.wait(ac, "act", CM[f"fdf{b}_{s}_{m}"])
                        ins = ac.activation(A_FFNO[:, m * 512:(m + 1) * 512],
                                            psum[bank][:, 0:512], AF.Identity,
                                            scale=QS)
                        mk = pr.inc(ins, "act", 1)
                        bank_free[bank] = mk
                        ffno_marks.append(mk)
                act_sec(fev4)
                assert pr.c["act"] == act_base4 + ND
                return ffno_marks

            def emit_ffn_tail(s, ffno_marks):
                fills5 = fm_to_tm_pe(A_FFNO, ffno_marks, "dve", 1)
                xo_marks = []

                def fcons3(dv, s=s, fills5=fills5, xo_marks=xo_marks):
                    for i in range(NTT * 2):
                        tau, cg = divmod(i, 2)
                        bank = (4, 5)[cg]
                        g = s * NTT + tau
                        pr.wait(dv, "dve", fills5[i])
                        sl = slice(g * 1024 + cg * 512, g * 1024 + (cg + 1) * 512)
                        ins = dv.tensor_add(X1[:, sl], X1[:, sl], psum[bank][:, 0:512])
                        mk = pr.inc(ins, "dve", 1)
                        bank_free[bank] = mk
                        xo_marks.append(mk)
                dve_sec(fcons3)

                def fst(gp, s=s, xo_marks=xo_marks):
                    pr.waits(gp, "pool", xo_marks)
                    x1_store_marks[s] = pr.dma(
                        gp, "pool", "qs",
                        y_out[b, s].rearrange("p tt d -> p (tt d)"),
                        X1[:, s * 4096:(s + 1) * 4096])
                gp_sec(fst)

            emit_ln3(0, reg_free["T2"])
            zn0, _ = emit_zn(0)
            h0 = emit_gu(0, zn0, reg_free["M1"])
            emit_ln3(1, [pr.mark("pe")])
            zn1, _ = emit_zn(1)
            d0base = emit_down(0, h0)
            ffno0 = emit_fev4(0, d0base)
            h1 = emit_gu(1, zn1, [])
            ffn_gu_done = pr.mark("pe")
            emit_ffn_tail(0, ffno0)
            if b + 1 < n_blocks:
                emit_A_chain(b + 1, 0, defer_x1gen=True)
                emit_X1gen(b + 1, 0)
            d1base = emit_down(1, h1)
            ffno1 = emit_fev4(1, d1base)
            if b + 1 < n_blocks:
                wmix_free_next = pr.mark("pe")
                reg_free["FMB"] = [ffn_gu_done]
                emit_A_tail_sub(b + 1, 0, pending_xn[0])
            emit_ffn_tail(1, ffno1)
            reg_free["M1"] = [pr.mark("pe")]
            if b + 1 < n_blocks:
                emit_A_chain(b + 1, 1, defer_x1gen=True)
                emit_X1gen(b + 1, 1)
                emit_A_tail_sub(b + 1, 1, pending_xn[1])

            endpe = pr.mark("pe")
            enddve = pr.mark("dve")
            reg_free["B1"] = [endpe, enddve]
            reg_free["B2"] = [endpe, enddve]
            reg_free["B3"] = [endpe, enddve]
            reg_free["SCBC"] = [endpe, enddve]
            if b + 1 < n_blocks:
                wmix_free = wmix_free_next
            else:
                wmix_free = endpe

        def fin(gp):
            for i in range(4):
                gp.wait_ge(pr.h[f"qs{i}"], pr.c[f"qs{i}"])
        gp_sec(fin)

    return nc



def _tile_pm(w, mc, kc):
    K, M = w.shape
    assert K == kc * P and M == mc * P
    return np.ascontiguousarray(w.reshape(kc, P, mc, P).transpose(1, 2, 0, 3))


def _cols(v):
    n = v.shape[0] // P
    return np.ascontiguousarray(v.reshape(n, P).T)


def _x_layout(x):
    return np.ascontiguousarray(
        x.reshape(NB, NSUB, NTT, P, D).transpose(0, 1, 3, 2, 4))


def _y_unlayout(y):
    return np.asarray(y).transpose(0, 1, 3, 2, 4).reshape(NSEQ, TSEQ, D)


def make_inputs(L0_attn_x, L0_attn_norm1_w, L0_attn_norm1_b, L0_attn_gate_w,
                L0_attn_gate_b, L0_attn_state_w, L0_attn_state_b, L0_attn_out_w,
                L0_attn_out_b, L0_attn_norm2_w, L0_attn_norm2_b, L0_attn_mlp_w1,
                L0_attn_mlp_b1, L0_attn_mlp_w2, L0_attn_mlp_b2, L0_ffn_w_gate,
                L0_ffn_w_up, L0_ffn_w_down, n_cores=8):
    x = np.asarray(L0_attn_x, dtype=np.float32)
    common = {
        "w_gate": _tile_pm(np.asarray(L0_attn_gate_w, np.float32), NM_G, ND),
        "w_state": _tile_pm(np.asarray(L0_attn_state_w, np.float32), ND, ND),
        "w_outp": _tile_pm(np.asarray(L0_attn_out_w, np.float32), ND, ND),
        "w_mlp1": _tile_pm(np.asarray(L0_attn_mlp_w1, np.float32), NM_MLP, ND),
        "w_mlp2": _tile_pm(np.asarray(L0_attn_mlp_w2, np.float32), ND, NM_MLP),
        "w_fg": _tile_pm(np.ascontiguousarray(np.asarray(L0_ffn_w_gate, np.int32).T),
                         NM_FFN, ND),
        "w_fu": _tile_pm(np.ascontiguousarray(np.asarray(L0_ffn_w_up, np.int32).T),
                         NM_FFN, ND),
        "w_fd": _tile_pm(np.ascontiguousarray(np.asarray(L0_ffn_w_down, np.int32).T),
                         ND, NM_FFN),
        "c_gate_b": _cols(np.asarray(L0_attn_gate_b, np.float32)),
        "c_state_b": _cols(np.asarray(L0_attn_state_b, np.float32)),
        "c_out_b": _cols(np.asarray(L0_attn_out_b, np.float32)),
        "c_mlp_b1": _cols(np.asarray(L0_attn_mlp_b1, np.float32)),
        "c_mlp_b2": _cols(np.asarray(L0_attn_mlp_b2, np.float32)),
        "c_n1w": _cols(np.asarray(L0_attn_norm1_w, np.float32)),
        "c_n1b": _cols(np.asarray(L0_attn_norm1_b, np.float32)),
        "c_n2w": _cols(np.asarray(L0_attn_norm2_w, np.float32)),
        "c_n2b": _cols(np.asarray(L0_attn_norm2_b, np.float32)),
        "r_state_b": np.ascontiguousarray(
            np.asarray(L0_attn_state_b, np.float32).reshape(1, D)),
        "c_ident": np.eye(P, dtype=np.float32),
        "c_ones": np.ones((1, P), dtype=np.float32),
    }
    in_maps = []
    for c in range(n_cores):
        m = dict(common)
        m["x_in"] = _x_layout(x[c * NSEQ:(c + 1) * NSEQ])
        in_maps.append(m)
    return in_maps


def kernel(**inputs):
    in_maps = make_inputs(**inputs)
    nc = build_program()
    res = run_bass_kernel_spmd(nc, in_maps, core_ids=list(range(8)))
    return np.concatenate([_y_unlayout(r["y_out"]) for r in res.results], axis=0)


# version 16
# speedup vs baseline: 1.1317x; 1.1317x over previous
import contextlib
import numpy as np
import concourse.bass as bass
import concourse.mybir as mybir
from concourse.bass_utils import run_bass_kernel_spmd

F32 = mybir.dt.float32
BF16 = mybir.dt.bfloat16
I32 = mybir.dt.int32
AF = mybir.ActivationFunctionType
ALU = mybir.AluOpType
AXX = mybir.AxisListType.X

P = 128
D = 1024
ND = 8
NSEQ = 4
TSEQ = 2048
TB = 1024
NB = 8
NSUB = 2
SUB = 512
NTT = 4
NM_G = 16
NM_MLP = 32
NM_FFN = 24
EPS = 1e-5
QS = 1.0 / 127.0
INV_D = 1.0 / 1024.0


class Prog:
    def __init__(self):
        self.h = {}
        self.c = {}
        self.waited = {}

    def add_sem(self, name, handle):
        self.h[name] = handle
        self.c[name] = 0

    def inc(self, instr, sem, n=1):
        instr.then_inc(self.h[sem], n)
        self.c[sem] += n
        return (sem, self.c[sem])

    def mark(self, sem):
        return (sem, self.c[sem])

    def wait(self, eng, engname, m):
        if m is None:
            return
        sem, v = m
        if v <= 0:
            return
        key = (engname, sem)
        if self.waited.get(key, -1) >= v:
            return
        eng.wait_ge(self.h[sem], v)
        self.waited[key] = v

    def waits(self, eng, engname, marks):
        for m in marks:
            self.wait(eng, engname, m)

    def init_dmaq(self, qname, semnames):
        if not hasattr(self, "q"):
            self.q = {}
        self.q[qname] = {"sems": list(semnames), "rr": 0}

    def dma(self, eng, engname, qname, out, in_):
        q = self.q[qname]
        i = q["rr"]
        q["rr"] = (i + 1) % len(q["sems"])
        s = q["sems"][i]
        self.wait(eng, engname, (s, self.c[s]))
        return self.inc(eng.dma_start(out=out, in_=in_), s, 16)


def build_program(n_blocks=NB):
    nc = bass.Bass("TRN2", target_bir_lowering=False, debug=False)

    x_in = nc.dram_tensor("x_in", [NB, NSUB, P, NTT, D], F32, kind="ExternalInput").ap()
    y_out = nc.dram_tensor("y_out", [NB, NSUB, P, NTT, D], F32, kind="ExternalOutput").ap()

    w_gate = nc.dram_tensor("w_gate", [P, NM_G, ND, P], F32, kind="ExternalInput").ap()
    w_state = nc.dram_tensor("w_state", [P, ND, ND, P], F32, kind="ExternalInput").ap()
    w_outp = nc.dram_tensor("w_outp", [P, ND, ND, P], F32, kind="ExternalInput").ap()
    w_mlp1 = nc.dram_tensor("w_mlp1", [P, NM_MLP, ND, P], F32, kind="ExternalInput").ap()
    w_mlp2 = nc.dram_tensor("w_mlp2", [P, ND, NM_MLP, P], F32, kind="ExternalInput").ap()
    w_fg = nc.dram_tensor("w_fg", [P, NM_FFN, ND, P], I32, kind="ExternalInput").ap()
    w_fu = nc.dram_tensor("w_fu", [P, NM_FFN, ND, P], I32, kind="ExternalInput").ap()
    w_fd = nc.dram_tensor("w_fd", [P, ND, NM_FFN, P], I32, kind="ExternalInput").ap()

    c_gate_b = nc.dram_tensor("c_gate_b", [P, NM_G], F32, kind="ExternalInput").ap()
    c_state_b = nc.dram_tensor("c_state_b", [P, ND], F32, kind="ExternalInput").ap()
    c_out_b = nc.dram_tensor("c_out_b", [P, ND], F32, kind="ExternalInput").ap()
    c_mlp_b1 = nc.dram_tensor("c_mlp_b1", [P, NM_MLP], F32, kind="ExternalInput").ap()
    c_mlp_b2 = nc.dram_tensor("c_mlp_b2", [P, ND], F32, kind="ExternalInput").ap()
    c_n1w = nc.dram_tensor("c_n1w", [P, ND], F32, kind="ExternalInput").ap()
    c_n1b = nc.dram_tensor("c_n1b", [P, ND], F32, kind="ExternalInput").ap()
    c_n2w = nc.dram_tensor("c_n2w", [P, ND], F32, kind="ExternalInput").ap()
    c_n2b = nc.dram_tensor("c_n2b", [P, ND], F32, kind="ExternalInput").ap()
    r_state_b = nc.dram_tensor("r_state_b", [1, D], F32, kind="ExternalInput").ap()
    c_ident = nc.dram_tensor("c_ident", [P, P], F32, kind="ExternalInput").ap()
    c_ones = nc.dram_tensor("c_ones", [1, P], F32, kind="ExternalInput").ap()

    s_gate = nc.dram_tensor("s_gate", [P, NM_G, ND, P], BF16, kind="Internal").ap()
    s_state = nc.dram_tensor("s_state", [P, ND, ND, P], BF16, kind="Internal").ap()
    s_outp = nc.dram_tensor("s_outp", [P, ND, ND, P], BF16, kind="Internal").ap()
    s_mlp1 = nc.dram_tensor("s_mlp1", [P, NM_MLP, ND, P], BF16, kind="Internal").ap()
    s_mlp2 = nc.dram_tensor("s_mlp2", [P, ND, NM_MLP, P], BF16, kind="Internal").ap()
    s_fg = nc.dram_tensor("s_fg", [P, NM_FFN, ND, P], BF16, kind="Internal").ap()
    s_fu = nc.dram_tensor("s_fu", [P, NM_FFN, ND, P], BF16, kind="Internal").ap()
    s_fd = nc.dram_tensor("s_fd", [P, ND, NM_FFN, P], BF16, kind="Internal").ap()

    pr = Prog()

    with contextlib.ExitStack() as es:
        E = es.enter_context

        X1 = E(nc.sbuf_tensor("X1", [P, 8192], F32))
        FMB = E(nc.sbuf_tensor("FMB", [P, 8192], BF16))
        TTR = E(nc.sbuf_tensor("TTR", [P, 8192], F32))
        T1 = TTR[:, 0:4096]
        T2 = TTR[:, 4096:8192]
        M1R = E(nc.sbuf_tensor("M1R", [P, 16384], BF16))
        M1 = M1R
        W_MIX = M1R
        REG = E(nc.sbuf_tensor("REG", [P, 20480], F32))
        Z = REG[:, 0:4096].bitcast(BF16)
        B1 = REG[:, 4096:8192].bitcast(BF16)
        B2 = REG[:, 8192:12288].bitcast(BF16)
        B3 = REG[:, 12288:16384].bitcast(BF16)
        SCBC = REG[:, 16384:20480]
        W_BIG = REG[:, :].bitcast(BF16)
        A_PRO = REG[:, 0:1024]
        A_FS = REG[:, 9216:10240]
        A_FFNO = REG[:, 10240:14336]
        A_RING = TTR[:, 0:4096].bitcast(BF16)

        def small(name, shape, dtype):
            return E(nc.sbuf_tensor(name, shape, dtype))

        IDF = small("IDF", [P, P], F32)
        ONES = small("ONES", [1, P], F32)
        CB_G = small("CB_G", [P, NM_G], F32)
        CB_S = small("CB_S", [P, ND], F32)
        CB_O = small("CB_O", [P, ND], F32)
        CB_M1 = small("CB_M1", [P, NM_MLP], F32)
        CB_M2 = small("CB_M2", [P, ND], F32)
        CN1W = small("CN1W", [P, ND], F32)
        CN1B = small("CN1B", [P, ND], F32)
        CN2W = small("CN2W", [P, ND], F32)
        CN2B = small("CN2B", [P, ND], F32)
        WMS = small("WMS", [P, ND], BF16)
        HCAR = small("HCAR", [P, ND], BF16)
        BN6 = small("BN6", [P, 96], F32)
        BN2 = small("BN2", [P, 16], F32)
        MV = small("MV", [P, 32], F32)
        ZS = small("ZS", [P, 16], F32)
        RS = small("RS", [P, 8], F32)
        WSA = small("WSA", [P, 8], F32)
        BSB = small("BSB", [1, 4], F32)
        MAGIC = small("MAGIC", [P, 4], I32)
        NEWT = small("NEWT", [P, 8], F32)
        BN6A = small("BN6A", [P, 96], F32)
        BN2A = small("BN2A", [P, 16], F32)
        MVA = small("MVA", [P, 32], F32)
        R0 = small("R0", [1, 1024], F32)
        R1 = small("R1", [1, 1024], F32)
        R2 = small("R2", [1, 1024], F32)

        A_BC = SCBC[:, 0:1024]
        D_BC = SCBC[:, 2048:3072]
        BCH = [SCBC[:, 1024:2048], SCBC[:, 3072:4096]]

        psum = [E(nc.psum_tensor(f"pb{i}", [P, 512], F32)) for i in range(8)]
        bank_fill = [None] * 8
        bank_free = [None] * 8

        for s_ in (["pe", "act", "dve"]
                   + [f"ql{i}" for i in range(8)] + [f"qs{i}" for i in range(4)]):
            pr.add_sem(s_, E(nc.semaphore(s_)))
        pr.init_dmaq("ql", [f"ql{i}" for i in range(8)])
        pr.init_dmaq("qs", [f"qs{i}" for i in range(4)])
        block = E(nc.Block())

        def sync_sec(f):
            @block.sync
            def _(e):
                f(e)

        def gp_sec(f):
            @block.gpsimd
            def _(e):
                f(e)

        def pe_sec(f):
            @block.tensor
            def _(e):
                f(e)

        def act_sec(f):
            @block.scalar
            def _(e):
                f(e)

        def dve_sec(f):
            @block.vector
            def _(e):
                f(e)

        EV = {"act": act_sec, "dve": dve_sec}
        CM = {}

        def prologue():
            def f(sy):
                for name, dst, src in [
                    ("ident", IDF[:], c_ident[:, :]), ("ones", ONES[:], c_ones[:, :]),
                    ("gb", CB_G[:], c_gate_b[:, :]), ("sb", CB_S[:], c_state_b[:, :]),
                    ("ob", CB_O[:], c_out_b[:, :]), ("m1b", CB_M1[:], c_mlp_b1[:, :]),
                    ("m2b", CB_M2[:], c_mlp_b2[:, :]),
                    ("n1w", CN1W[:], c_n1w[:, :]), ("n1b", CN1B[:], c_n1b[:, :]),
                    ("n2w", CN2W[:], c_n2w[:, :]), ("n2b", CN2B[:], c_n2b[:, :]),
                    ("sbr", R2[0:1, :], r_state_b[:, :]),
                ]:
                    CM[name] = pr.dma(sy, "sync", "ql", dst, src)
            sync_sec(f)

            def g(gp):
                prev = None
                for name, dst, src in [
                    ("gate", s_gate, w_gate), ("state", s_state, w_state),
                    ("outp", s_outp, w_outp), ("mlp1", s_mlp1, w_mlp1),
                    ("mlp2", s_mlp2, w_mlp2), ("fg", s_fg, w_fg),
                    ("fu", s_fu, w_fu), ("fd", s_fd, w_fd),
                ]:
                    if prev is not None:
                        pr.wait(gp, "pool", prev)
                    CM["cast_" + name] = pr.dma(
                        gp, "pool", "qs",
                        dst.rearrange("p m k f -> p (m k f)"),
                        src.rearrange("p m k f -> p (m k f)"))
                    prev = CM["cast_" + name]
            gp_sec(g)

            SW = REG[:, 8192:16384]
            SWR = REG[:, 16384:16448]

            def fl(sy):
                CM["wsm"] = pr.dma(sy, "sync", "ql", SW,
                                   w_state.rearrange("p m k f -> p (m k f)"))
            sync_sec(fl)

            def vl(dv):
                pr.wait(dv, "dve", CM["wsm"])
                rm = pr.inc(dv.tensor_reduce(
                    SWR[:, 0:64],
                    SW.rearrange("p (mk f) -> p mk f", f=P),
                    AXX, ALU.add), "dve", 1)
                pr.wait(dv, "dve", rm)
                mm = pr.inc(dv.tensor_copy(WSA[:], SWR[:, 0:ND]), "dve", 1)
                for m in range(1, ND):
                    pr.wait(dv, "dve", mm)
                    mm = pr.inc(dv.tensor_add(WSA[:], WSA[:],
                                              SWR[:, m * ND:(m + 1) * ND]),
                                "dve", 1)
                pr.wait(dv, "dve", mm)
                CM["wsum"] = mm
            dve_sec(vl)

            def vf(dv):
                mgm = pr.inc(dv.memset(MAGIC[:], 0x5f3759df), "dve", 1)
                pr.wait(dv, "dve", mgm)
                CM["wms"] = pr.inc(
                    dv.tensor_scalar(WMS[:], WSA[:], INV_D, None, ALU.mult), "dve", 1)
                pr.wait(dv, "dve", CM["sbr"])
                mm = pr.inc(dv.tensor_reduce(BSB[0:1, 0:1], R2[0:1, :], AXX, ALU.add),
                            "dve", 1)
                pr.wait(dv, "dve", mm)
                CM["bsb"] = pr.inc(
                    dv.tensor_scalar(BSB[0:1, 1:2], BSB[0:1, 0:1], INV_D, None, ALU.mult),
                    "dve", 1)
            dve_sec(vf)

        prologue()

        reg_free = {
            "T1": [], "T2": [], "Z": [], "B1": [],
            "B2": [CM["wsum"]], "B3": [CM["wsum"]],
            "FMB": [], "M1": [], "SCBC": [CM["wms"]], "X1": [],
        }
        CONST_MARKS = [CM[n] for n in (
            "ident", "ones", "gb", "sb", "ob", "m1b", "m2b",
            "n1w", "n1b", "n2w", "n2b", "sbr")]

        def rsqrt_newton(dv, dst, src):
            dsti = dst.bitcast(I32)
            srci = src.bitcast(I32)
            n = dst.shape[1]
            sh = NEWT[:, 0:n].bitcast(I32)
            tq = NEWT[:, 4:4 + n]
            m0 = pr.inc(dv.tensor_scalar(sh, srci, 1, None,
                                         ALU.logical_shift_right), "dve", 1)
            pr.wait(dv, "dve", m0)
            m0 = pr.inc(dv.tensor_tensor(dsti, MAGIC[:, 0:n], sh, ALU.subtract),
                        "dve", 1)
            pr.wait(dv, "dve", m0)
            for _ in range(3):
                m0 = pr.inc(dv.tensor_mul(tq, dst, dst), "dve", 1)
                pr.wait(dv, "dve", m0)
                m0 = pr.inc(dv.tensor_mul(tq, tq, src), "dve", 1)
                pr.wait(dv, "dve", m0)
                m0 = pr.inc(dv.tensor_scalar(tq, tq, -0.5, 1.5, ALU.mult, ALU.add),
                            "dve", 1)
                pr.wait(dv, "dve", m0)
                m0 = pr.inc(dv.tensor_mul(dst, dst, tq), "dve", 1)
                pr.wait(dv, "dve", m0)
            return m0

        def mm_stage(wsb, w_tile, n_m, n_k, rhs_fn, eng, dst_fn, w_ready,
                     func=AF.Identity, bias_col=None, scale=1.0, pre_waits=(),
                     extra_pe_waits=(), banks=(0, 1, 2, 3), subs=(0, 1),
                     group_waits=None, m_order=None):
            sem = eng
            evac_base = pr.c[sem]
            nbk = len(banks)
            idx = [(s, m) for s in subs
                   for m in (m_order if m_order is not None else range(n_m))]
            fills = []

            def fpe(t):
                if not callable(w_ready):
                    pr.wait(t, "pe", w_ready)
                for mk in extra_pe_waits:
                    pr.wait(t, "pe", mk)
                for i, (s, m) in enumerate(idx):
                    if callable(w_ready):
                        pr.wait(t, "pe", w_ready(m))
                    if group_waits is not None:
                        pr.waits(t, "pe", group_waits(s, m))
                    bank = banks[i % nbk]
                    if i >= nbk:
                        pr.wait(t, "pe", (sem, evac_base + i - nbk + 1))
                    else:
                        pr.wait(t, "pe", bank_free[bank])
                    ins = None
                    for k in range(n_k):
                        ins = t.matmul(
                            psum[bank][:, 0:512],
                            wsb[:, w_tile(m, k) * P:(w_tile(m, k) + 1) * P],
                            rhs_fn(s, k), start=(k == 0), stop=(k == n_k - 1))
                    fills.append(pr.inc(ins, "pe", 1))
            pe_sec(fpe)

            marks = []

            def fev(e):
                pr.waits(e, eng, CONST_MARKS)
                for mk in pre_waits:
                    pr.wait(e, eng, mk)
                for i, (s, m) in enumerate(idx):
                    bank = banks[i % nbk]
                    pr.wait(e, eng, fills[i])
                    if eng == "act":
                        ins = e.activation(dst_fn(s, m), psum[bank][:, 0:512], func,
                                           bias=(bias_col(m) if bias_col else 0.0),
                                           scale=scale)
                    else:
                        if bias_col is not None:
                            ins = e.tensor_scalar(dst_fn(s, m), psum[bank][:, 0:512],
                                                  bias_col(m), None, ALU.add)
                        elif scale != 1.0:
                            ins = e.tensor_scalar(dst_fn(s, m), psum[bank][:, 0:512],
                                                  scale, None, ALU.mult)
                        else:
                            ins = e.tensor_copy(dst_fn(s, m), psum[bank][:, 0:512])
                    mk = pr.inc(ins, eng, 1)
                    bank_free[bank] = mk
                    marks.append(mk)
            EV[eng](fev)
            assert pr.c[sem] == evac_base + len(idx)
            return marks

        def tm_to_fm_pe(src, src_ready, eng, banks=(4, 5)):
            sem = eng
            base = pr.c[sem]
            fills = []

            def fpe(t):
                for mk in src_ready:
                    pr.wait(t, "pe", mk)
                pr.wait(t, "pe", CM["ident"])
                for c in range(ND):
                    bank = banks[c % 2]
                    if c >= 2:
                        pr.wait(t, "pe", (sem, base + c - 1))
                    else:
                        pr.wait(t, "pe", bank_free[bank])
                    ins = None
                    for tau in range(NTT):
                        ins = t.transpose(
                            psum[bank][:, tau * P:(tau + 1) * P],
                            src[:, tau * 1024 + c * P: tau * 1024 + (c + 1) * P],
                            IDF[:])
                    fills.append(pr.inc(ins, "pe", 1))
            pe_sec(fpe)
            return fills

        def fm_to_tm_pe(src, src_ready, eng, incs_per_group, banks=(4, 5)):
            sem = eng
            base = pr.c[sem]
            fills = []

            def fpe(t):
                for mk in src_ready:
                    pr.wait(t, "pe", mk)
                pr.wait(t, "pe", CM["ident"])
                for i in range(NTT * 2):
                    tau, cg = divmod(i, 2)
                    bank = banks[cg]
                    if i >= 2:
                        pr.wait(t, "pe", (sem, base + (i - 1) * incs_per_group))
                    else:
                        pr.wait(t, "pe", bank_free[bank])
                    ins = None
                    for j in range(4):
                        c = cg * 4 + j
                        ins = t.transpose(
                            psum[bank][:, j * P:(j + 1) * P],
                            src[:, c * 512 + tau * P: c * 512 + (tau + 1) * P],
                            IDF[:])
                    fills.append(pr.inc(ins, "pe", 1))
            pe_sec(fpe)
            return fills

        x1_store_marks = [None] * NSUB
        wmix_free = None
        scan_h_mark = None

        def load_w(name, dst2, src2, cast_mark, prior_use):
            def f(sy):
                pr.wait(sy, "sync", cast_mark)
                pr.wait(sy, "sync", prior_use)
                CM[name] = pr.dma(sy, "sync", "ql", dst2, src2)
            sync_sec(f)

        def emit_A_chain(b, s, defer_x1gen):
            def fx(sy):
                if s == 0:
                    for mk in reg_free["T1"]:
                        pr.wait(sy, "sync", mk)
                else:
                    pr.wait(sy, "sync", CM[f"x1g{b}_0"])
                CM[f"x{b}_{s}"] = pr.dma(
                    sy, "sync", "ql",
                    T1[:, :], x_in[b, s].rearrange("p tt d -> p (tt d)"))
            sync_sec(fx)

            def fv(dv):
                pr.wait(dv, "dve", CM[f"x{b}_{s}"])
                ins = None
                for tau in range(NTT):
                    g = s * NTT + tau
                    dv.bn_stats(BN6A[:, g * 12:g * 12 + 6],
                                T1[:, tau * 1024:tau * 1024 + 512])
                    ins = dv.bn_stats(BN6A[:, g * 12 + 6:g * 12 + 12],
                                      T1[:, tau * 1024 + 512:(tau + 1) * 1024])
                m0 = pr.inc(ins, "dve", 1)
                pr.wait(dv, "dve", m0)
                for tau in range(NTT):
                    g = s * NTT + tau
                    ins = dv.bn_aggr(BN2A[:, g * 2:g * 2 + 2],
                                     BN6A[:, g * 12:(g + 1) * 12])
                m1_ = pr.inc(ins, "dve", 1)
                pr.wait(dv, "dve", m1_)
                m2_ = pr.inc(dv.tensor_scalar(MVA[:, 24 + s * 4:24 + s * 4 + 4],
                                              BN2A[:, s * 8 + 1:s * 8 + 8:2],
                                              EPS, None, ALU.add), "dve", 1)
                pr.wait(dv, "dve", m2_)
                rsqrt_newton(dv, MVA[:, s * 4:s * 4 + 4],
                             MVA[:, 24 + s * 4:24 + s * 4 + 4])
                m3_ = pr.inc(dv.scalar_tensor_tensor(
                    MVA[:, 8 + s * 4:8 + s * 4 + 4], BN2A[:, s * 8:s * 8 + 8:2],
                    -1.0, MVA[:, s * 4:s * 4 + 4], ALU.mult, ALU.mult), "dve", 1)
                CM[f"ln1c{b}_{s}"] = pr.inc(
                    dv.tensor_scalar(MVA[:, 16 + s * 4:16 + s * 4 + 4],
                                     MVA[:, s * 4:s * 4 + 4], 1.0, None, ALU.add),
                    "dve", 1)
                pr.wait(dv, "dve", CM[f"ln1c{b}_{s}"])
            dve_sec(fv)

            def fa2(dv):
                for mk in reg_free["Z"]:
                    pr.wait(dv, "dve", mk)
                ins = None
                for tau in range(NTT):
                    g = s * NTT + tau
                    m_c = BN2A[:, s * 8 + 2 * tau:s * 8 + 2 * tau + 1]
                    rst = MVA[:, s * 4 + tau:s * 4 + tau + 1]
                    ins = dv.tensor_scalar(
                        Z[:, g * 1024:(g + 1) * 1024],
                        T1[:, tau * 1024:(tau + 1) * 1024], m_c, rst,
                        ALU.subtract, ALU.mult)
                CM[f"zx{b}_{s}"] = pr.inc(ins, "dve", 1)
                pr.wait(dv, "dve", CM[f"zx{b}_{s}"])
            dve_sec(fa2)

            def fv3(dv):
                pr.wait(dv, "dve", CM[f"zx{b}_{s}"])
                m0 = pr.inc(dv.tensor_mul(RS[:, 0:4], MVA[:, s * 4:s * 4 + 4],
                                          MVA[:, s * 4:s * 4 + 4]), "dve", 1)
                pr.wait(dv, "dve", m0)
                m1_ = pr.inc(dv.tensor_mul(RS[:, 0:4], RS[:, 0:4],
                                           BN2A[:, s * 8 + 1:s * 8 + 8:2]), "dve", 1)
                pr.wait(dv, "dve", m1_)
                m2_ = pr.inc(dv.tensor_scalar(RS[:, 0:4], RS[:, 0:4], EPS, None,
                                              ALU.add), "dve", 1)
                pr.wait(dv, "dve", m2_)
                rsqrt_newton(dv, RS[:, 4:8], RS[:, 0:4])
                for mk in reg_free["T2"]:
                    pr.wait(dv, "dve", mk)
                ins = None
                for tau in range(NTT):
                    g = s * NTT + tau
                    ins = dv.tensor_scalar(T2[:, tau * 1024:(tau + 1) * 1024],
                                           Z[:, g * 1024:(g + 1) * 1024],
                                           RS[:, 4 + tau:4 + tau + 1], None,
                                           ALU.mult)
                CM[f"t1{b}_{s}"] = pr.inc(ins, "dve", 1)
            dve_sec(fv3)
            reg_free["T2"] = [CM[f"t1{b}_{s}"]]
            if not defer_x1gen:
                emit_X1gen(b, s)

        def emit_X1gen(b, s):
            def fg(dv):
                if x1_store_marks[s] is not None:
                    pr.wait(dv, "dve", x1_store_marks[s])
                pr.wait(dv, "dve", CM[f"zx{b}_{s}"])
                ins = None
                for tau in range(NTT):
                    g = s * NTT + tau
                    nb = MVA[:, 8 + s * 4 + tau:8 + s * 4 + tau + 1]
                    rp1 = MVA[:, 16 + s * 4 + tau:16 + s * 4 + tau + 1]
                    ins = dv.tensor_scalar(X1[:, g * 1024:(g + 1) * 1024],
                                           T1[:, tau * 1024:(tau + 1) * 1024],
                                           rp1, nb, ALU.mult, ALU.add)
                CM[f"x1g{b}_{s}"] = pr.inc(ins, "dve", 1)
            dve_sec(fg)
            if s == 1:
                reg_free["T1"] = [CM[f"x1g{b}_{s}"]]

        def emit_A_tail_sub(b, s, xn_marks):
            fills = tm_to_fm_pe(T2, [CM[f"t1{b}_{s}"]], "act")

            def fev_xn(ac):
                pr.waits(ac, "act", CONST_MARKS)
                for mk in reg_free["FMB"]:
                    pr.wait(ac, "act", mk)
                for c in range(ND):
                    bank = (4, 5)[c % 2]
                    pr.wait(ac, "act", fills[c])
                    ins = ac.activation(
                        FMB[:, c * 1024 + s * 512: c * 1024 + (s + 1) * 512],
                        psum[bank][:, 0:512], AF.Identity,
                        scale=CN1W[:, c:c + 1], bias=CN1B[:, c:c + 1])
                    mk = pr.inc(ins, "act", 1)
                    bank_free[bank] = mk
                    xn_marks.append(mk)
            act_sec(fev_xn)
            reg_free["T2"] = [pr.mark("pe")]

        def emit_A_full(b, xn_bysub):
            emit_A_chain(b, 0, defer_x1gen=False)
            emit_A_tail_sub(b, 0, xn_bysub[0])
            emit_A_chain(b, 1, defer_x1gen=False)
            emit_A_tail_sub(b, 1, xn_bysub[1])

        pending_xn = [[], []]
        emit_A_full(0, pending_xn)

        for b in range(n_blocks):
            xn_by_sub = pending_xn
            xn_marks = xn_by_sub[0] + xn_by_sub[1]
            pending_xn = [[], []]

            load_w(f"gate{b}", W_MIX[:, 0:16384],
                   s_gate.rearrange("p m k f -> p (m k f)"),
                   CM["cast_gate"], wmix_free)

            def gate_dst(s, m):
                if m < 8:
                    return B1[:, m * 1024 + s * 512: m * 1024 + (s + 1) * 512]
                return B2[:, (m - 8) * 1024 + s * 512: (m - 8) * 1024 + (s + 1) * 512]

            g1s_marks = mm_stage(
                W_MIX, lambda m, k: m * ND + k, NM_G, ND,
                lambda s, k: FMB[:, k * 1024 + s * 512: k * 1024 + (s + 1) * 512],
                "act", gate_dst, CM[f"gate{b}"], func=AF.Silu,
                pre_waits=list(reg_free["B1"]) + list(reg_free["B2"]),
                group_waits=lambda s, m: xn_by_sub[s], m_order=list(range(8)))
            gate_lo_done = pr.mark("pe")

            load_w(f"state{b}", W_MIX[:, 0:8192],
                   s_state.rearrange("p m k f -> p (m k f)"),
                   CM["cast_state"], gate_lo_done)

            g2s_marks = mm_stage(
                W_MIX, lambda m, k: m * ND + k, NM_G, ND,
                lambda s, k: FMB[:, k * 1024 + s * 512: k * 1024 + (s + 1) * 512],
                "act", gate_dst, CM[f"gate{b}"], func=AF.Silu,
                group_waits=lambda s, m: xn_by_sub[s],
                m_order=list(range(8, 16)))
            gate_last_mm = pr.mark("pe")

            def fv1(dv):
                pr.waits(dv, "dve", g1s_marks)
                pr.waits(dv, "dve", xn_marks)
                for mk in reg_free["B3"]:
                    pr.wait(dv, "dve", mk)
                CM[f"v1{b}"] = pr.inc(dv.tensor_mul(B3, FMB[:], B1), "dve", 1)
            dve_sec(fv1)

            dpre_base = pr.c["dve"]

            def fdpre_pe(t):
                pr.wait(t, "pe", CM["wms"])
                pr.waits(t, "pe", xn_marks)
                for s in range(NSUB):
                    if s > 0:
                        pr.wait(t, "pe", ("dve", dpre_base + s))
                    else:
                        pr.wait(t, "pe", bank_free[6])
                    ins = None
                    for k in range(ND):
                        ins = t.matmul(
                            psum[6][0:1, 0:512], WMS[:, k:k + 1],
                            FMB[:, k * 1024 + s * 512: k * 1024 + (s + 1) * 512],
                            start=(k == 0), stop=(k == ND - 1))
                    CM[f"dpref{b}_{s}"] = pr.inc(ins, "pe", 1)
            pe_sec(fdpre_pe)

            def fdpre_dv(dv):
                pr.wait(dv, "dve", CM["bsb"])
                for s in range(NSUB):
                    pr.wait(dv, "dve", CM[f"dpref{b}_{s}"])
                    mk = pr.inc(dv.tensor_scalar(R0[0:1, s * 512:(s + 1) * 512],
                                                 psum[6][0:1, 0:512], BSB[0:1, 1:2],
                                                 None, ALU.add), "dve", 1)
                    bank_free[6] = mk
                CM[f"dpre{b}"] = pr.mark("dve")
            dve_sec(fdpre_dv)
            assert pr.c["dve"] == dpre_base + NSUB

            def fda(ac):
                pr.wait(ac, "act", CM[f"dpre{b}"])
                m0 = pr.inc(ac.activation(R1[0:1, :], R0[0:1, :], AF.Abs), "act", 1)
                pr.wait(ac, "act", m0)
                m1_ = pr.inc(ac.activation(R2[0:1, :], R1[0:1, :], AF.Exp, scale=-1.0),
                             "act", 1)
                pr.wait(ac, "act", m1_)
                CM[f"sp{b}"] = pr.inc(
                    ac.activation(R1[0:1, :], R2[0:1, :], AF.Ln, bias=1.0), "act", 1)
                pr.wait(ac, "act", ("dve", pr.c["dve"] + 2))
                CM[f"arow{b}"] = pr.inc(
                    ac.activation(R1[0:1, :], R0[0:1, :], AF.Exp, scale=-1.0), "act", 1)
            act_sec(fda)

            def fdv2(dv):
                pr.wait(dv, "dve", CM[f"sp{b}"])
                m0 = pr.inc(dv.tensor_scalar_max(R2[0:1, :], R0[0:1, :], 0.0), "dve", 1)
                pr.wait(dv, "dve", m0)
                CM[f"delta{b}"] = pr.inc(
                    dv.tensor_add(R0[0:1, :], R2[0:1, :], R1[0:1, :]), "dve", 1)
            dve_sec(fdv2)

            bc_base = pr.c["dve"]

            def fbc_pe(t):
                pr.wait(t, "pe", CM["ones"])
                i = 0
                for bank, rm, row in [(6, f"delta{b}", R0), (7, f"arow{b}", R1)]:
                    pr.wait(t, "pe", CM[rm])
                    for s in range(NSUB):
                        if i % 2 == 1:
                            pr.wait(t, "pe", ("dve", bc_base + i))
                        else:
                            pr.wait(t, "pe", bank_free[bank])
                        ins = t.matmul(psum[bank][:, 0:512], ONES[:],
                                       row[0:1, s * 512:(s + 1) * 512],
                                       start=True, stop=True)
                        CM[f"bcf{b}_{i}"] = pr.inc(ins, "pe", 1)
                        i += 1
            pe_sec(fbc_pe)

            bc_marks = []

            def fbc_dv(dv):
                for mk in reg_free["SCBC"]:
                    pr.wait(dv, "dve", mk)
                i = 0
                for bank, dstv in [(6, D_BC), (7, A_BC)]:
                    for s in range(NSUB):
                        pr.wait(dv, "dve", CM[f"bcf{b}_{i}"])
                        mk = pr.inc(dv.tensor_copy(dstv[:, s * 512:(s + 1) * 512],
                                                   psum[bank][:, 0:512]), "dve", 1)
                        bank_free[bank] = mk
                        bc_marks.append(mk)
                        i += 1
            dve_sec(fbc_dv)
            assert pr.c["dve"] == bc_base + 4
            reg_free["SCBC"] = []

            u_marks = mm_stage(
                W_MIX, lambda m, k: m * ND + k, ND, ND,
                lambda s, k: B3[:, k * 1024 + s * 512: k * 1024 + (s + 1) * 512],
                "dve",
                lambda s, m: B1[:, m * 1024 + s * 512: m * 1024 + (s + 1) * 512],
                CM[f"state{b}"], bias_col=lambda m: CB_S[:, m:m + 1],
                pre_waits=[CM[f"v1{b}"]],
                extra_pe_waits=[CM[f"v1{b}"]] + g1s_marks)

            g2s_by_sub = [g2s_marks[0:8], g2s_marks[8:16]]
            bm_marks = mm_stage(
                W_MIX, lambda m, k: m * ND + k, ND, ND,
                lambda s, k: B2[:, k * 1024 + s * 512: k * 1024 + (s + 1) * 512],
                "act",
                lambda s, m: B3[:, m * 1024 + s * 512: m * 1024 + (s + 1) * 512],
                CM[f"state{b}"], bias_col=lambda m: CB_S[:, m:m + 1],
                pre_waits=[CM[f"v1{b}"]],
                extra_pe_waits=[CM[f"v1{b}"]],
                group_waits=lambda s, m: g2s_by_sub[s])
            state_last_mm = pr.mark("pe")

            load_w(f"outp{b}", W_MIX[:, 0:8192],
                   s_outp.rearrange("p m k f -> p (m k f)"),
                   CM["cast_outp"], state_last_mm)

            load_w(f"mlp1a{b}", W_BIG[:, 16384:24576],
                   s_mlp1[:, 8:16].rearrange("p m k f -> p (m k f)"),
                   CM["cast_mlp1"], state_last_mm)

            scan_marks = []

            def fscan(dv):
                pr.waits(dv, "dve", bc_marks)
                pr.waits(dv, "dve", u_marks)
                udm = pr.inc(dv.tensor_tensor(
                    B1[:, :].rearrange("p (c f) -> p c f", c=ND),
                    B1[:, :].rearrange("p (c f) -> p c f", c=ND),
                    D_BC[:, None, :].broadcast_to([P, ND, 1024]), ALU.mult),
                    "dve", 1)
                pr.wait(dv, "dve", udm)
                for c in range(ND):
                    pr.wait(dv, "dve", bm_marks[8 + c])
                    m0 = pr.inc(dv.tensor_tensor(BCH[c % 2],
                                                 B1[:, c * 1024:(c + 1) * 1024],
                                                 B3[:, c * 1024:(c + 1) * 1024],
                                                 ALU.mult), "dve", 1)
                    pr.wait(dv, "dve", m0)
                    init = 0.0 if b % 2 == 0 else HCAR[:, c:c + 1]
                    ins = dv.tensor_tensor_scan(B1[:, c * 1024:(c + 1) * 1024],
                                                A_BC, BCH[c % 2], init,
                                                ALU.mult, ALU.add)
                    scan_marks.append(pr.inc(ins, "dve", 1))
                if b % 2 == 0:
                    pr.waits(dv, "dve", scan_marks)
                    ins = None
                    for c in range(ND):
                        ins = dv.tensor_copy(
                            HCAR[:, c:c + 1],
                            B1[:, (c + 1) * 1024 - 1:(c + 1) * 1024])
                    CM[f"hcar{b}"] = pr.inc(ins, "dve", 1)
            dve_sec(fscan)
            if b % 2 == 0:
                scan_h_mark = CM[f"hcar{b}"]

            scan_done_dve = pr.mark("dve")
            load_w(f"mlp1b{b}", W_BIG[:, 24576:40960],
                   s_mlp1[:, 16:32].rearrange("p m k f -> p (m k f)"),
                   CM["cast_mlp1"], scan_done_dve)

            x2n_by_sub = [[], []]
            xb_last = None
            t1_free = list(reg_free["T1"])

            def emit_out_mm(s, pre):
                return mm_stage(
                    W_MIX, lambda m, k: m * ND + k, ND, ND,
                    lambda s_, k: B1[:, k * 1024 + s * 512: k * 1024 + (s + 1) * 512],
                    "act", lambda s_, m: T1[:, m * 512:(m + 1) * 512],
                    CM[f"outp{b}"], bias_col=lambda m: CB_O[:, m:m + 1],
                    pre_waits=pre, extra_pe_waits=list(scan_marks), subs=(s,))

            def emit_fcons(s, fills):
                xb_marks = []

                def fcons(dv, s=s, fills=fills, xb_marks=xb_marks):
                    pr.wait(dv, "dve", CM[f"x1g{b}_{s}"])
                    for i in range(NTT * 2):
                        tau, cg = divmod(i, 2)
                        bank = (4, 5)[cg]
                        g = s * NTT + tau
                        pr.wait(dv, "dve", fills[i])
                        sl = slice(g * 1024 + cg * 512, g * 1024 + (cg + 1) * 512)
                        sl2 = slice(tau * 1024 + cg * 512, tau * 1024 + (cg + 1) * 512)
                        pr.inc(dv.tensor_add(X1[:, sl], X1[:, sl],
                                             psum[bank][:, 0:512]), "dve", 1)
                        ins = dv.tensor_add(T2[:, sl2], Z[:, sl], psum[bank][:, 0:512])
                        mk = pr.inc(ins, "dve", 1)
                        bank_free[bank] = mk
                        xb_marks.append(mk)
                dve_sec(fcons)
                return xb_marks

            def emit_ln2_t2(s, xb_marks):
                def fv5(dv, s=s, xb_marks=xb_marks):
                    pr.waits(dv, "dve", xb_marks)
                    ins = None
                    for tau in range(NTT):
                        g = s * NTT + tau
                        dv.bn_stats(BN6[:, g * 12:g * 12 + 6],
                                    T2[:, tau * 1024:tau * 1024 + 512])
                        ins = dv.bn_stats(BN6[:, g * 12 + 6:g * 12 + 12],
                                          T2[:, tau * 1024 + 512:(tau + 1) * 1024])
                    m0 = pr.inc(ins, "dve", 1)
                    pr.wait(dv, "dve", m0)
                    for tau in range(NTT):
                        g = s * NTT + tau
                        ins = dv.bn_aggr(BN2[:, g * 2:g * 2 + 2],
                                         BN6[:, g * 12:(g + 1) * 12])
                    m1_ = pr.inc(ins, "dve", 1)
                    pr.wait(dv, "dve", m1_)
                    m2_ = pr.inc(dv.tensor_scalar(MV[:, 24 + s * 4:24 + s * 4 + 4],
                                                  BN2[:, s * 8 + 1:s * 8 + 8:2],
                                                  EPS, None, ALU.add), "dve", 1)
                    pr.wait(dv, "dve", m2_)
                    rsqrt_newton(dv, MV[:, s * 4:s * 4 + 4],
                                 MV[:, 24 + s * 4:24 + s * 4 + 4])
                    ins = None
                    for tau in range(NTT):
                        g = s * NTT + tau
                        ins = dv.tensor_scalar(
                            T2[:, tau * 1024:(tau + 1) * 1024],
                            T2[:, tau * 1024:(tau + 1) * 1024],
                            BN2[:, g * 2:g * 2 + 1],
                            MV[:, s * 4 + tau:s * 4 + tau + 1],
                            ALU.subtract, ALU.mult)
                    CM[f"t2{b}_{s}"] = pr.inc(ins, "dve", 1)
                dve_sec(fv5)

            def emit_x2n(s, fills2):
                def fev_x2n(ac, s=s, fills2=fills2):
                    pr.waits(ac, "act", CONST_MARKS)
                    for c in range(ND):
                        bank = (4, 5)[c % 2]
                        pr.wait(ac, "act", fills2[c])
                        ins = ac.activation(
                            FMB[:, c * 1024 + s * 512: c * 1024 + (s + 1) * 512],
                            psum[bank][:, 0:512], AF.Identity,
                            scale=CN2W[:, c:c + 1], bias=CN2B[:, c:c + 1])
                        mk = pr.inc(ins, "act", 1)
                        bank_free[bank] = mk
                        x2n_by_sub[s].append(mk)
                act_sec(fev_x2n)

            ssm0 = emit_out_mm(0, t1_free)
            fills_s0 = fm_to_tm_pe(T1, ssm0, "dve", 2)
            ssmT0_done = pr.mark("pe")
            xb0 = emit_fcons(0, fills_s0)
            emit_ln2_t2(0, xb0)
            xb_last = xb0[-1]

            ssm1 = emit_out_mm(1, [ssmT0_done])
            outmm1_pe = pr.mark("pe")
            load_w(f"mlp1c{b}", W_BIG[:, 8192:16384],
                   s_mlp1[:, 0:8].rearrange("p m k f -> p (m k f)"),
                   CM["cast_mlp1"], outmm1_pe)

            def mlp1_ready(m, b=b):
                if m < 8:
                    return CM[f"mlp1c{b}"]
                if m < 16:
                    return CM[f"mlp1a{b}"]
                return CM[f"mlp1b{b}"]
            mlp1_m_order = list(range(8, 32)) + list(range(0, 8))
            fills_t20 = tm_to_fm_pe(T2, [CM[f"t2{b}_0"]], "act")
            emit_x2n(0, fills_t20)
            t2T0_done = pr.mark("pe")
            fills_s1 = fm_to_tm_pe(T1, ssm1, "dve", 2)
            ssmT1_done = pr.mark("pe")
            xb1 = emit_fcons(1, fills_s1)
            emit_ln2_t2(1, xb1)
            xb_last = xb1[-1]

            m1_0 = mm_stage(
                W_BIG, lambda m, k: 64 + m * ND + k, NM_MLP, ND,
                lambda s_, k: FMB[:, k * 1024 + 0 * 512: k * 1024 + 1 * 512],
                "act", lambda s_, m: M1[:, m * 512:(m + 1) * 512],
                mlp1_ready, func=AF.Gelu,
                bias_col=lambda m: CB_M1[:, m:m + 1],
                pre_waits=list(reg_free["M1"]),
                extra_pe_waits=list(x2n_by_sub[0]) + list(reg_free["M1"]),
                subs=(0,), m_order=mlp1_m_order)

            fills_t21 = tm_to_fm_pe(T2, [CM[f"t2{b}_1"]], "act")
            emit_x2n(1, fills_t21)
            t2T1_done = pr.mark("pe")

            prev_ring = [None, None]
            ringslots = [0, 4096]

            def emit_mlp2(s, ring_free_mark, m1_marks):
                pe_base2 = pr.c["pe"]
                dve_base2 = pr.c["dve"]
                ring_marks = {}

                def fr(sy, s=s):
                    pr.wait(sy, "sync", CM["cast_mlp2"])
                    pr.wait(sy, "sync", ring_free_mark)
                    for m in range(ND):
                        slot = ringslots[m % 2]
                        if m < 2:
                            pr.wait(sy, "sync", prev_ring[m % 2])
                        else:
                            pr.wait(sy, "sync", ("pe", pe_base2 + m - 1))
                        ring_marks[m] = pr.dma(
                            sy, "sync", "ql", A_RING[:, slot:slot + 4096],
                            s_mlp2[:, m].rearrange("p k f -> p (k f)"))
                sync_sec(fr)

                def fpe2(t, s=s):
                    pr.waits(t, "pe", m1_marks)
                    for m in range(ND):
                        slot = ringslots[m % 2]
                        bank = m % 4
                        pr.wait(t, "pe", ring_marks[m])
                        if m >= 4:
                            pr.wait(t, "pe", ("dve", dve_base2 + m - 3))
                        else:
                            pr.wait(t, "pe", bank_free[bank])
                        ins = None
                        for k in range(NM_MLP):
                            ins = t.matmul(psum[bank][:, 0:512],
                                           A_RING[:, slot + k * P:slot + (k + 1) * P],
                                           M1[:, k * 512:(k + 1) * 512],
                                           start=(k == 0), stop=(k == NM_MLP - 1))
                        CM[f"m2f{b}_{s}_{m}"] = pr.inc(ins, "pe", 1)
                        prev_ring[m % 2] = CM[f"m2f{b}_{s}_{m}"]
                pe_sec(fpe2)

                mlpo_marks = []

                def fev2(dv, s=s):
                    for mk in reg_free["T2"]:
                        pr.wait(dv, "dve", mk)
                    for m in range(ND):
                        bank = m % 4
                        pr.wait(dv, "dve", CM[f"m2f{b}_{s}_{m}"])
                        ins = dv.tensor_scalar(T2[:, m * 512:(m + 1) * 512],
                                               psum[bank][:, 0:512],
                                               CB_M2[:, m:m + 1], None, ALU.add)
                        mk = pr.inc(ins, "dve", 1)
                        bank_free[bank] = mk
                        mlpo_marks.append(mk)
                dve_sec(fev2)
                assert pr.c["dve"] == dve_base2 + ND

                fills3 = fm_to_tm_pe(T2, mlpo_marks, "dve", 1)

                def fcons2(dv, s=s, fills3=fills3):
                    for i in range(NTT * 2):
                        tau, cg = divmod(i, 2)
                        bank = (4, 5)[cg]
                        g = s * NTT + tau
                        pr.wait(dv, "dve", fills3[i])
                        sl = slice(g * 1024 + cg * 512, g * 1024 + (cg + 1) * 512)
                        ins = dv.tensor_add(X1[:, sl], X1[:, sl], psum[bank][:, 0:512])
                        mk = pr.inc(ins, "dve", 1)
                        bank_free[bank] = mk
                dve_sec(fcons2)
                reg_free["T2"] = [pr.mark("pe")]
                CM[f"xmid{b}_{s}"] = pr.mark("dve")

            reg_free["T2"] = [t2T1_done]
            emit_mlp2(0, ssmT1_done, m1_0)
            mlp2_0_pe = pr.mark("pe")

            m1_1 = mm_stage(
                W_BIG, lambda m, k: 64 + m * ND + k, NM_MLP, ND,
                lambda s_, k: FMB[:, k * 1024 + 1 * 512: k * 1024 + 2 * 512],
                "act", lambda s_, m: M1[:, m * 512:(m + 1) * 512],
                mlp1_ready, func=AF.Gelu,
                bias_col=lambda m: CB_M1[:, m:m + 1],
                pre_waits=[mlp2_0_pe],
                extra_pe_waits=list(x2n_by_sub[1]) + [mlp2_0_pe],
                subs=(1,), m_order=mlp1_m_order)

            def emit_ln3(s, affno_free):
                def fv7(dv, s=s):
                    pr.wait(dv, "dve", CM[f"xmid{b}_{s}"])
                    ins = None
                    for tau in range(NTT):
                        g = s * NTT + tau
                        dv.bn_stats(BN6[:, g * 12:g * 12 + 6],
                                    X1[:, g * 1024:g * 1024 + 512])
                        ins = dv.bn_stats(BN6[:, g * 12 + 6:g * 12 + 12],
                                          X1[:, g * 1024 + 512:(g + 1) * 1024])
                    m0 = pr.inc(ins, "dve", 1)
                    pr.wait(dv, "dve", m0)
                    for tau in range(NTT):
                        g = s * NTT + tau
                        ins = dv.bn_aggr(BN2[:, g * 2:g * 2 + 2],
                                         BN6[:, g * 12:(g + 1) * 12])
                    m1_ = pr.inc(ins, "dve", 1)
                    pr.wait(dv, "dve", m1_)
                    m2_ = pr.inc(dv.tensor_scalar(MV[:, 24 + s * 4:24 + s * 4 + 4],
                                                  BN2[:, s * 8 + 1:s * 8 + 8:2],
                                                  EPS, None, ALU.add), "dve", 1)
                    pr.wait(dv, "dve", m2_)
                    rsqrt_newton(dv, MV[:, s * 4:s * 4 + 4],
                                 MV[:, 24 + s * 4:24 + s * 4 + 4])
                    CM[f"ln3c{b}_{s}"] = pr.inc(dv.scalar_tensor_tensor(
                        MV[:, 8 + s * 4:8 + s * 4 + 4], BN2[:, s * 8:s * 8 + 8:2],
                        -1.0, MV[:, s * 4:s * 4 + 4], ALU.mult, ALU.mult), "dve", 1)
                dve_sec(fv7)

                def fa8(ac, s=s, affno_free=affno_free):
                    pr.wait(ac, "act", CM[f"ln3c{b}_{s}"])
                    for mk in affno_free:
                        pr.wait(ac, "act", mk)
                    ins = None
                    for tau in range(NTT):
                        g = s * NTT + tau
                        ins = ac.activation(
                            A_FFNO[:, tau * 1024:(tau + 1) * 1024],
                            X1[:, g * 1024:(g + 1) * 1024], AF.Identity,
                            scale=MV[:, s * 4 + tau:s * 4 + tau + 1],
                            bias=MV[:, 8 + s * 4 + tau:8 + s * 4 + tau + 1])
                    CM[f"t3{b}_{s}"] = pr.inc(ins, "act", 1)
                act_sec(fa8)

            m11_pe = pr.mark("pe")
            emit_ln3(0, [m11_pe])

            emit_mlp2(1, ssmT1_done, m1_1)
            reg_free["M1"] = [pr.mark("pe")]
            reg_free["Z"] = [xb_last]
            mlp_last_pe = pr.mark("pe")
            reg_free["T1"] = [CM[f"m2f{b}_1_{ND - 1}"]]

            prev_g = [None, None]
            prev_u = [None, None]
            prev_d = [None, None]
            rgs = [8192, 9216]
            rus = [10240, 11264]
            rds = [12288, 15360]
            ringslots_d = rds

            def emit_zn(s):
                fills4 = tm_to_fm_pe(A_FFNO, [CM[f"t3{b}_{s}"]], "dve")
                zn_marks = []

                def fev_zn(dv, s=s, fills4=fills4):
                    for c in range(ND):
                        bank = (4, 5)[c % 2]
                        pr.wait(dv, "dve", fills4[c])
                        ins = dv.tensor_copy(
                            FMB[:, c * 1024 + s * 512: c * 1024 + (s + 1) * 512],
                            psum[bank][:, 0:512])
                        mk = pr.inc(ins, "dve", 1)
                        bank_free[bank] = mk
                        zn_marks.append(mk)
                dve_sec(fev_zn)
                return zn_marks, pr.mark("pe")

            def emit_gu(s, zn_marks, m1_free):
                pe_base3 = pr.c["pe"]
                act_base3 = pr.c["act"]
                dve_base3 = pr.c["dve"]

                def frg(sy, s=s):
                    pr.wait(sy, "sync", CM["cast_fg"])
                    pr.wait(sy, "sync", CM["cast_fu"])
                    pr.wait(sy, "sync", mlp_last_pe)
                    for m in range(NM_FFN):
                        sg_slot = rgs[m % 2]
                        su_slot = rus[m % 2]
                        if m < 2:
                            pr.wait(sy, "sync", prev_g[m % 2])
                        else:
                            pr.wait(sy, "sync", ("pe", pe_base3 + (m - 2) * 2 + 1))
                        CM[f"rg{b}_{s}_{m}"] = pr.dma(
                            sy, "sync", "ql", W_BIG[:, sg_slot:sg_slot + 1024],
                            s_fg[:, m].rearrange("p k f -> p (k f)"))
                        if m < 2:
                            pr.wait(sy, "sync", prev_u[m % 2])
                        else:
                            pr.wait(sy, "sync", ("pe", pe_base3 + (m - 2) * 2 + 2))
                        CM[f"ru{b}_{s}_{m}"] = pr.dma(
                            sy, "sync", "ql", W_BIG[:, su_slot:su_slot + 1024],
                            s_fu[:, m].rearrange("p k f -> p (k f)"))
                sync_sec(frg)

                def fpe3(t, s=s, act_base3=act_base3, dve_base3=dve_base3):
                    pr.waits(t, "pe", m1_free)
                    for m in range(NM_FFN):
                        sg_slot = rgs[m % 2]
                        su_slot = rus[m % 2]
                        bg = m % 2
                        bu = 2 + m % 2
                        pr.wait(t, "pe", CM[f"rg{b}_{s}_{m}"])
                        if m < 2:
                            pr.wait(t, "pe", bank_free[bg])
                            pr.waits(t, "pe", zn_marks)
                        else:
                            pr.wait(t, "pe", ("act", act_base3 + m - 1))
                        ins = None
                        for k in range(ND):
                            ins = t.matmul(
                                psum[bg][:, 0:512],
                                W_BIG[:, sg_slot + k * P:sg_slot + (k + 1) * P],
                                FMB[:, k * 1024 + s * 512: k * 1024 + (s + 1) * 512],
                                start=(k == 0), stop=(k == ND - 1))
                        CM[f"fgf{b}_{s}_{m}"] = pr.inc(ins, "pe", 1)
                        pr.wait(t, "pe", CM[f"ru{b}_{s}_{m}"])
                        if m < 2:
                            pr.wait(t, "pe", bank_free[bu])
                        else:
                            pr.wait(t, "pe", ("dve", dve_base3 + (m - 2) * 2 + 1))
                        for k in range(ND):
                            ins = t.matmul(
                                psum[bu][:, 0:512],
                                W_BIG[:, su_slot + k * P:su_slot + (k + 1) * P],
                                FMB[:, k * 1024 + s * 512: k * 1024 + (s + 1) * 512],
                                start=(k == 0), stop=(k == ND - 1))
                        CM[f"fuf{b}_{s}_{m}"] = pr.inc(ins, "pe", 1)
                pe_sec(fpe3)

                def fact3(ac, s=s, dve_base3=dve_base3):
                    for m in range(NM_FFN):
                        bg = m % 2
                        pr.wait(ac, "act", CM[f"fgf{b}_{s}_{m}"])
                        if m > 0:
                            pr.wait(ac, "act", ("dve", dve_base3 + (m - 1) * 2 + 2))
                        elif CM.get(f"hlast{b}_{1 - s}") is not None and s == 1:
                            pr.wait(ac, "act", CM[f"hlast{b}_0"])
                        ins = ac.activation(A_FS[:, 0:512], psum[bg][:, 0:512],
                                            AF.Silu, scale=QS)
                        CM[f"fsg{b}_{s}_{m}"] = pr.inc(ins, "act", 1)
                act_sec(fact3)
                assert pr.c["act"] == act_base3 + NM_FFN

                h_marks = []

                def fdve3(dv, s=s):
                    for m in range(NM_FFN):
                        bu = 2 + m % 2
                        pr.wait(dv, "dve", CM[f"fuf{b}_{s}_{m}"])
                        if s == 1 and m == 0:
                            pr.wait(dv, "dve", CM[f"fdf{b}_0_{ND - 1}"])
                        if h_marks:
                            pr.wait(dv, "dve", h_marks[-1])
                        m0 = pr.inc(dv.tensor_scalar(A_FS[:, 512:1024],
                                                     psum[bu][:, 0:512], QS, None,
                                                     ALU.mult), "dve", 1)
                        pr.wait(dv, "dve", m0)
                        pr.wait(dv, "dve", CM[f"fsg{b}_{s}_{m}"])
                        ins = dv.tensor_mul(M1[:, m * 512:(m + 1) * 512],
                                            A_FS[:, 0:512], A_FS[:, 512:1024])
                        mk = pr.inc(ins, "dve", 1)
                        h_marks.append(mk)
                dve_sec(fdve3)
                assert pr.c["dve"] == dve_base3 + 2 * NM_FFN
                CM[f"hlast{b}_{s}"] = h_marks[-1]
                prev_g[0] = CM[f"fgf{b}_{s}_{NM_FFN - 2}"]
                prev_g[1] = CM[f"fgf{b}_{s}_{NM_FFN - 1}"]
                prev_u[0] = CM[f"fuf{b}_{s}_{NM_FFN - 2}"]
                prev_u[1] = CM[f"fuf{b}_{s}_{NM_FFN - 1}"]
                return h_marks

            def emit_down(s, h_marks, extra_pe=()):
                pe_base4 = pr.c["pe"]

                def frd(sy, s=s):
                    pr.wait(sy, "sync", CM["cast_fd"])
                    pr.wait(sy, "sync", mlp_last_pe)
                    for m in range(ND):
                        slot = rds[m % 2]
                        if m < 2:
                            pr.wait(sy, "sync", prev_d[m % 2])
                        else:
                            pr.wait(sy, "sync", ("pe", pe_base4 + m - 1))
                        CM[f"rd{b}_{s}_{m}"] = pr.dma(
                            sy, "sync", "ql", W_BIG[:, slot:slot + 3072],
                            s_fd[:, m].rearrange("p k f -> p (k f)"))
                sync_sec(frd)

                act_base4 = pr.c["act"]

                def fpe4(t, s=s, act_base4=act_base4):
                    pr.waits(t, "pe", h_marks)
                    for mk in extra_pe:
                        pr.wait(t, "pe", mk)
                    for m in range(ND):
                        slot = rds[m % 2]
                        bank = m % 4
                        pr.wait(t, "pe", CM[f"rd{b}_{s}_{m}"])
                        if m >= 4:
                            pr.wait(t, "pe", ("act", act_base4 + m - 3))
                        else:
                            pr.wait(t, "pe", bank_free[bank])
                        ins = None
                        for k in range(NM_FFN):
                            ins = t.matmul(psum[bank][:, 0:512],
                                           W_BIG[:, slot + k * P:slot + (k + 1) * P],
                                           M1[:, k * 512:(k + 1) * 512],
                                           start=(k == 0), stop=(k == NM_FFN - 1))
                        CM[f"fdf{b}_{s}_{m}"] = pr.inc(ins, "pe", 1)
                pe_sec(fpe4)
                prev_d[0] = CM[f"fdf{b}_{s}_{ND - 2}"]
                prev_d[1] = CM[f"fdf{b}_{s}_{ND - 1}"]
                return act_base4

            def emit_fev4(s, act_base4):
                ffno_marks = []

                def fev4(ac, s=s):
                    for m in range(ND):
                        bank = m % 4
                        pr.wait(ac, "act", CM[f"fdf{b}_{s}_{m}"])
                        ins = ac.activation(A_FFNO[:, m * 512:(m + 1) * 512],
                                            psum[bank][:, 0:512], AF.Identity,
                                            scale=QS)
                        mk = pr.inc(ins, "act", 1)
                        bank_free[bank] = mk
                        ffno_marks.append(mk)
                act_sec(fev4)
                assert pr.c["act"] == act_base4 + ND
                return ffno_marks

            def emit_ffn_tail(s, ffno_marks):
                fills5 = fm_to_tm_pe(A_FFNO, ffno_marks, "dve", 1)
                xo_marks = []

                def fcons3(dv, s=s, fills5=fills5, xo_marks=xo_marks):
                    for i in range(NTT * 2):
                        tau, cg = divmod(i, 2)
                        bank = (4, 5)[cg]
                        g = s * NTT + tau
                        pr.wait(dv, "dve", fills5[i])
                        sl = slice(g * 1024 + cg * 512, g * 1024 + (cg + 1) * 512)
                        ins = dv.tensor_add(X1[:, sl], X1[:, sl], psum[bank][:, 0:512])
                        mk = pr.inc(ins, "dve", 1)
                        bank_free[bank] = mk
                        xo_marks.append(mk)
                dve_sec(fcons3)

                def fst(gp, s=s, xo_marks=xo_marks):
                    pr.waits(gp, "pool", xo_marks)
                    x1_store_marks[s] = pr.dma(
                        gp, "pool", "qs",
                        y_out[b, s].rearrange("p tt d -> p (tt d)"),
                        X1[:, s * 4096:(s + 1) * 4096])
                gp_sec(fst)

            zn0, zn0_pe = emit_zn(0)
            h0 = emit_gu(0, zn0, reg_free["M1"])
            emit_ln3(1, [zn0_pe])
            zn1, _ = emit_zn(1)
            d0base = emit_down(0, h0)
            ffno0 = emit_fev4(0, d0base)
            h1 = emit_gu(1, zn1, [])
            ffn_gu_done = pr.mark("pe")
            emit_ffn_tail(0, ffno0)
            if b + 1 < n_blocks:
                emit_A_chain(b + 1, 0, defer_x1gen=True)
                emit_X1gen(b + 1, 0)
            d1base = emit_down(1, h1)
            ffno1 = emit_fev4(1, d1base)
            if b + 1 < n_blocks:
                wmix_free_next = pr.mark("pe")
                reg_free["FMB"] = [ffn_gu_done]
                emit_A_tail_sub(b + 1, 0, pending_xn[0])
            emit_ffn_tail(1, ffno1)
            reg_free["M1"] = [pr.mark("pe")]
            if b + 1 < n_blocks:
                emit_A_chain(b + 1, 1, defer_x1gen=True)
                emit_X1gen(b + 1, 1)
                emit_A_tail_sub(b + 1, 1, pending_xn[1])

            endpe = pr.mark("pe")
            enddve = pr.mark("dve")
            reg_free["B1"] = [endpe, enddve]
            reg_free["B2"] = [endpe, enddve]
            reg_free["B3"] = [endpe, enddve]
            reg_free["SCBC"] = [endpe, enddve]
            if b + 1 < n_blocks:
                wmix_free = wmix_free_next
            else:
                wmix_free = endpe

        def fin(gp):
            for i in range(4):
                gp.wait_ge(pr.h[f"qs{i}"], pr.c[f"qs{i}"])
        gp_sec(fin)

    return nc



def _tile_pm(w, mc, kc):
    K, M = w.shape
    assert K == kc * P and M == mc * P
    return np.ascontiguousarray(w.reshape(kc, P, mc, P).transpose(1, 2, 0, 3))


def _cols(v):
    n = v.shape[0] // P
    return np.ascontiguousarray(v.reshape(n, P).T)


def _x_layout(x):
    return np.ascontiguousarray(
        x.reshape(NB, NSUB, NTT, P, D).transpose(0, 1, 3, 2, 4))


def _y_unlayout(y):
    return np.asarray(y).transpose(0, 1, 3, 2, 4).reshape(NSEQ, TSEQ, D)


def make_inputs(L0_attn_x, L0_attn_norm1_w, L0_attn_norm1_b, L0_attn_gate_w,
                L0_attn_gate_b, L0_attn_state_w, L0_attn_state_b, L0_attn_out_w,
                L0_attn_out_b, L0_attn_norm2_w, L0_attn_norm2_b, L0_attn_mlp_w1,
                L0_attn_mlp_b1, L0_attn_mlp_w2, L0_attn_mlp_b2, L0_ffn_w_gate,
                L0_ffn_w_up, L0_ffn_w_down, n_cores=8):
    x = np.asarray(L0_attn_x, dtype=np.float32)
    common = {
        "w_gate": _tile_pm(np.asarray(L0_attn_gate_w, np.float32), NM_G, ND),
        "w_state": _tile_pm(np.asarray(L0_attn_state_w, np.float32), ND, ND),
        "w_outp": _tile_pm(np.asarray(L0_attn_out_w, np.float32), ND, ND),
        "w_mlp1": _tile_pm(np.asarray(L0_attn_mlp_w1, np.float32), NM_MLP, ND),
        "w_mlp2": _tile_pm(np.asarray(L0_attn_mlp_w2, np.float32), ND, NM_MLP),
        "w_fg": _tile_pm(np.ascontiguousarray(np.asarray(L0_ffn_w_gate, np.int32).T),
                         NM_FFN, ND),
        "w_fu": _tile_pm(np.ascontiguousarray(np.asarray(L0_ffn_w_up, np.int32).T),
                         NM_FFN, ND),
        "w_fd": _tile_pm(np.ascontiguousarray(np.asarray(L0_ffn_w_down, np.int32).T),
                         ND, NM_FFN),
        "c_gate_b": _cols(np.asarray(L0_attn_gate_b, np.float32)),
        "c_state_b": _cols(np.asarray(L0_attn_state_b, np.float32)),
        "c_out_b": _cols(np.asarray(L0_attn_out_b, np.float32)),
        "c_mlp_b1": _cols(np.asarray(L0_attn_mlp_b1, np.float32)),
        "c_mlp_b2": _cols(np.asarray(L0_attn_mlp_b2, np.float32)),
        "c_n1w": _cols(np.asarray(L0_attn_norm1_w, np.float32)),
        "c_n1b": _cols(np.asarray(L0_attn_norm1_b, np.float32)),
        "c_n2w": _cols(np.asarray(L0_attn_norm2_w, np.float32)),
        "c_n2b": _cols(np.asarray(L0_attn_norm2_b, np.float32)),
        "r_state_b": np.ascontiguousarray(
            np.asarray(L0_attn_state_b, np.float32).reshape(1, D)),
        "c_ident": np.eye(P, dtype=np.float32),
        "c_ones": np.ones((1, P), dtype=np.float32),
    }
    in_maps = []
    for c in range(n_cores):
        m = dict(common)
        m["x_in"] = _x_layout(x[c * NSEQ:(c + 1) * NSEQ])
        in_maps.append(m)
    return in_maps


def kernel(**inputs):
    in_maps = make_inputs(**inputs)
    nc = build_program()
    res = run_bass_kernel_spmd(nc, in_maps, core_ids=list(range(8)))
    return np.concatenate([_y_unlayout(r["y_out"]) for r in res.results], axis=0)


# version 17
# speedup vs baseline: 1.1575x; 1.0228x over previous
import contextlib
import numpy as np
import concourse.bass as bass
import concourse.mybir as mybir
from concourse.bass_utils import run_bass_kernel_spmd

F32 = mybir.dt.float32
BF16 = mybir.dt.bfloat16
I32 = mybir.dt.int32
AF = mybir.ActivationFunctionType
ALU = mybir.AluOpType
AXX = mybir.AxisListType.X

P = 128
D = 1024
ND = 8
NSEQ = 4
TSEQ = 2048
TB = 1024
NB = 8
NSUB = 2
SUB = 512
NTT = 4
NM_G = 16
NM_MLP = 32
NM_FFN = 24
EPS = 1e-5
QS = 1.0 / 127.0
INV_D = 1.0 / 1024.0


class Prog:
    def __init__(self):
        self.h = {}
        self.c = {}
        self.waited = {}

    def add_sem(self, name, handle):
        self.h[name] = handle
        self.c[name] = 0

    def inc(self, instr, sem, n=1):
        instr.then_inc(self.h[sem], n)
        self.c[sem] += n
        return (sem, self.c[sem])

    def mark(self, sem):
        return (sem, self.c[sem])

    def wait(self, eng, engname, m):
        if m is None:
            return
        sem, v = m
        if v <= 0:
            return
        key = (engname, sem)
        if self.waited.get(key, -1) >= v:
            return
        eng.wait_ge(self.h[sem], v)
        self.waited[key] = v

    def waits(self, eng, engname, marks):
        for m in marks:
            self.wait(eng, engname, m)

    def init_dmaq(self, qname, semnames):
        if not hasattr(self, "q"):
            self.q = {}
        self.q[qname] = {"sems": list(semnames), "rr": 0}

    def dma(self, eng, engname, qname, out, in_):
        q = self.q[qname]
        i = q["rr"]
        q["rr"] = (i + 1) % len(q["sems"])
        s = q["sems"][i]
        self.wait(eng, engname, (s, self.c[s]))
        return self.inc(eng.dma_start(out=out, in_=in_), s, 16)


def build_program(n_blocks=NB):
    nc = bass.Bass("TRN2", target_bir_lowering=False, debug=False)

    x_in = nc.dram_tensor("x_in", [NB, NSUB, P, NTT, D], F32, kind="ExternalInput").ap()
    y_out = nc.dram_tensor("y_out", [NB, NSUB, P, NTT, D], F32, kind="ExternalOutput").ap()

    w_gate = nc.dram_tensor("w_gate", [P, NM_G, ND, P], F32, kind="ExternalInput").ap()
    w_state = nc.dram_tensor("w_state", [P, ND, ND, P], F32, kind="ExternalInput").ap()
    w_outp = nc.dram_tensor("w_outp", [P, ND, ND, P], F32, kind="ExternalInput").ap()
    w_mlp1 = nc.dram_tensor("w_mlp1", [P, NM_MLP, ND, P], F32, kind="ExternalInput").ap()
    w_mlp2 = nc.dram_tensor("w_mlp2", [P, ND, NM_MLP, P], F32, kind="ExternalInput").ap()
    w_fg = nc.dram_tensor("w_fg", [P, NM_FFN, ND, P], I32, kind="ExternalInput").ap()
    w_fu = nc.dram_tensor("w_fu", [P, NM_FFN, ND, P], I32, kind="ExternalInput").ap()
    w_fd = nc.dram_tensor("w_fd", [P, ND, NM_FFN, P], I32, kind="ExternalInput").ap()

    c_gate_b = nc.dram_tensor("c_gate_b", [P, NM_G], F32, kind="ExternalInput").ap()
    c_state_b = nc.dram_tensor("c_state_b", [P, ND], F32, kind="ExternalInput").ap()
    c_out_b = nc.dram_tensor("c_out_b", [P, ND], F32, kind="ExternalInput").ap()
    c_mlp_b1 = nc.dram_tensor("c_mlp_b1", [P, NM_MLP], F32, kind="ExternalInput").ap()
    c_mlp_b2 = nc.dram_tensor("c_mlp_b2", [P, ND], F32, kind="ExternalInput").ap()
    c_n1w = nc.dram_tensor("c_n1w", [P, ND], F32, kind="ExternalInput").ap()
    c_n1b = nc.dram_tensor("c_n1b", [P, ND], F32, kind="ExternalInput").ap()
    c_n2w = nc.dram_tensor("c_n2w", [P, ND], F32, kind="ExternalInput").ap()
    c_n2b = nc.dram_tensor("c_n2b", [P, ND], F32, kind="ExternalInput").ap()
    r_state_b = nc.dram_tensor("r_state_b", [1, D], F32, kind="ExternalInput").ap()
    c_ident = nc.dram_tensor("c_ident", [P, P], F32, kind="ExternalInput").ap()
    c_ones = nc.dram_tensor("c_ones", [1, P], F32, kind="ExternalInput").ap()

    s_gate = nc.dram_tensor("s_gate", [P, NM_G, ND, P], BF16, kind="Internal").ap()
    s_state = nc.dram_tensor("s_state", [P, ND, ND, P], BF16, kind="Internal").ap()
    s_outp = nc.dram_tensor("s_outp", [P, ND, ND, P], BF16, kind="Internal").ap()
    s_mlp1 = nc.dram_tensor("s_mlp1", [P, NM_MLP, ND, P], BF16, kind="Internal").ap()
    s_mlp2 = nc.dram_tensor("s_mlp2", [P, ND, NM_MLP, P], BF16, kind="Internal").ap()
    s_fg = nc.dram_tensor("s_fg", [P, NM_FFN, ND, P], BF16, kind="Internal").ap()
    s_fu = nc.dram_tensor("s_fu", [P, NM_FFN, ND, P], BF16, kind="Internal").ap()
    s_fd = nc.dram_tensor("s_fd", [P, ND, NM_FFN, P], BF16, kind="Internal").ap()

    pr = Prog()

    with contextlib.ExitStack() as es:
        E = es.enter_context

        X1 = E(nc.sbuf_tensor("X1", [P, 8192], F32))
        FMB = E(nc.sbuf_tensor("FMB", [P, 8192], BF16))
        TTR = E(nc.sbuf_tensor("TTR", [P, 8192], F32))
        T1 = TTR[:, 0:4096]
        T2 = TTR[:, 4096:8192]
        M1R = E(nc.sbuf_tensor("M1R", [P, 16384], BF16))
        M1 = M1R
        W_MIX = M1R
        REG = E(nc.sbuf_tensor("REG", [P, 20480], F32))
        Z = REG[:, 0:4096].bitcast(BF16)
        B1 = REG[:, 4096:8192].bitcast(BF16)
        B2 = REG[:, 8192:12288].bitcast(BF16)
        B3 = REG[:, 12288:16384].bitcast(BF16)
        SCBC = REG[:, 16384:20480]
        W_BIG = REG[:, :].bitcast(BF16)
        A_PRO = REG[:, 0:1024]
        A_FS = REG[:, 9216:10240]
        A_FFNO = REG[:, 10240:14336]
        A_RING = TTR[:, 0:4096].bitcast(BF16)

        def small(name, shape, dtype):
            return E(nc.sbuf_tensor(name, shape, dtype))

        IDF = small("IDF", [P, P], F32)
        ONES = small("ONES", [1, P], F32)
        CB_G = small("CB_G", [P, NM_G], F32)
        CB_S = small("CB_S", [P, ND], F32)
        CB_O = small("CB_O", [P, ND], F32)
        CB_M1 = small("CB_M1", [P, NM_MLP], F32)
        CB_M2 = small("CB_M2", [P, ND], F32)
        CN1W = small("CN1W", [P, ND], F32)
        CN1B = small("CN1B", [P, ND], F32)
        CN2W = small("CN2W", [P, ND], F32)
        CN2B = small("CN2B", [P, ND], F32)
        WMS = small("WMS", [P, ND], BF16)
        HCAR = small("HCAR", [P, ND], BF16)
        BN6 = small("BN6", [P, 96], F32)
        BN2 = small("BN2", [P, 16], F32)
        MV = small("MV", [P, 32], F32)
        ZS = small("ZS", [P, 16], F32)
        RS = small("RS", [P, 8], F32)
        AC = small("AC", [P, 16], F32)
        WSA = small("WSA", [P, 8], F32)
        BSB = small("BSB", [1, 4], F32)
        MAGIC = small("MAGIC", [P, 4], I32)
        NEWT = small("NEWT", [P, 8], F32)
        BN6A = small("BN6A", [P, 96], F32)
        BN2A = small("BN2A", [P, 16], F32)
        MVA = small("MVA", [P, 32], F32)
        R0 = small("R0", [1, 1024], F32)
        R1 = small("R1", [1, 1024], F32)
        R2 = small("R2", [1, 1024], F32)

        A_BC = SCBC[:, 0:1024]
        D_BC = SCBC[:, 2048:3072]
        BCH = [SCBC[:, 1024:2048], SCBC[:, 3072:4096]]

        psum = [E(nc.psum_tensor(f"pb{i}", [P, 512], F32)) for i in range(8)]
        bank_fill = [None] * 8
        bank_free = [None] * 8

        for s_ in (["pe", "act", "dve"]
                   + [f"ql{i}" for i in range(8)] + [f"qs{i}" for i in range(4)]):
            pr.add_sem(s_, E(nc.semaphore(s_)))
        pr.init_dmaq("ql", [f"ql{i}" for i in range(8)])
        pr.init_dmaq("qs", [f"qs{i}" for i in range(4)])
        block = E(nc.Block())

        def sync_sec(f):
            @block.sync
            def _(e):
                f(e)

        def gp_sec(f):
            @block.gpsimd
            def _(e):
                f(e)

        def pe_sec(f):
            @block.tensor
            def _(e):
                f(e)

        def act_sec(f):
            @block.scalar
            def _(e):
                f(e)

        def dve_sec(f):
            @block.vector
            def _(e):
                f(e)

        EV = {"act": act_sec, "dve": dve_sec}
        CM = {}

        def prologue():
            def f(sy):
                for name, dst, src in [
                    ("ident", IDF[:], c_ident[:, :]), ("ones", ONES[:], c_ones[:, :]),
                    ("gb", CB_G[:], c_gate_b[:, :]), ("sb", CB_S[:], c_state_b[:, :]),
                    ("ob", CB_O[:], c_out_b[:, :]), ("m1b", CB_M1[:], c_mlp_b1[:, :]),
                    ("m2b", CB_M2[:], c_mlp_b2[:, :]),
                    ("n1w", CN1W[:], c_n1w[:, :]), ("n1b", CN1B[:], c_n1b[:, :]),
                    ("n2w", CN2W[:], c_n2w[:, :]), ("n2b", CN2B[:], c_n2b[:, :]),
                    ("sbr", R2[0:1, :], r_state_b[:, :]),
                ]:
                    CM[name] = pr.dma(sy, "sync", "ql", dst, src)
            sync_sec(f)

            def g(gp):
                prev = None
                for name, dst, src in [
                    ("gate", s_gate, w_gate), ("state", s_state, w_state),
                    ("outp", s_outp, w_outp), ("mlp1", s_mlp1, w_mlp1),
                    ("mlp2", s_mlp2, w_mlp2), ("fg", s_fg, w_fg),
                    ("fu", s_fu, w_fu), ("fd", s_fd, w_fd),
                ]:
                    if prev is not None:
                        pr.wait(gp, "pool", prev)
                    CM["cast_" + name] = pr.dma(
                        gp, "pool", "qs",
                        dst.rearrange("p m k f -> p (m k f)"),
                        src.rearrange("p m k f -> p (m k f)"))
                    prev = CM["cast_" + name]
            gp_sec(g)

            SW = REG[:, 8192:16384]
            SWR = REG[:, 16384:16448]

            def fl(sy):
                CM["wsm"] = pr.dma(sy, "sync", "ql", SW,
                                   w_state.rearrange("p m k f -> p (m k f)"))
            sync_sec(fl)

            def vl(dv):
                pr.wait(dv, "dve", CM["wsm"])
                rm = pr.inc(dv.tensor_reduce(
                    SWR[:, 0:64],
                    SW.rearrange("p (mk f) -> p mk f", f=P),
                    AXX, ALU.add), "dve", 1)
                pr.wait(dv, "dve", rm)
                mm = pr.inc(dv.tensor_copy(WSA[:], SWR[:, 0:ND]), "dve", 1)
                for m in range(1, ND):
                    pr.wait(dv, "dve", mm)
                    mm = pr.inc(dv.tensor_add(WSA[:], WSA[:],
                                              SWR[:, m * ND:(m + 1) * ND]),
                                "dve", 1)
                pr.wait(dv, "dve", mm)
                CM["wsum"] = mm
            dve_sec(vl)

            def vf(dv):
                mgm = pr.inc(dv.memset(MAGIC[:], 0x5f3759df), "dve", 1)
                pr.wait(dv, "dve", mgm)
                CM["wms"] = pr.inc(
                    dv.tensor_scalar(WMS[:], WSA[:], INV_D, None, ALU.mult), "dve", 1)
                pr.wait(dv, "dve", CM["sbr"])
                mm = pr.inc(dv.tensor_reduce(BSB[0:1, 0:1], R2[0:1, :], AXX, ALU.add),
                            "dve", 1)
                pr.wait(dv, "dve", mm)
                CM["bsb"] = pr.inc(
                    dv.tensor_scalar(BSB[0:1, 1:2], BSB[0:1, 0:1], INV_D, None, ALU.mult),
                    "dve", 1)
            dve_sec(vf)

        prologue()

        reg_free = {
            "T1": [], "T2": [], "Z": [], "B1": [],
            "B2": [CM["wsum"]], "B3": [CM["wsum"]],
            "FMB": [], "M1": [], "SCBC": [CM["wms"]], "X1": [],
        }
        CONST_MARKS = [CM[n] for n in (
            "ident", "ones", "gb", "sb", "ob", "m1b", "m2b",
            "n1w", "n1b", "n2w", "n2b", "sbr")]

        def rsqrt_newton(dv, dst, src):
            dsti = dst.bitcast(I32)
            srci = src.bitcast(I32)
            n = dst.shape[1]
            sh = NEWT[:, 0:n].bitcast(I32)
            tq = NEWT[:, 4:4 + n]
            m0 = pr.inc(dv.tensor_scalar(sh, srci, 1, None,
                                         ALU.logical_shift_right), "dve", 1)
            pr.wait(dv, "dve", m0)
            m0 = pr.inc(dv.tensor_tensor(dsti, MAGIC[:, 0:n], sh, ALU.subtract),
                        "dve", 1)
            pr.wait(dv, "dve", m0)
            for _ in range(3):
                m0 = pr.inc(dv.tensor_mul(tq, dst, dst), "dve", 1)
                pr.wait(dv, "dve", m0)
                m0 = pr.inc(dv.tensor_mul(tq, tq, src), "dve", 1)
                pr.wait(dv, "dve", m0)
                m0 = pr.inc(dv.tensor_scalar(tq, tq, -0.5, 1.5, ALU.mult, ALU.add),
                            "dve", 1)
                pr.wait(dv, "dve", m0)
                m0 = pr.inc(dv.tensor_mul(dst, dst, tq), "dve", 1)
                pr.wait(dv, "dve", m0)
            return m0

        def mm_stage(wsb, w_tile, n_m, n_k, rhs_fn, eng, dst_fn, w_ready,
                     func=AF.Identity, bias_col=None, scale=1.0, pre_waits=(),
                     extra_pe_waits=(), banks=(0, 1, 2, 3), subs=(0, 1),
                     group_waits=None, m_order=None):
            sem = eng
            evac_base = pr.c[sem]
            nbk = len(banks)
            idx = [(s, m) for s in subs
                   for m in (m_order if m_order is not None else range(n_m))]
            fills = []

            def fpe(t):
                if not callable(w_ready):
                    pr.wait(t, "pe", w_ready)
                for mk in extra_pe_waits:
                    pr.wait(t, "pe", mk)
                for i, (s, m) in enumerate(idx):
                    if callable(w_ready):
                        pr.wait(t, "pe", w_ready(m))
                    if group_waits is not None:
                        pr.waits(t, "pe", group_waits(s, m))
                    bank = banks[i % nbk]
                    if i >= nbk:
                        pr.wait(t, "pe", (sem, evac_base + i - nbk + 1))
                    else:
                        pr.wait(t, "pe", bank_free[bank])
                    ins = None
                    for k in range(n_k):
                        ins = t.matmul(
                            psum[bank][:, 0:512],
                            wsb[:, w_tile(m, k) * P:(w_tile(m, k) + 1) * P],
                            rhs_fn(s, k), start=(k == 0), stop=(k == n_k - 1))
                    fills.append(pr.inc(ins, "pe", 1))
            pe_sec(fpe)

            marks = []

            def fev(e):
                pr.waits(e, eng, CONST_MARKS)
                for mk in pre_waits:
                    pr.wait(e, eng, mk)
                for i, (s, m) in enumerate(idx):
                    bank = banks[i % nbk]
                    pr.wait(e, eng, fills[i])
                    if eng == "act":
                        ins = e.activation(dst_fn(s, m), psum[bank][:, 0:512], func,
                                           bias=(bias_col(m) if bias_col else 0.0),
                                           scale=scale)
                    else:
                        if bias_col is not None:
                            ins = e.tensor_scalar(dst_fn(s, m), psum[bank][:, 0:512],
                                                  bias_col(m), None, ALU.add)
                        elif scale != 1.0:
                            ins = e.tensor_scalar(dst_fn(s, m), psum[bank][:, 0:512],
                                                  scale, None, ALU.mult)
                        else:
                            ins = e.tensor_copy(dst_fn(s, m), psum[bank][:, 0:512])
                    mk = pr.inc(ins, eng, 1)
                    bank_free[bank] = mk
                    marks.append(mk)
            EV[eng](fev)
            assert pr.c[sem] == evac_base + len(idx)
            return marks

        def tm_to_fm_pe(src, src_ready, eng, banks=(4, 5)):
            sem = eng
            base = pr.c[sem]
            fills = []

            def fpe(t):
                for mk in src_ready:
                    pr.wait(t, "pe", mk)
                pr.wait(t, "pe", CM["ident"])
                for c in range(ND):
                    bank = banks[c % 2]
                    if c >= 2:
                        pr.wait(t, "pe", (sem, base + c - 1))
                    else:
                        pr.wait(t, "pe", bank_free[bank])
                    ins = None
                    for tau in range(NTT):
                        ins = t.transpose(
                            psum[bank][:, tau * P:(tau + 1) * P],
                            src[:, tau * 1024 + c * P: tau * 1024 + (c + 1) * P],
                            IDF[:])
                    fills.append(pr.inc(ins, "pe", 1))
            pe_sec(fpe)
            return fills

        def fm_to_tm_pe(src, src_ready, eng, incs_per_group, banks=(4, 5)):
            sem = eng
            base = pr.c[sem]
            fills = []

            def fpe(t):
                for mk in src_ready:
                    pr.wait(t, "pe", mk)
                pr.wait(t, "pe", CM["ident"])
                for i in range(NTT * 2):
                    tau, cg = divmod(i, 2)
                    bank = banks[cg]
                    if i >= 2:
                        pr.wait(t, "pe", (sem, base + (i - 1) * incs_per_group))
                    else:
                        pr.wait(t, "pe", bank_free[bank])
                    ins = None
                    for j in range(4):
                        c = cg * 4 + j
                        ins = t.transpose(
                            psum[bank][:, j * P:(j + 1) * P],
                            src[:, c * 512 + tau * P: c * 512 + (tau + 1) * P],
                            IDF[:])
                    fills.append(pr.inc(ins, "pe", 1))
            pe_sec(fpe)
            return fills

        x1_store_marks = [None] * NSUB
        wmix_free = None
        scan_h_mark = None

        def load_w(name, dst2, src2, cast_mark, prior_use):
            def f(sy):
                pr.wait(sy, "sync", cast_mark)
                pr.wait(sy, "sync", prior_use)
                CM[name] = pr.dma(sy, "sync", "ql", dst2, src2)
            sync_sec(f)

        def emit_A_chain(b, s, defer_x1gen):
            def fx(sy):
                if s == 0:
                    for mk in reg_free["T1"]:
                        pr.wait(sy, "sync", mk)
                else:
                    pr.wait(sy, "sync", CM[f"x1g{b}_0"])
                    pr.wait(sy, "sync", CM[f"zx{b}_0"])
                CM[f"x{b}_{s}"] = pr.dma(
                    sy, "sync", "ql",
                    T1[:, :], x_in[b, s].rearrange("p tt d -> p (tt d)"))
            sync_sec(fx)

            def fv(dv):
                pr.wait(dv, "dve", CM[f"x{b}_{s}"])
                ins = None
                for tau in range(NTT):
                    g = s * NTT + tau
                    dv.bn_stats(BN6A[:, g * 12:g * 12 + 6],
                                T1[:, tau * 1024:tau * 1024 + 512])
                    ins = dv.bn_stats(BN6A[:, g * 12 + 6:g * 12 + 12],
                                      T1[:, tau * 1024 + 512:(tau + 1) * 1024])
                m0 = pr.inc(ins, "dve", 1)
                pr.wait(dv, "dve", m0)
                for tau in range(NTT):
                    g = s * NTT + tau
                    ins = dv.bn_aggr(BN2A[:, g * 2:g * 2 + 2],
                                     BN6A[:, g * 12:(g + 1) * 12])
                m1_ = pr.inc(ins, "dve", 1)
                pr.wait(dv, "dve", m1_)
                m2_ = pr.inc(dv.tensor_scalar(MVA[:, 24 + s * 4:24 + s * 4 + 4],
                                              BN2A[:, s * 8 + 1:s * 8 + 8:2],
                                              EPS, None, ALU.add), "dve", 1)
                pr.wait(dv, "dve", m2_)
                rsqrt_newton(dv, MVA[:, s * 4:s * 4 + 4],
                             MVA[:, 24 + s * 4:24 + s * 4 + 4])
                m3_ = pr.inc(dv.scalar_tensor_tensor(
                    MVA[:, 8 + s * 4:8 + s * 4 + 4], BN2A[:, s * 8:s * 8 + 8:2],
                    -1.0, MVA[:, s * 4:s * 4 + 4], ALU.mult, ALU.mult), "dve", 1)
                CM[f"ln1c{b}_{s}"] = pr.inc(
                    dv.tensor_scalar(MVA[:, 16 + s * 4:16 + s * 4 + 4],
                                     MVA[:, s * 4:s * 4 + 4], 1.0, None, ALU.add),
                    "dve", 1)
                pr.wait(dv, "dve", CM[f"ln1c{b}_{s}"])
                m4_ = pr.inc(dv.tensor_mul(RS[:, 0:4], MVA[:, s * 4:s * 4 + 4],
                                           MVA[:, s * 4:s * 4 + 4]), "dve", 1)
                pr.wait(dv, "dve", m4_)
                m5_ = pr.inc(dv.tensor_mul(RS[:, 0:4], RS[:, 0:4],
                                           BN2A[:, s * 8 + 1:s * 8 + 8:2]), "dve", 1)
                pr.wait(dv, "dve", m5_)
                m6_ = pr.inc(dv.tensor_scalar(RS[:, 0:4], RS[:, 0:4], EPS, None,
                                              ALU.add), "dve", 1)
                pr.wait(dv, "dve", m6_)
                rsqrt_newton(dv, RS[:, 4:8], RS[:, 0:4])
                pr.inc(dv.tensor_mul(AC[:, s * 4:s * 4 + 4],
                                     MVA[:, s * 4:s * 4 + 4], RS[:, 4:8]),
                       "dve", 1)
                CM[f"ln1f{b}_{s}"] = pr.inc(
                    dv.tensor_mul(AC[:, 8 + s * 4:8 + s * 4 + 4],
                                  MVA[:, 8 + s * 4:8 + s * 4 + 4], RS[:, 4:8]),
                    "dve", 1)
                pr.wait(dv, "dve", CM[f"ln1f{b}_{s}"])
            dve_sec(fv)

            def fzt(ac):
                pr.wait(ac, "act", CM[f"ln1f{b}_{s}"])
                for mk in reg_free["T2"]:
                    pr.wait(ac, "act", mk)
                ins = None
                for tau in range(NTT):
                    ins = ac.activation(
                        T2[:, tau * 1024:(tau + 1) * 1024],
                        T1[:, tau * 1024:(tau + 1) * 1024], AF.Identity,
                        scale=AC[:, s * 4 + tau:s * 4 + tau + 1],
                        bias=AC[:, 8 + s * 4 + tau:8 + s * 4 + tau + 1])
                CM[f"t1{b}_{s}"] = pr.inc(ins, "act", 1)
                for mk in reg_free["Z"]:
                    pr.wait(ac, "act", mk)
                ins = None
                for tau in range(NTT):
                    g = s * NTT + tau
                    ins = ac.activation(
                        Z[:, g * 1024:(g + 1) * 1024],
                        T1[:, tau * 1024:(tau + 1) * 1024], AF.Identity,
                        scale=MVA[:, s * 4 + tau:s * 4 + tau + 1],
                        bias=MVA[:, 8 + s * 4 + tau:8 + s * 4 + tau + 1])
                CM[f"zx{b}_{s}"] = pr.inc(ins, "act", 1)
            act_sec(fzt)
            reg_free["T2"] = [CM[f"t1{b}_{s}"]]
            if not defer_x1gen:
                emit_X1gen(b, s)

        def emit_X1gen(b, s):
            def fg(dv):
                if x1_store_marks[s] is not None:
                    pr.wait(dv, "dve", x1_store_marks[s])
                pr.wait(dv, "dve", CM[f"ln1c{b}_{s}"])
                ins = None
                for tau in range(NTT):
                    g = s * NTT + tau
                    nb = MVA[:, 8 + s * 4 + tau:8 + s * 4 + tau + 1]
                    rp1 = MVA[:, 16 + s * 4 + tau:16 + s * 4 + tau + 1]
                    ins = dv.tensor_scalar(X1[:, g * 1024:(g + 1) * 1024],
                                           T1[:, tau * 1024:(tau + 1) * 1024],
                                           rp1, nb, ALU.mult, ALU.add)
                CM[f"x1g{b}_{s}"] = pr.inc(ins, "dve", 1)
            dve_sec(fg)
            if s == 1:
                reg_free["T1"] = [CM[f"x1g{b}_{s}"], CM[f"zx{b}_{s}"]]

        def emit_A_tail_sub(b, s, xn_marks):
            fills = tm_to_fm_pe(T2, [CM[f"t1{b}_{s}"]], "act")

            def fev_xn(ac):
                pr.waits(ac, "act", CONST_MARKS)
                for mk in reg_free["FMB"]:
                    pr.wait(ac, "act", mk)
                for c in range(ND):
                    bank = (4, 5)[c % 2]
                    pr.wait(ac, "act", fills[c])
                    ins = ac.activation(
                        FMB[:, c * 1024 + s * 512: c * 1024 + (s + 1) * 512],
                        psum[bank][:, 0:512], AF.Identity,
                        scale=CN1W[:, c:c + 1], bias=CN1B[:, c:c + 1])
                    mk = pr.inc(ins, "act", 1)
                    bank_free[bank] = mk
                    xn_marks.append(mk)
            act_sec(fev_xn)
            reg_free["T2"] = [pr.mark("pe")]

        def emit_A_full(b, xn_bysub):
            emit_A_chain(b, 0, defer_x1gen=False)
            emit_A_tail_sub(b, 0, xn_bysub[0])
            emit_A_chain(b, 1, defer_x1gen=False)
            emit_A_tail_sub(b, 1, xn_bysub[1])

        pending_xn = [[], []]
        emit_A_full(0, pending_xn)

        for b in range(n_blocks):
            xn_by_sub = pending_xn
            xn_marks = xn_by_sub[0] + xn_by_sub[1]
            pending_xn = [[], []]

            load_w(f"gate{b}", W_MIX[:, 0:16384],
                   s_gate.rearrange("p m k f -> p (m k f)"),
                   CM["cast_gate"], wmix_free)

            def gate_dst(s, m):
                if m < 8:
                    return B1[:, m * 1024 + s * 512: m * 1024 + (s + 1) * 512]
                return B2[:, (m - 8) * 1024 + s * 512: (m - 8) * 1024 + (s + 1) * 512]

            g1s_marks = mm_stage(
                W_MIX, lambda m, k: m * ND + k, NM_G, ND,
                lambda s, k: FMB[:, k * 1024 + s * 512: k * 1024 + (s + 1) * 512],
                "act", gate_dst, CM[f"gate{b}"], func=AF.Silu,
                pre_waits=list(reg_free["B1"]) + list(reg_free["B2"]),
                group_waits=lambda s, m: xn_by_sub[s], m_order=list(range(8)))
            gate_lo_done = pr.mark("pe")

            load_w(f"state{b}", W_MIX[:, 0:8192],
                   s_state.rearrange("p m k f -> p (m k f)"),
                   CM["cast_state"], gate_lo_done)

            g2s_marks = mm_stage(
                W_MIX, lambda m, k: m * ND + k, NM_G, ND,
                lambda s, k: FMB[:, k * 1024 + s * 512: k * 1024 + (s + 1) * 512],
                "act", gate_dst, CM[f"gate{b}"], func=AF.Silu,
                group_waits=lambda s, m: xn_by_sub[s],
                m_order=list(range(8, 16)))
            gate_last_mm = pr.mark("pe")

            def fv1(dv):
                pr.waits(dv, "dve", g1s_marks)
                pr.waits(dv, "dve", xn_marks)
                for mk in reg_free["B3"]:
                    pr.wait(dv, "dve", mk)
                CM[f"v1{b}"] = pr.inc(dv.tensor_mul(B3, FMB[:], B1), "dve", 1)
            dve_sec(fv1)

            dpre_base = pr.c["dve"]

            def fdpre_pe(t):
                pr.wait(t, "pe", CM["wms"])
                pr.waits(t, "pe", xn_marks)
                for s in range(NSUB):
                    if s > 0:
                        pr.wait(t, "pe", ("dve", dpre_base + s))
                    else:
                        pr.wait(t, "pe", bank_free[6])
                    ins = None
                    for k in range(ND):
                        ins = t.matmul(
                            psum[6][0:1, 0:512], WMS[:, k:k + 1],
                            FMB[:, k * 1024 + s * 512: k * 1024 + (s + 1) * 512],
                            start=(k == 0), stop=(k == ND - 1))
                    CM[f"dpref{b}_{s}"] = pr.inc(ins, "pe", 1)
            pe_sec(fdpre_pe)

            def fdpre_dv(dv):
                pr.wait(dv, "dve", CM["bsb"])
                for s in range(NSUB):
                    pr.wait(dv, "dve", CM[f"dpref{b}_{s}"])
                    mk = pr.inc(dv.tensor_scalar(R0[0:1, s * 512:(s + 1) * 512],
                                                 psum[6][0:1, 0:512], BSB[0:1, 1:2],
                                                 None, ALU.add), "dve", 1)
                    bank_free[6] = mk
                CM[f"dpre{b}"] = pr.mark("dve")
            dve_sec(fdpre_dv)
            assert pr.c["dve"] == dpre_base + NSUB

            def fda(ac):
                pr.wait(ac, "act", CM[f"dpre{b}"])
                m0 = pr.inc(ac.activation(R1[0:1, :], R0[0:1, :], AF.Abs), "act", 1)
                pr.wait(ac, "act", m0)
                m1_ = pr.inc(ac.activation(R2[0:1, :], R1[0:1, :], AF.Exp, scale=-1.0),
                             "act", 1)
                pr.wait(ac, "act", m1_)
                CM[f"sp{b}"] = pr.inc(
                    ac.activation(R1[0:1, :], R2[0:1, :], AF.Ln, bias=1.0), "act", 1)
                pr.wait(ac, "act", ("dve", pr.c["dve"] + 2))
                CM[f"arow{b}"] = pr.inc(
                    ac.activation(R1[0:1, :], R0[0:1, :], AF.Exp, scale=-1.0), "act", 1)
            act_sec(fda)

            def fdv2(dv):
                pr.wait(dv, "dve", CM[f"sp{b}"])
                m0 = pr.inc(dv.tensor_scalar_max(R2[0:1, :], R0[0:1, :], 0.0), "dve", 1)
                pr.wait(dv, "dve", m0)
                CM[f"delta{b}"] = pr.inc(
                    dv.tensor_add(R0[0:1, :], R2[0:1, :], R1[0:1, :]), "dve", 1)
            dve_sec(fdv2)

            bc_base = pr.c["dve"]

            def fbc_pe(t):
                pr.wait(t, "pe", CM["ones"])
                i = 0
                for bank, rm, row in [(6, f"delta{b}", R0), (7, f"arow{b}", R1)]:
                    pr.wait(t, "pe", CM[rm])
                    for s in range(NSUB):
                        if i % 2 == 1:
                            pr.wait(t, "pe", ("dve", bc_base + i))
                        else:
                            pr.wait(t, "pe", bank_free[bank])
                        ins = t.matmul(psum[bank][:, 0:512], ONES[:],
                                       row[0:1, s * 512:(s + 1) * 512],
                                       start=True, stop=True)
                        CM[f"bcf{b}_{i}"] = pr.inc(ins, "pe", 1)
                        i += 1
            pe_sec(fbc_pe)

            bc_marks = []

            def fbc_dv(dv):
                for mk in reg_free["SCBC"]:
                    pr.wait(dv, "dve", mk)
                i = 0
                for bank, dstv in [(6, D_BC), (7, A_BC)]:
                    for s in range(NSUB):
                        pr.wait(dv, "dve", CM[f"bcf{b}_{i}"])
                        mk = pr.inc(dv.tensor_copy(dstv[:, s * 512:(s + 1) * 512],
                                                   psum[bank][:, 0:512]), "dve", 1)
                        bank_free[bank] = mk
                        bc_marks.append(mk)
                        i += 1
            dve_sec(fbc_dv)
            assert pr.c["dve"] == bc_base + 4
            reg_free["SCBC"] = []

            u_marks = mm_stage(
                W_MIX, lambda m, k: m * ND + k, ND, ND,
                lambda s, k: B3[:, k * 1024 + s * 512: k * 1024 + (s + 1) * 512],
                "dve",
                lambda s, m: B1[:, m * 1024 + s * 512: m * 1024 + (s + 1) * 512],
                CM[f"state{b}"], bias_col=lambda m: CB_S[:, m:m + 1],
                pre_waits=[CM[f"v1{b}"]],
                extra_pe_waits=[CM[f"v1{b}"]] + g1s_marks)

            g2s_by_sub = [g2s_marks[0:8], g2s_marks[8:16]]
            bm_marks = mm_stage(
                W_MIX, lambda m, k: m * ND + k, ND, ND,
                lambda s, k: B2[:, k * 1024 + s * 512: k * 1024 + (s + 1) * 512],
                "act",
                lambda s, m: B3[:, m * 1024 + s * 512: m * 1024 + (s + 1) * 512],
                CM[f"state{b}"], bias_col=lambda m: CB_S[:, m:m + 1],
                pre_waits=[CM[f"v1{b}"]],
                extra_pe_waits=[CM[f"v1{b}"]],
                group_waits=lambda s, m: g2s_by_sub[s])
            state_last_mm = pr.mark("pe")

            load_w(f"outp{b}", W_MIX[:, 0:8192],
                   s_outp.rearrange("p m k f -> p (m k f)"),
                   CM["cast_outp"], state_last_mm)

            load_w(f"mlp1a{b}", W_BIG[:, 16384:24576],
                   s_mlp1[:, 8:16].rearrange("p m k f -> p (m k f)"),
                   CM["cast_mlp1"], state_last_mm)

            scan_marks = []

            def fscan(dv):
                pr.waits(dv, "dve", bc_marks)
                pr.waits(dv, "dve", u_marks)
                udm = pr.inc(dv.tensor_tensor(
                    B1[:, :].rearrange("p (c f) -> p c f", c=ND),
                    B1[:, :].rearrange("p (c f) -> p c f", c=ND),
                    D_BC[:, None, :].broadcast_to([P, ND, 1024]), ALU.mult),
                    "dve", 1)
                pr.wait(dv, "dve", udm)
                for c in range(ND):
                    pr.wait(dv, "dve", bm_marks[8 + c])
                    m0 = pr.inc(dv.tensor_tensor(BCH[c % 2],
                                                 B1[:, c * 1024:(c + 1) * 1024],
                                                 B3[:, c * 1024:(c + 1) * 1024],
                                                 ALU.mult), "dve", 1)
                    pr.wait(dv, "dve", m0)
                    init = 0.0 if b % 2 == 0 else HCAR[:, c:c + 1]
                    ins = dv.tensor_tensor_scan(B1[:, c * 1024:(c + 1) * 1024],
                                                A_BC, BCH[c % 2], init,
                                                ALU.mult, ALU.add)
                    scan_marks.append(pr.inc(ins, "dve", 1))
                if b % 2 == 0:
                    pr.waits(dv, "dve", scan_marks)
                    ins = None
                    for c in range(ND):
                        ins = dv.tensor_copy(
                            HCAR[:, c:c + 1],
                            B1[:, (c + 1) * 1024 - 1:(c + 1) * 1024])
                    CM[f"hcar{b}"] = pr.inc(ins, "dve", 1)
            dve_sec(fscan)
            if b % 2 == 0:
                scan_h_mark = CM[f"hcar{b}"]

            scan_done_dve = pr.mark("dve")
            load_w(f"mlp1b{b}", W_BIG[:, 24576:40960],
                   s_mlp1[:, 16:32].rearrange("p m k f -> p (m k f)"),
                   CM["cast_mlp1"], scan_done_dve)

            x2n_by_sub = [[], []]
            xb_last = None
            t1_free = list(reg_free["T1"])

            def emit_out_mm(s, pre):
                return mm_stage(
                    W_MIX, lambda m, k: m * ND + k, ND, ND,
                    lambda s_, k: B1[:, k * 1024 + s * 512: k * 1024 + (s + 1) * 512],
                    "act", lambda s_, m: T1[:, m * 512:(m + 1) * 512],
                    CM[f"outp{b}"], bias_col=lambda m: CB_O[:, m:m + 1],
                    pre_waits=pre, extra_pe_waits=list(scan_marks), subs=(s,))

            def emit_fcons(s, fills):
                xb_marks = []

                def fcons(dv, s=s, fills=fills, xb_marks=xb_marks):
                    pr.wait(dv, "dve", CM[f"x1g{b}_{s}"])
                    pr.wait(dv, "dve", CM[f"zx{b}_{s}"])
                    for i in range(NTT * 2):
                        tau, cg = divmod(i, 2)
                        bank = (4, 5)[cg]
                        g = s * NTT + tau
                        pr.wait(dv, "dve", fills[i])
                        sl = slice(g * 1024 + cg * 512, g * 1024 + (cg + 1) * 512)
                        sl2 = slice(tau * 1024 + cg * 512, tau * 1024 + (cg + 1) * 512)
                        pr.inc(dv.tensor_add(X1[:, sl], X1[:, sl],
                                             psum[bank][:, 0:512]), "dve", 1)
                        ins = dv.tensor_add(T2[:, sl2], Z[:, sl], psum[bank][:, 0:512])
                        mk = pr.inc(ins, "dve", 1)
                        bank_free[bank] = mk
                        xb_marks.append(mk)
                dve_sec(fcons)
                return xb_marks

            def emit_ln2_t2(s, xb_marks):
                def fv5(dv, s=s, xb_marks=xb_marks):
                    pr.waits(dv, "dve", xb_marks)
                    ins = None
                    for tau in range(NTT):
                        g = s * NTT + tau
                        dv.bn_stats(BN6[:, g * 12:g * 12 + 6],
                                    T2[:, tau * 1024:tau * 1024 + 512])
                        ins = dv.bn_stats(BN6[:, g * 12 + 6:g * 12 + 12],
                                          T2[:, tau * 1024 + 512:(tau + 1) * 1024])
                    m0 = pr.inc(ins, "dve", 1)
                    pr.wait(dv, "dve", m0)
                    for tau in range(NTT):
                        g = s * NTT + tau
                        ins = dv.bn_aggr(BN2[:, g * 2:g * 2 + 2],
                                         BN6[:, g * 12:(g + 1) * 12])
                    m1_ = pr.inc(ins, "dve", 1)
                    pr.wait(dv, "dve", m1_)
                    m2_ = pr.inc(dv.tensor_scalar(MV[:, 24 + s * 4:24 + s * 4 + 4],
                                                  BN2[:, s * 8 + 1:s * 8 + 8:2],
                                                  EPS, None, ALU.add), "dve", 1)
                    pr.wait(dv, "dve", m2_)
                    rsqrt_newton(dv, MV[:, s * 4:s * 4 + 4],
                                 MV[:, 24 + s * 4:24 + s * 4 + 4])
                    ins = None
                    for tau in range(NTT):
                        g = s * NTT + tau
                        ins = dv.tensor_scalar(
                            T2[:, tau * 1024:(tau + 1) * 1024],
                            T2[:, tau * 1024:(tau + 1) * 1024],
                            BN2[:, g * 2:g * 2 + 1],
                            MV[:, s * 4 + tau:s * 4 + tau + 1],
                            ALU.subtract, ALU.mult)
                    CM[f"t2{b}_{s}"] = pr.inc(ins, "dve", 1)
                dve_sec(fv5)

            def emit_x2n(s, fills2):
                def fev_x2n(ac, s=s, fills2=fills2):
                    pr.waits(ac, "act", CONST_MARKS)
                    for c in range(ND):
                        bank = (4, 5)[c % 2]
                        pr.wait(ac, "act", fills2[c])
                        ins = ac.activation(
                            FMB[:, c * 1024 + s * 512: c * 1024 + (s + 1) * 512],
                            psum[bank][:, 0:512], AF.Identity,
                            scale=CN2W[:, c:c + 1], bias=CN2B[:, c:c + 1])
                        mk = pr.inc(ins, "act", 1)
                        bank_free[bank] = mk
                        x2n_by_sub[s].append(mk)
                act_sec(fev_x2n)

            ssm0 = emit_out_mm(0, t1_free)
            fills_s0 = fm_to_tm_pe(T1, ssm0, "dve", 2)
            ssmT0_done = pr.mark("pe")
            xb0 = emit_fcons(0, fills_s0)
            emit_ln2_t2(0, xb0)
            xb_last = xb0[-1]

            ssm1 = emit_out_mm(1, [ssmT0_done])
            outmm1_pe = pr.mark("pe")
            load_w(f"mlp1c{b}", W_BIG[:, 8192:16384],
                   s_mlp1[:, 0:8].rearrange("p m k f -> p (m k f)"),
                   CM["cast_mlp1"], outmm1_pe)

            def mlp1_ready(m, b=b):
                if m < 8:
                    return CM[f"mlp1c{b}"]
                if m < 16:
                    return CM[f"mlp1a{b}"]
                return CM[f"mlp1b{b}"]
            mlp1_m_order = list(range(8, 32)) + list(range(0, 8))
            fills_t20 = tm_to_fm_pe(T2, [CM[f"t2{b}_0"]], "act")
            emit_x2n(0, fills_t20)
            t2T0_done = pr.mark("pe")
            fills_s1 = fm_to_tm_pe(T1, ssm1, "dve", 2)
            ssmT1_done = pr.mark("pe")
            xb1 = emit_fcons(1, fills_s1)
            emit_ln2_t2(1, xb1)
            xb_last = xb1[-1]

            m1_0 = mm_stage(
                W_BIG, lambda m, k: 64 + m * ND + k, NM_MLP, ND,
                lambda s_, k: FMB[:, k * 1024 + 0 * 512: k * 1024 + 1 * 512],
                "act", lambda s_, m: M1[:, m * 512:(m + 1) * 512],
                mlp1_ready, func=AF.Gelu,
                bias_col=lambda m: CB_M1[:, m:m + 1],
                pre_waits=list(reg_free["M1"]),
                extra_pe_waits=list(x2n_by_sub[0]) + list(reg_free["M1"]),
                subs=(0,), m_order=mlp1_m_order)

            fills_t21 = tm_to_fm_pe(T2, [CM[f"t2{b}_1"]], "act")
            emit_x2n(1, fills_t21)
            t2T1_done = pr.mark("pe")

            prev_ring = [None, None]
            ringslots = [0, 4096]

            def emit_mlp2(s, ring_free_mark, m1_marks):
                pe_base2 = pr.c["pe"]
                dve_base2 = pr.c["dve"]
                ring_marks = {}

                def fr(sy, s=s):
                    pr.wait(sy, "sync", CM["cast_mlp2"])
                    pr.wait(sy, "sync", ring_free_mark)
                    for m in range(ND):
                        slot = ringslots[m % 2]
                        if m < 2:
                            pr.wait(sy, "sync", prev_ring[m % 2])
                        else:
                            pr.wait(sy, "sync", ("pe", pe_base2 + m - 1))
                        ring_marks[m] = pr.dma(
                            sy, "sync", "ql", A_RING[:, slot:slot + 4096],
                            s_mlp2[:, m].rearrange("p k f -> p (k f)"))
                sync_sec(fr)

                def fpe2(t, s=s):
                    pr.waits(t, "pe", m1_marks)
                    for m in range(ND):
                        slot = ringslots[m % 2]
                        bank = m % 4
                        pr.wait(t, "pe", ring_marks[m])
                        if m >= 4:
                            pr.wait(t, "pe", ("dve", dve_base2 + m - 3))
                        else:
                            pr.wait(t, "pe", bank_free[bank])
                        ins = None
                        for k in range(NM_MLP):
                            ins = t.matmul(psum[bank][:, 0:512],
                                           A_RING[:, slot + k * P:slot + (k + 1) * P],
                                           M1[:, k * 512:(k + 1) * 512],
                                           start=(k == 0), stop=(k == NM_MLP - 1))
                        CM[f"m2f{b}_{s}_{m}"] = pr.inc(ins, "pe", 1)
                        prev_ring[m % 2] = CM[f"m2f{b}_{s}_{m}"]
                pe_sec(fpe2)

                mlpo_marks = []

                def fev2(dv, s=s):
                    for mk in reg_free["T2"]:
                        pr.wait(dv, "dve", mk)
                    for m in range(ND):
                        bank = m % 4
                        pr.wait(dv, "dve", CM[f"m2f{b}_{s}_{m}"])
                        ins = dv.tensor_scalar(T2[:, m * 512:(m + 1) * 512],
                                               psum[bank][:, 0:512],
                                               CB_M2[:, m:m + 1], None, ALU.add)
                        mk = pr.inc(ins, "dve", 1)
                        bank_free[bank] = mk
                        mlpo_marks.append(mk)
                dve_sec(fev2)
                assert pr.c["dve"] == dve_base2 + ND

                fills3 = fm_to_tm_pe(T2, mlpo_marks, "dve", 1)

                def fcons2(dv, s=s, fills3=fills3):
                    for i in range(NTT * 2):
                        tau, cg = divmod(i, 2)
                        bank = (4, 5)[cg]
                        g = s * NTT + tau
                        pr.wait(dv, "dve", fills3[i])
                        sl = slice(g * 1024 + cg * 512, g * 1024 + (cg + 1) * 512)
                        ins = dv.tensor_add(X1[:, sl], X1[:, sl], psum[bank][:, 0:512])
                        mk = pr.inc(ins, "dve", 1)
                        bank_free[bank] = mk
                dve_sec(fcons2)
                reg_free["T2"] = [pr.mark("pe")]
                CM[f"xmid{b}_{s}"] = pr.mark("dve")

            reg_free["T2"] = [t2T1_done]
            emit_mlp2(0, ssmT1_done, m1_0)
            mlp2_0_pe = pr.mark("pe")

            m1_1 = mm_stage(
                W_BIG, lambda m, k: 64 + m * ND + k, NM_MLP, ND,
                lambda s_, k: FMB[:, k * 1024 + 1 * 512: k * 1024 + 2 * 512],
                "act", lambda s_, m: M1[:, m * 512:(m + 1) * 512],
                mlp1_ready, func=AF.Gelu,
                bias_col=lambda m: CB_M1[:, m:m + 1],
                pre_waits=[mlp2_0_pe],
                extra_pe_waits=list(x2n_by_sub[1]) + [mlp2_0_pe],
                subs=(1,), m_order=mlp1_m_order)

            def emit_ln3(s, affno_free):
                def fv7(dv, s=s):
                    pr.wait(dv, "dve", CM[f"xmid{b}_{s}"])
                    ins = None
                    for tau in range(NTT):
                        g = s * NTT + tau
                        dv.bn_stats(BN6[:, g * 12:g * 12 + 6],
                                    X1[:, g * 1024:g * 1024 + 512])
                        ins = dv.bn_stats(BN6[:, g * 12 + 6:g * 12 + 12],
                                          X1[:, g * 1024 + 512:(g + 1) * 1024])
                    m0 = pr.inc(ins, "dve", 1)
                    pr.wait(dv, "dve", m0)
                    for tau in range(NTT):
                        g = s * NTT + tau
                        ins = dv.bn_aggr(BN2[:, g * 2:g * 2 + 2],
                                         BN6[:, g * 12:(g + 1) * 12])
                    m1_ = pr.inc(ins, "dve", 1)
                    pr.wait(dv, "dve", m1_)
                    m2_ = pr.inc(dv.tensor_scalar(MV[:, 24 + s * 4:24 + s * 4 + 4],
                                                  BN2[:, s * 8 + 1:s * 8 + 8:2],
                                                  EPS, None, ALU.add), "dve", 1)
                    pr.wait(dv, "dve", m2_)
                    rsqrt_newton(dv, MV[:, s * 4:s * 4 + 4],
                                 MV[:, 24 + s * 4:24 + s * 4 + 4])
                    CM[f"ln3c{b}_{s}"] = pr.inc(dv.scalar_tensor_tensor(
                        MV[:, 8 + s * 4:8 + s * 4 + 4], BN2[:, s * 8:s * 8 + 8:2],
                        -1.0, MV[:, s * 4:s * 4 + 4], ALU.mult, ALU.mult), "dve", 1)
                dve_sec(fv7)

                def fa8(ac, s=s, affno_free=affno_free):
                    pr.wait(ac, "act", CM[f"ln3c{b}_{s}"])
                    for mk in affno_free:
                        pr.wait(ac, "act", mk)
                    ins = None
                    for tau in range(NTT):
                        g = s * NTT + tau
                        ins = ac.activation(
                            A_FFNO[:, tau * 1024:(tau + 1) * 1024],
                            X1[:, g * 1024:(g + 1) * 1024], AF.Identity,
                            scale=MV[:, s * 4 + tau:s * 4 + tau + 1],
                            bias=MV[:, 8 + s * 4 + tau:8 + s * 4 + tau + 1])
                    CM[f"t3{b}_{s}"] = pr.inc(ins, "act", 1)
                act_sec(fa8)

            m11_pe = pr.mark("pe")
            emit_ln3(0, [m11_pe])

            emit_mlp2(1, ssmT1_done, m1_1)
            reg_free["M1"] = [pr.mark("pe")]
            reg_free["Z"] = [xb_last]
            mlp_last_pe = pr.mark("pe")
            reg_free["T1"] = [CM[f"m2f{b}_1_{ND - 1}"]]

            prev_g = [None, None]
            prev_u = [None, None]
            prev_d = [None, None]
            rgs = [8192, 9216]
            rus = [10240, 11264]
            rds = [12288, 15360]
            ringslots_d = rds

            def emit_zn(s):
                fills4 = tm_to_fm_pe(A_FFNO, [CM[f"t3{b}_{s}"]], "dve")
                zn_marks = []

                def fev_zn(dv, s=s, fills4=fills4):
                    for c in range(ND):
                        bank = (4, 5)[c % 2]
                        pr.wait(dv, "dve", fills4[c])
                        ins = dv.tensor_copy(
                            FMB[:, c * 1024 + s * 512: c * 1024 + (s + 1) * 512],
                            psum[bank][:, 0:512])
                        mk = pr.inc(ins, "dve", 1)
                        bank_free[bank] = mk
                        zn_marks.append(mk)
                dve_sec(fev_zn)
                return zn_marks, pr.mark("pe")

            def emit_gu(s, zn_marks, m1_free):
                pe_base3 = pr.c["pe"]
                act_base3 = pr.c["act"]
                dve_base3 = pr.c["dve"]

                def frg(sy, s=s):
                    pr.wait(sy, "sync", CM["cast_fg"])
                    pr.wait(sy, "sync", CM["cast_fu"])
                    pr.wait(sy, "sync", mlp_last_pe)
                    for m in range(NM_FFN):
                        sg_slot = rgs[m % 2]
                        su_slot = rus[m % 2]
                        if m < 2:
                            pr.wait(sy, "sync", prev_g[m % 2])
                        else:
                            pr.wait(sy, "sync", ("pe", pe_base3 + (m - 2) * 2 + 1))
                        CM[f"rg{b}_{s}_{m}"] = pr.dma(
                            sy, "sync", "ql", W_BIG[:, sg_slot:sg_slot + 1024],
                            s_fg[:, m].rearrange("p k f -> p (k f)"))
                        if m < 2:
                            pr.wait(sy, "sync", prev_u[m % 2])
                        else:
                            pr.wait(sy, "sync", ("pe", pe_base3 + (m - 2) * 2 + 2))
                        CM[f"ru{b}_{s}_{m}"] = pr.dma(
                            sy, "sync", "ql", W_BIG[:, su_slot:su_slot + 1024],
                            s_fu[:, m].rearrange("p k f -> p (k f)"))
                sync_sec(frg)

                def fpe3(t, s=s, act_base3=act_base3, dve_base3=dve_base3):
                    pr.waits(t, "pe", m1_free)
                    for m in range(NM_FFN):
                        sg_slot = rgs[m % 2]
                        su_slot = rus[m % 2]
                        bg = m % 2
                        bu = 2 + m % 2
                        pr.wait(t, "pe", CM[f"rg{b}_{s}_{m}"])
                        if m < 2:
                            pr.wait(t, "pe", bank_free[bg])
                            pr.waits(t, "pe", zn_marks)
                        else:
                            pr.wait(t, "pe", ("act", act_base3 + m - 1))
                        ins = None
                        for k in range(ND):
                            ins = t.matmul(
                                psum[bg][:, 0:512],
                                W_BIG[:, sg_slot + k * P:sg_slot + (k + 1) * P],
                                FMB[:, k * 1024 + s * 512: k * 1024 + (s + 1) * 512],
                                start=(k == 0), stop=(k == ND - 1))
                        CM[f"fgf{b}_{s}_{m}"] = pr.inc(ins, "pe", 1)
                        pr.wait(t, "pe", CM[f"ru{b}_{s}_{m}"])
                        if m < 2:
                            pr.wait(t, "pe", bank_free[bu])
                        else:
                            pr.wait(t, "pe", ("dve", dve_base3 + (m - 2) * 2 + 1))
                        for k in range(ND):
                            ins = t.matmul(
                                psum[bu][:, 0:512],
                                W_BIG[:, su_slot + k * P:su_slot + (k + 1) * P],
                                FMB[:, k * 1024 + s * 512: k * 1024 + (s + 1) * 512],
                                start=(k == 0), stop=(k == ND - 1))
                        CM[f"fuf{b}_{s}_{m}"] = pr.inc(ins, "pe", 1)
                pe_sec(fpe3)

                def fact3(ac, s=s, dve_base3=dve_base3):
                    for m in range(NM_FFN):
                        bg = m % 2
                        pr.wait(ac, "act", CM[f"fgf{b}_{s}_{m}"])
                        if m > 0:
                            pr.wait(ac, "act", ("dve", dve_base3 + (m - 1) * 2 + 2))
                        elif CM.get(f"hlast{b}_{1 - s}") is not None and s == 1:
                            pr.wait(ac, "act", CM[f"hlast{b}_0"])
                        ins = ac.activation(A_FS[:, 0:512], psum[bg][:, 0:512],
                                            AF.Silu, scale=QS)
                        CM[f"fsg{b}_{s}_{m}"] = pr.inc(ins, "act", 1)
                act_sec(fact3)
                assert pr.c["act"] == act_base3 + NM_FFN

                h_marks = []

                def fdve3(dv, s=s):
                    for m in range(NM_FFN):
                        bu = 2 + m % 2
                        pr.wait(dv, "dve", CM[f"fuf{b}_{s}_{m}"])
                        if s == 1 and m == 0:
                            pr.wait(dv, "dve", CM[f"fdf{b}_0_{ND - 1}"])
                        if h_marks:
                            pr.wait(dv, "dve", h_marks[-1])
                        m0 = pr.inc(dv.tensor_scalar(A_FS[:, 512:1024],
                                                     psum[bu][:, 0:512], QS, None,
                                                     ALU.mult), "dve", 1)
                        pr.wait(dv, "dve", m0)
                        pr.wait(dv, "dve", CM[f"fsg{b}_{s}_{m}"])
                        ins = dv.tensor_mul(M1[:, m * 512:(m + 1) * 512],
                                            A_FS[:, 0:512], A_FS[:, 512:1024])
                        mk = pr.inc(ins, "dve", 1)
                        h_marks.append(mk)
                dve_sec(fdve3)
                assert pr.c["dve"] == dve_base3 + 2 * NM_FFN
                CM[f"hlast{b}_{s}"] = h_marks[-1]
                prev_g[0] = CM[f"fgf{b}_{s}_{NM_FFN - 2}"]
                prev_g[1] = CM[f"fgf{b}_{s}_{NM_FFN - 1}"]
                prev_u[0] = CM[f"fuf{b}_{s}_{NM_FFN - 2}"]
                prev_u[1] = CM[f"fuf{b}_{s}_{NM_FFN - 1}"]
                return h_marks

            def emit_down(s, h_marks, extra_pe=()):
                pe_base4 = pr.c["pe"]

                def frd(sy, s=s):
                    pr.wait(sy, "sync", CM["cast_fd"])
                    pr.wait(sy, "sync", mlp_last_pe)
                    for m in range(ND):
                        slot = rds[m % 2]
                        if m < 2:
                            pr.wait(sy, "sync", prev_d[m % 2])
                        else:
                            pr.wait(sy, "sync", ("pe", pe_base4 + m - 1))
                        CM[f"rd{b}_{s}_{m}"] = pr.dma(
                            sy, "sync", "ql", W_BIG[:, slot:slot + 3072],
                            s_fd[:, m].rearrange("p k f -> p (k f)"))
                sync_sec(frd)

                act_base4 = pr.c["act"]

                def fpe4(t, s=s, act_base4=act_base4):
                    pr.waits(t, "pe", h_marks)
                    for mk in extra_pe:
                        pr.wait(t, "pe", mk)
                    for m in range(ND):
                        slot = rds[m % 2]
                        bank = m % 4
                        pr.wait(t, "pe", CM[f"rd{b}_{s}_{m}"])
                        if m >= 4:
                            pr.wait(t, "pe", ("act", act_base4 + m - 3))
                        else:
                            pr.wait(t, "pe", bank_free[bank])
                        ins = None
                        for k in range(NM_FFN):
                            ins = t.matmul(psum[bank][:, 0:512],
                                           W_BIG[:, slot + k * P:slot + (k + 1) * P],
                                           M1[:, k * 512:(k + 1) * 512],
                                           start=(k == 0), stop=(k == NM_FFN - 1))
                        CM[f"fdf{b}_{s}_{m}"] = pr.inc(ins, "pe", 1)
                pe_sec(fpe4)
                prev_d[0] = CM[f"fdf{b}_{s}_{ND - 2}"]
                prev_d[1] = CM[f"fdf{b}_{s}_{ND - 1}"]
                return act_base4

            def emit_fev4(s, act_base4):
                ffno_marks = []

                def fev4(ac, s=s):
                    for m in range(ND):
                        bank = m % 4
                        pr.wait(ac, "act", CM[f"fdf{b}_{s}_{m}"])
                        ins = ac.activation(A_FFNO[:, m * 512:(m + 1) * 512],
                                            psum[bank][:, 0:512], AF.Identity,
                                            scale=QS)
                        mk = pr.inc(ins, "act", 1)
                        bank_free[bank] = mk
                        ffno_marks.append(mk)
                act_sec(fev4)
                assert pr.c["act"] == act_base4 + ND
                return ffno_marks

            def emit_ffn_tail(s, ffno_marks):
                fills5 = fm_to_tm_pe(A_FFNO, ffno_marks, "dve", 1)
                xo_marks = []

                def fcons3(dv, s=s, fills5=fills5, xo_marks=xo_marks):
                    for i in range(NTT * 2):
                        tau, cg = divmod(i, 2)
                        bank = (4, 5)[cg]
                        g = s * NTT + tau
                        pr.wait(dv, "dve", fills5[i])
                        sl = slice(g * 1024 + cg * 512, g * 1024 + (cg + 1) * 512)
                        ins = dv.tensor_add(X1[:, sl], X1[:, sl], psum[bank][:, 0:512])
                        mk = pr.inc(ins, "dve", 1)
                        bank_free[bank] = mk
                        xo_marks.append(mk)
                dve_sec(fcons3)

                def fst(gp, s=s, xo_marks=xo_marks):
                    pr.waits(gp, "pool", xo_marks)
                    x1_store_marks[s] = pr.dma(
                        gp, "pool", "qs",
                        y_out[b, s].rearrange("p tt d -> p (tt d)"),
                        X1[:, s * 4096:(s + 1) * 4096])
                gp_sec(fst)

            zn0, zn0_pe = emit_zn(0)
            h0 = emit_gu(0, zn0, reg_free["M1"])
            emit_ln3(1, [zn0_pe])
            zn1, _ = emit_zn(1)
            d0base = emit_down(0, h0)
            ffno0 = emit_fev4(0, d0base)
            h1 = emit_gu(1, zn1, [])
            ffn_gu_done = pr.mark("pe")
            emit_ffn_tail(0, ffno0)
            if b + 1 < n_blocks:
                emit_A_chain(b + 1, 0, defer_x1gen=True)
                emit_X1gen(b + 1, 0)
            d1base = emit_down(1, h1)
            ffno1 = emit_fev4(1, d1base)
            if b + 1 < n_blocks:
                wmix_free_next = pr.mark("pe")
                reg_free["FMB"] = [ffn_gu_done]
                emit_A_tail_sub(b + 1, 0, pending_xn[0])
            emit_ffn_tail(1, ffno1)
            reg_free["M1"] = [pr.mark("pe")]
            if b + 1 < n_blocks:
                emit_A_chain(b + 1, 1, defer_x1gen=True)
                emit_X1gen(b + 1, 1)
                emit_A_tail_sub(b + 1, 1, pending_xn[1])

            endpe = pr.mark("pe")
            enddve = pr.mark("dve")
            reg_free["B1"] = [endpe, enddve]
            reg_free["B2"] = [endpe, enddve]
            reg_free["B3"] = [endpe, enddve]
            reg_free["SCBC"] = [endpe, enddve]
            if b + 1 < n_blocks:
                wmix_free = wmix_free_next
            else:
                wmix_free = endpe

        def fin(gp):
            for i in range(4):
                gp.wait_ge(pr.h[f"qs{i}"], pr.c[f"qs{i}"])
        gp_sec(fin)

    return nc



def _tile_pm(w, mc, kc):
    K, M = w.shape
    assert K == kc * P and M == mc * P
    return np.ascontiguousarray(w.reshape(kc, P, mc, P).transpose(1, 2, 0, 3))


def _cols(v):
    n = v.shape[0] // P
    return np.ascontiguousarray(v.reshape(n, P).T)


def _x_layout(x):
    return np.ascontiguousarray(
        x.reshape(NB, NSUB, NTT, P, D).transpose(0, 1, 3, 2, 4))


def _y_unlayout(y):
    return np.asarray(y).transpose(0, 1, 3, 2, 4).reshape(NSEQ, TSEQ, D)


def make_inputs(L0_attn_x, L0_attn_norm1_w, L0_attn_norm1_b, L0_attn_gate_w,
                L0_attn_gate_b, L0_attn_state_w, L0_attn_state_b, L0_attn_out_w,
                L0_attn_out_b, L0_attn_norm2_w, L0_attn_norm2_b, L0_attn_mlp_w1,
                L0_attn_mlp_b1, L0_attn_mlp_w2, L0_attn_mlp_b2, L0_ffn_w_gate,
                L0_ffn_w_up, L0_ffn_w_down, n_cores=8):
    x = np.asarray(L0_attn_x, dtype=np.float32)
    common = {
        "w_gate": _tile_pm(np.asarray(L0_attn_gate_w, np.float32), NM_G, ND),
        "w_state": _tile_pm(np.asarray(L0_attn_state_w, np.float32), ND, ND),
        "w_outp": _tile_pm(np.asarray(L0_attn_out_w, np.float32), ND, ND),
        "w_mlp1": _tile_pm(np.asarray(L0_attn_mlp_w1, np.float32), NM_MLP, ND),
        "w_mlp2": _tile_pm(np.asarray(L0_attn_mlp_w2, np.float32), ND, NM_MLP),
        "w_fg": _tile_pm(np.ascontiguousarray(np.asarray(L0_ffn_w_gate, np.int32).T),
                         NM_FFN, ND),
        "w_fu": _tile_pm(np.ascontiguousarray(np.asarray(L0_ffn_w_up, np.int32).T),
                         NM_FFN, ND),
        "w_fd": _tile_pm(np.ascontiguousarray(np.asarray(L0_ffn_w_down, np.int32).T),
                         ND, NM_FFN),
        "c_gate_b": _cols(np.asarray(L0_attn_gate_b, np.float32)),
        "c_state_b": _cols(np.asarray(L0_attn_state_b, np.float32)),
        "c_out_b": _cols(np.asarray(L0_attn_out_b, np.float32)),
        "c_mlp_b1": _cols(np.asarray(L0_attn_mlp_b1, np.float32)),
        "c_mlp_b2": _cols(np.asarray(L0_attn_mlp_b2, np.float32)),
        "c_n1w": _cols(np.asarray(L0_attn_norm1_w, np.float32)),
        "c_n1b": _cols(np.asarray(L0_attn_norm1_b, np.float32)),
        "c_n2w": _cols(np.asarray(L0_attn_norm2_w, np.float32)),
        "c_n2b": _cols(np.asarray(L0_attn_norm2_b, np.float32)),
        "r_state_b": np.ascontiguousarray(
            np.asarray(L0_attn_state_b, np.float32).reshape(1, D)),
        "c_ident": np.eye(P, dtype=np.float32),
        "c_ones": np.ones((1, P), dtype=np.float32),
    }
    in_maps = []
    for c in range(n_cores):
        m = dict(common)
        m["x_in"] = _x_layout(x[c * NSEQ:(c + 1) * NSEQ])
        in_maps.append(m)
    return in_maps


def kernel(**inputs):
    in_maps = make_inputs(**inputs)
    nc = build_program()
    res = run_bass_kernel_spmd(nc, in_maps, core_ids=list(range(8)))
    return np.concatenate([_y_unlayout(r["y_out"]) for r in res.results], axis=0)


# version 18
# speedup vs baseline: 1.1790x; 1.0186x over previous
import contextlib
import numpy as np
import concourse.bass as bass
import concourse.mybir as mybir
from concourse.bass_utils import run_bass_kernel_spmd

F32 = mybir.dt.float32
BF16 = mybir.dt.bfloat16
I32 = mybir.dt.int32
AF = mybir.ActivationFunctionType
ALU = mybir.AluOpType
AXX = mybir.AxisListType.X

P = 128
D = 1024
ND = 8
NSEQ = 4
TSEQ = 2048
TB = 1024
NB = 8
NSUB = 2
SUB = 512
NTT = 4
NM_G = 16
NM_MLP = 32
NM_FFN = 24
EPS = 1e-5
QS = 1.0 / 127.0
INV_D = 1.0 / 1024.0


class Prog:
    def __init__(self):
        self.h = {}
        self.c = {}
        self.waited = {}

    def add_sem(self, name, handle):
        self.h[name] = handle
        self.c[name] = 0

    def inc(self, instr, sem, n=1):
        instr.then_inc(self.h[sem], n)
        self.c[sem] += n
        return (sem, self.c[sem])

    def mark(self, sem):
        return (sem, self.c[sem])

    def wait(self, eng, engname, m):
        if m is None:
            return
        sem, v = m
        if v <= 0:
            return
        key = (engname, sem)
        if self.waited.get(key, -1) >= v:
            return
        eng.wait_ge(self.h[sem], v)
        self.waited[key] = v

    def waits(self, eng, engname, marks):
        for m in marks:
            self.wait(eng, engname, m)

    def init_dmaq(self, qname, semnames):
        if not hasattr(self, "q"):
            self.q = {}
        self.q[qname] = {"sems": list(semnames), "rr": 0}

    def dma(self, eng, engname, qname, out, in_):
        q = self.q[qname]
        i = q["rr"]
        q["rr"] = (i + 1) % len(q["sems"])
        s = q["sems"][i]
        self.wait(eng, engname, (s, self.c[s]))
        return self.inc(eng.dma_start(out=out, in_=in_), s, 16)


def build_program(n_blocks=NB):
    nc = bass.Bass("TRN2", target_bir_lowering=False, debug=False)

    x_in = nc.dram_tensor("x_in", [NB, NSUB, P, NTT, D], F32, kind="ExternalInput").ap()
    y_out = nc.dram_tensor("y_out", [NB, NSUB, P, NTT, D], F32, kind="ExternalOutput").ap()

    w_gate = nc.dram_tensor("w_gate", [P, NM_G, ND, P], F32, kind="ExternalInput").ap()
    w_state = nc.dram_tensor("w_state", [P, ND, ND, P], F32, kind="ExternalInput").ap()
    w_outp = nc.dram_tensor("w_outp", [P, ND, ND, P], F32, kind="ExternalInput").ap()
    w_mlp1 = nc.dram_tensor("w_mlp1", [P, NM_MLP, ND, P], F32, kind="ExternalInput").ap()
    w_mlp2 = nc.dram_tensor("w_mlp2", [P, ND, NM_MLP, P], F32, kind="ExternalInput").ap()
    w_fg = nc.dram_tensor("w_fg", [P, NM_FFN, ND, P], I32, kind="ExternalInput").ap()
    w_fu = nc.dram_tensor("w_fu", [P, NM_FFN, ND, P], I32, kind="ExternalInput").ap()
    w_fd = nc.dram_tensor("w_fd", [P, ND, NM_FFN, P], I32, kind="ExternalInput").ap()

    c_gate_b = nc.dram_tensor("c_gate_b", [P, NM_G], F32, kind="ExternalInput").ap()
    c_state_b = nc.dram_tensor("c_state_b", [P, ND], F32, kind="ExternalInput").ap()
    c_out_b = nc.dram_tensor("c_out_b", [P, ND], F32, kind="ExternalInput").ap()
    c_mlp_b1 = nc.dram_tensor("c_mlp_b1", [P, NM_MLP], F32, kind="ExternalInput").ap()
    c_mlp_b2 = nc.dram_tensor("c_mlp_b2", [P, ND], F32, kind="ExternalInput").ap()
    c_n1w = nc.dram_tensor("c_n1w", [P, ND], F32, kind="ExternalInput").ap()
    c_n1b = nc.dram_tensor("c_n1b", [P, ND], F32, kind="ExternalInput").ap()
    c_n2w = nc.dram_tensor("c_n2w", [P, ND], F32, kind="ExternalInput").ap()
    c_n2b = nc.dram_tensor("c_n2b", [P, ND], F32, kind="ExternalInput").ap()
    r_state_b = nc.dram_tensor("r_state_b", [1, D], F32, kind="ExternalInput").ap()
    c_ident = nc.dram_tensor("c_ident", [P, P], F32, kind="ExternalInput").ap()
    c_ones = nc.dram_tensor("c_ones", [1, P], F32, kind="ExternalInput").ap()

    s_gate = nc.dram_tensor("s_gate", [P, NM_G, ND, P], BF16, kind="Internal").ap()
    s_state = nc.dram_tensor("s_state", [P, ND, ND, P], BF16, kind="Internal").ap()
    s_outp = nc.dram_tensor("s_outp", [P, ND, ND, P], BF16, kind="Internal").ap()
    s_mlp1 = nc.dram_tensor("s_mlp1", [P, NM_MLP, ND, P], BF16, kind="Internal").ap()
    s_mlp2 = nc.dram_tensor("s_mlp2", [P, ND, NM_MLP, P], BF16, kind="Internal").ap()
    s_fg = nc.dram_tensor("s_fg", [P, NM_FFN, ND, P], BF16, kind="Internal").ap()
    s_fu = nc.dram_tensor("s_fu", [P, NM_FFN, ND, P], BF16, kind="Internal").ap()
    s_fd = nc.dram_tensor("s_fd", [P, ND, NM_FFN, P], BF16, kind="Internal").ap()

    pr = Prog()

    with contextlib.ExitStack() as es:
        E = es.enter_context

        X1 = E(nc.sbuf_tensor("X1", [P, 8192], F32))
        FMB = E(nc.sbuf_tensor("FMB", [P, 8192], BF16))
        TTR = E(nc.sbuf_tensor("TTR", [P, 8192], F32))
        T1 = TTR[:, 0:4096]
        T2 = TTR[:, 4096:8192]
        M1R = E(nc.sbuf_tensor("M1R", [P, 16384], BF16))
        M1 = M1R
        W_MIX = M1R
        REG = E(nc.sbuf_tensor("REG", [P, 20480], F32))
        Z = REG[:, 0:4096].bitcast(BF16)
        B1 = REG[:, 4096:8192].bitcast(BF16)
        B2 = REG[:, 8192:12288].bitcast(BF16)
        B3 = REG[:, 12288:16384].bitcast(BF16)
        SCBC = REG[:, 16384:20480]
        W_BIG = REG[:, :].bitcast(BF16)
        A_PRO = REG[:, 0:1024]
        A_FS = REG[:, 9216:10240]
        A_FFNO = REG[:, 10240:14336]
        A_RING = TTR[:, 0:4096].bitcast(BF16)

        def small(name, shape, dtype):
            return E(nc.sbuf_tensor(name, shape, dtype))

        IDF = small("IDF", [P, P], F32)
        ONES = small("ONES", [1, P], F32)
        CB_G = small("CB_G", [P, NM_G], F32)
        CB_S = small("CB_S", [P, ND], F32)
        CB_O = small("CB_O", [P, ND], F32)
        CB_M1 = small("CB_M1", [P, NM_MLP], F32)
        CB_M2 = small("CB_M2", [P, ND], F32)
        CN1W = small("CN1W", [P, ND], F32)
        CN1B = small("CN1B", [P, ND], F32)
        CN2W = small("CN2W", [P, ND], F32)
        CN2B = small("CN2B", [P, ND], F32)
        WMS = small("WMS", [P, ND], BF16)
        HCAR = small("HCAR", [P, ND], BF16)
        BN6 = small("BN6", [P, 96], F32)
        BN2 = small("BN2", [P, 16], F32)
        MV = small("MV", [P, 32], F32)
        ZS = small("ZS", [P, 16], F32)
        RS = small("RS", [P, 8], F32)
        AC = small("AC", [P, 16], F32)
        WSA = small("WSA", [P, 8], F32)
        BSB = small("BSB", [1, 4], F32)
        MAGIC = small("MAGIC", [P, 4], I32)
        NEWT = small("NEWT", [P, 8], F32)
        BN6A = small("BN6A", [P, 96], F32)
        BN2A = small("BN2A", [P, 16], F32)
        MVA = small("MVA", [P, 32], F32)
        R0 = small("R0", [1, 1024], F32)
        R1 = small("R1", [1, 1024], F32)
        R2 = small("R2", [1, 1024], F32)

        XS1 = SCBC
        A_BC = SCBC[:, 0:1024]
        D_BC = SCBC[:, 2048:3072]
        BCH = [SCBC[:, 1024:2048], SCBC[:, 3072:4096]]

        psum = [E(nc.psum_tensor(f"pb{i}", [P, 512], F32)) for i in range(8)]
        bank_fill = [None] * 8
        bank_free = [None] * 8

        for s_ in (["pe", "act", "dve"]
                   + [f"ql{i}" for i in range(8)] + [f"qs{i}" for i in range(4)]):
            pr.add_sem(s_, E(nc.semaphore(s_)))
        pr.init_dmaq("ql", [f"ql{i}" for i in range(8)])
        pr.init_dmaq("qs", [f"qs{i}" for i in range(4)])
        block = E(nc.Block())

        def sync_sec(f):
            @block.sync
            def _(e):
                f(e)

        def gp_sec(f):
            @block.gpsimd
            def _(e):
                f(e)

        def pe_sec(f):
            @block.tensor
            def _(e):
                f(e)

        def act_sec(f):
            @block.scalar
            def _(e):
                f(e)

        def dve_sec(f):
            @block.vector
            def _(e):
                f(e)

        EV = {"act": act_sec, "dve": dve_sec}
        CM = {}

        def prologue():
            def f(sy):
                for name, dst, src in [
                    ("ident", IDF[:], c_ident[:, :]), ("ones", ONES[:], c_ones[:, :]),
                    ("gb", CB_G[:], c_gate_b[:, :]), ("sb", CB_S[:], c_state_b[:, :]),
                    ("ob", CB_O[:], c_out_b[:, :]), ("m1b", CB_M1[:], c_mlp_b1[:, :]),
                    ("m2b", CB_M2[:], c_mlp_b2[:, :]),
                    ("n1w", CN1W[:], c_n1w[:, :]), ("n1b", CN1B[:], c_n1b[:, :]),
                    ("n2w", CN2W[:], c_n2w[:, :]), ("n2b", CN2B[:], c_n2b[:, :]),
                    ("sbr", R2[0:1, :], r_state_b[:, :]),
                ]:
                    CM[name] = pr.dma(sy, "sync", "ql", dst, src)
            sync_sec(f)

            def g(gp):
                prev = None
                for name, dst, src in [
                    ("gate", s_gate, w_gate), ("state", s_state, w_state),
                    ("outp", s_outp, w_outp), ("mlp1", s_mlp1, w_mlp1),
                    ("mlp2", s_mlp2, w_mlp2), ("fg", s_fg, w_fg),
                    ("fu", s_fu, w_fu), ("fd", s_fd, w_fd),
                ]:
                    if prev is not None:
                        pr.wait(gp, "pool", prev)
                    CM["cast_" + name] = pr.dma(
                        gp, "pool", "qs",
                        dst.rearrange("p m k f -> p (m k f)"),
                        src.rearrange("p m k f -> p (m k f)"))
                    prev = CM["cast_" + name]
            gp_sec(g)

            SW = REG[:, 8192:16384]
            SWR = REG[:, 16384:16448]

            def fl(sy):
                CM["wsm"] = pr.dma(sy, "sync", "ql", SW,
                                   w_state.rearrange("p m k f -> p (m k f)"))
            sync_sec(fl)

            def vl(dv):
                pr.wait(dv, "dve", CM["wsm"])
                rm = pr.inc(dv.tensor_reduce(
                    SWR[:, 0:64],
                    SW.rearrange("p (mk f) -> p mk f", f=P),
                    AXX, ALU.add), "dve", 1)
                pr.wait(dv, "dve", rm)
                mm = pr.inc(dv.tensor_copy(WSA[:], SWR[:, 0:ND]), "dve", 1)
                for m in range(1, ND):
                    pr.wait(dv, "dve", mm)
                    mm = pr.inc(dv.tensor_add(WSA[:], WSA[:],
                                              SWR[:, m * ND:(m + 1) * ND]),
                                "dve", 1)
                pr.wait(dv, "dve", mm)
                CM["wsum"] = mm
            dve_sec(vl)

            def vf(dv):
                mgm = pr.inc(dv.memset(MAGIC[:], 0x5f3759df), "dve", 1)
                pr.wait(dv, "dve", mgm)
                CM["wms"] = pr.inc(
                    dv.tensor_scalar(WMS[:], WSA[:], INV_D, None, ALU.mult), "dve", 1)
                pr.wait(dv, "dve", CM["sbr"])
                mm = pr.inc(dv.tensor_reduce(BSB[0:1, 0:1], R2[0:1, :], AXX, ALU.add),
                            "dve", 1)
                pr.wait(dv, "dve", mm)
                CM["bsb"] = pr.inc(
                    dv.tensor_scalar(BSB[0:1, 1:2], BSB[0:1, 0:1], INV_D, None, ALU.mult),
                    "dve", 1)
            dve_sec(vf)

        prologue()

        reg_free = {
            "T1": [], "T2": [], "Z": [], "B1": [],
            "B2": [CM["wsum"]], "B3": [CM["wsum"]],
            "FMB": [], "M1": [], "SCBC": [CM["wms"]], "X1": [],
        }
        CONST_MARKS = [CM[n] for n in (
            "ident", "ones", "gb", "sb", "ob", "m1b", "m2b",
            "n1w", "n1b", "n2w", "n2b", "sbr")]

        def rsqrt_newton(dv, dst, src):
            dsti = dst.bitcast(I32)
            srci = src.bitcast(I32)
            n = dst.shape[1]
            sh = NEWT[:, 0:n].bitcast(I32)
            tq = NEWT[:, 4:4 + n]
            m0 = pr.inc(dv.tensor_scalar(sh, srci, 1, None,
                                         ALU.logical_shift_right), "dve", 1)
            pr.wait(dv, "dve", m0)
            m0 = pr.inc(dv.tensor_tensor(dsti, MAGIC[:, 0:n], sh, ALU.subtract),
                        "dve", 1)
            pr.wait(dv, "dve", m0)
            for _ in range(3):
                m0 = pr.inc(dv.tensor_mul(tq, dst, dst), "dve", 1)
                pr.wait(dv, "dve", m0)
                m0 = pr.inc(dv.tensor_mul(tq, tq, src), "dve", 1)
                pr.wait(dv, "dve", m0)
                m0 = pr.inc(dv.tensor_scalar(tq, tq, -0.5, 1.5, ALU.mult, ALU.add),
                            "dve", 1)
                pr.wait(dv, "dve", m0)
                m0 = pr.inc(dv.tensor_mul(dst, dst, tq), "dve", 1)
                pr.wait(dv, "dve", m0)
            return m0

        def mm_stage(wsb, w_tile, n_m, n_k, rhs_fn, eng, dst_fn, w_ready,
                     func=AF.Identity, bias_col=None, scale=1.0, pre_waits=(),
                     extra_pe_waits=(), banks=(0, 1, 2, 3), subs=(0, 1),
                     group_waits=None, m_order=None):
            sem = eng
            evac_base = pr.c[sem]
            nbk = len(banks)
            idx = [(s, m) for s in subs
                   for m in (m_order if m_order is not None else range(n_m))]
            fills = []

            def fpe(t):
                if not callable(w_ready):
                    pr.wait(t, "pe", w_ready)
                for mk in extra_pe_waits:
                    pr.wait(t, "pe", mk)
                for i, (s, m) in enumerate(idx):
                    if callable(w_ready):
                        pr.wait(t, "pe", w_ready(m))
                    if group_waits is not None:
                        pr.waits(t, "pe", group_waits(s, m))
                    bank = banks[i % nbk]
                    if i >= nbk:
                        pr.wait(t, "pe", (sem, evac_base + i - nbk + 1))
                    else:
                        pr.wait(t, "pe", bank_free[bank])
                    ins = None
                    for k in range(n_k):
                        ins = t.matmul(
                            psum[bank][:, 0:512],
                            wsb[:, w_tile(m, k) * P:(w_tile(m, k) + 1) * P],
                            rhs_fn(s, k), start=(k == 0), stop=(k == n_k - 1))
                    fills.append(pr.inc(ins, "pe", 1))
            pe_sec(fpe)

            marks = []

            def fev(e):
                pr.waits(e, eng, CONST_MARKS)
                for mk in pre_waits:
                    pr.wait(e, eng, mk)
                for i, (s, m) in enumerate(idx):
                    bank = banks[i % nbk]
                    pr.wait(e, eng, fills[i])
                    if eng == "act":
                        ins = e.activation(dst_fn(s, m), psum[bank][:, 0:512], func,
                                           bias=(bias_col(m) if bias_col else 0.0),
                                           scale=scale)
                    else:
                        if bias_col is not None:
                            ins = e.tensor_scalar(dst_fn(s, m), psum[bank][:, 0:512],
                                                  bias_col(m), None, ALU.add)
                        elif scale != 1.0:
                            ins = e.tensor_scalar(dst_fn(s, m), psum[bank][:, 0:512],
                                                  scale, None, ALU.mult)
                        else:
                            ins = e.tensor_copy(dst_fn(s, m), psum[bank][:, 0:512])
                    mk = pr.inc(ins, eng, 1)
                    bank_free[bank] = mk
                    marks.append(mk)
            EV[eng](fev)
            assert pr.c[sem] == evac_base + len(idx)
            return marks

        def tm_to_fm_pe(src, src_ready, eng, banks=(4, 5)):
            sem = eng
            base = pr.c[sem]
            fills = []

            def fpe(t):
                for mk in src_ready:
                    pr.wait(t, "pe", mk)
                pr.wait(t, "pe", CM["ident"])
                for c in range(ND):
                    bank = banks[c % 2]
                    if c >= 2:
                        pr.wait(t, "pe", (sem, base + c - 1))
                    else:
                        pr.wait(t, "pe", bank_free[bank])
                    ins = None
                    for tau in range(NTT):
                        ins = t.transpose(
                            psum[bank][:, tau * P:(tau + 1) * P],
                            src[:, tau * 1024 + c * P: tau * 1024 + (c + 1) * P],
                            IDF[:])
                    fills.append(pr.inc(ins, "pe", 1))
            pe_sec(fpe)
            return fills

        def fm_to_tm_pe(src, src_ready, eng, incs_per_group, banks=(4, 5)):
            sem = eng
            base = pr.c[sem]
            fills = []

            def fpe(t):
                for mk in src_ready:
                    pr.wait(t, "pe", mk)
                pr.wait(t, "pe", CM["ident"])
                for i in range(NTT * 2):
                    tau, cg = divmod(i, 2)
                    bank = banks[cg]
                    if i >= 2:
                        pr.wait(t, "pe", (sem, base + (i - 1) * incs_per_group))
                    else:
                        pr.wait(t, "pe", bank_free[bank])
                    ins = None
                    for j in range(4):
                        c = cg * 4 + j
                        ins = t.transpose(
                            psum[bank][:, j * P:(j + 1) * P],
                            src[:, c * 512 + tau * P: c * 512 + (tau + 1) * P],
                            IDF[:])
                    fills.append(pr.inc(ins, "pe", 1))
            pe_sec(fpe)
            return fills

        x1_store_marks = [None] * NSUB
        wmix_free = None
        scan_h_mark = None

        def load_w(name, dst2, src2, cast_mark, prior_use):
            def f(sy):
                pr.wait(sy, "sync", cast_mark)
                pr.wait(sy, "sync", prior_use)
                CM[name] = pr.dma(sy, "sync", "ql", dst2, src2)
            sync_sec(f)

        def emit_A_stats(b, s, x1_gate=()):
            XB = T1 if s == 0 else XS1

            def fx(sy):
                if s == 0:
                    for mk in reg_free["T1"]:
                        pr.wait(sy, "sync", mk)
                else:
                    for mk in x1_gate:
                        pr.wait(sy, "sync", mk)
                CM[f"x{b}_{s}"] = pr.dma(
                    sy, "sync", "ql",
                    XB[:, :], x_in[b, s].rearrange("p tt d -> p (tt d)"))
            sync_sec(fx)

            def fv(dv):
                pr.wait(dv, "dve", CM[f"x{b}_{s}"])
                ins = None
                for tau in range(NTT):
                    g = s * NTT + tau
                    dv.bn_stats(BN6A[:, g * 12:g * 12 + 6],
                                XB[:, tau * 1024:tau * 1024 + 512])
                    ins = dv.bn_stats(BN6A[:, g * 12 + 6:g * 12 + 12],
                                      XB[:, tau * 1024 + 512:(tau + 1) * 1024])
                m0 = pr.inc(ins, "dve", 1)
                pr.wait(dv, "dve", m0)
                for tau in range(NTT):
                    g = s * NTT + tau
                    ins = dv.bn_aggr(BN2A[:, g * 2:g * 2 + 2],
                                     BN6A[:, g * 12:(g + 1) * 12])
                m1_ = pr.inc(ins, "dve", 1)
                pr.wait(dv, "dve", m1_)
                m2_ = pr.inc(dv.tensor_scalar(MVA[:, 24 + s * 4:24 + s * 4 + 4],
                                              BN2A[:, s * 8 + 1:s * 8 + 8:2],
                                              EPS, None, ALU.add), "dve", 1)
                pr.wait(dv, "dve", m2_)
                rsqrt_newton(dv, MVA[:, s * 4:s * 4 + 4],
                             MVA[:, 24 + s * 4:24 + s * 4 + 4])
                m3_ = pr.inc(dv.scalar_tensor_tensor(
                    MVA[:, 8 + s * 4:8 + s * 4 + 4], BN2A[:, s * 8:s * 8 + 8:2],
                    -1.0, MVA[:, s * 4:s * 4 + 4], ALU.mult, ALU.mult), "dve", 1)
                CM[f"ln1c{b}_{s}"] = pr.inc(
                    dv.tensor_scalar(MVA[:, 16 + s * 4:16 + s * 4 + 4],
                                     MVA[:, s * 4:s * 4 + 4], 1.0, None, ALU.add),
                    "dve", 1)
                pr.wait(dv, "dve", CM[f"ln1c{b}_{s}"])
                m4_ = pr.inc(dv.tensor_mul(RS[:, 0:4], MVA[:, s * 4:s * 4 + 4],
                                           MVA[:, s * 4:s * 4 + 4]), "dve", 1)
                pr.wait(dv, "dve", m4_)
                m5_ = pr.inc(dv.tensor_mul(RS[:, 0:4], RS[:, 0:4],
                                           BN2A[:, s * 8 + 1:s * 8 + 8:2]), "dve", 1)
                pr.wait(dv, "dve", m5_)
                m6_ = pr.inc(dv.tensor_scalar(RS[:, 0:4], RS[:, 0:4], EPS, None,
                                              ALU.add), "dve", 1)
                pr.wait(dv, "dve", m6_)
                rsqrt_newton(dv, RS[:, 4:8], RS[:, 0:4])
                pr.inc(dv.tensor_mul(AC[:, s * 4:s * 4 + 4],
                                     MVA[:, s * 4:s * 4 + 4], RS[:, 4:8]),
                       "dve", 1)
                CM[f"ln1f{b}_{s}"] = pr.inc(
                    dv.tensor_mul(AC[:, 8 + s * 4:8 + s * 4 + 4],
                                  MVA[:, 8 + s * 4:8 + s * 4 + 4], RS[:, 4:8]),
                    "dve", 1)
                pr.wait(dv, "dve", CM[f"ln1f{b}_{s}"])
            dve_sec(fv)

        def emit_A_zt(b, s):
            XB = T1 if s == 0 else XS1

            def fzt(ac):
                pr.wait(ac, "act", CM[f"ln1f{b}_{s}"])
                for mk in reg_free["T2"]:
                    pr.wait(ac, "act", mk)
                ins = None
                for tau in range(NTT):
                    ins = ac.activation(
                        T2[:, tau * 1024:(tau + 1) * 1024],
                        XB[:, tau * 1024:(tau + 1) * 1024], AF.Identity,
                        scale=AC[:, s * 4 + tau:s * 4 + tau + 1],
                        bias=AC[:, 8 + s * 4 + tau:8 + s * 4 + tau + 1])
                CM[f"t1{b}_{s}"] = pr.inc(ins, "act", 1)
                for mk in reg_free["Z"]:
                    pr.wait(ac, "act", mk)
                ins = None
                for tau in range(NTT):
                    g = s * NTT + tau
                    ins = ac.activation(
                        Z[:, g * 1024:(g + 1) * 1024],
                        XB[:, tau * 1024:(tau + 1) * 1024], AF.Identity,
                        scale=MVA[:, s * 4 + tau:s * 4 + tau + 1],
                        bias=MVA[:, 8 + s * 4 + tau:8 + s * 4 + tau + 1])
                CM[f"zx{b}_{s}"] = pr.inc(ins, "act", 1)
            act_sec(fzt)
            reg_free["T2"] = [CM[f"t1{b}_{s}"]]

        def emit_A_chain(b, s, defer_x1gen, x1_gate=()):
            emit_A_stats(b, s, x1_gate)
            emit_A_zt(b, s)
            if not defer_x1gen:
                emit_X1gen(b, s)

        def emit_X1gen(b, s):
            XB = T1 if s == 0 else XS1

            def fg(dv):
                if x1_store_marks[s] is not None:
                    pr.wait(dv, "dve", x1_store_marks[s])
                pr.wait(dv, "dve", CM[f"ln1c{b}_{s}"])
                ins = None
                for tau in range(NTT):
                    g = s * NTT + tau
                    nb = MVA[:, 8 + s * 4 + tau:8 + s * 4 + tau + 1]
                    rp1 = MVA[:, 16 + s * 4 + tau:16 + s * 4 + tau + 1]
                    ins = dv.tensor_scalar(X1[:, g * 1024:(g + 1) * 1024],
                                           XB[:, tau * 1024:(tau + 1) * 1024],
                                           rp1, nb, ALU.mult, ALU.add)
                CM[f"x1g{b}_{s}"] = pr.inc(ins, "dve", 1)
            dve_sec(fg)

        def emit_A_tail_sub(b, s, xn_marks):
            fills = tm_to_fm_pe(T2, [CM[f"t1{b}_{s}"]], "act")

            def fev_xn(ac):
                pr.waits(ac, "act", CONST_MARKS)
                for mk in reg_free["FMB"]:
                    pr.wait(ac, "act", mk)
                for c in range(ND):
                    bank = (4, 5)[c % 2]
                    pr.wait(ac, "act", fills[c])
                    ins = ac.activation(
                        FMB[:, c * 1024 + s * 512: c * 1024 + (s + 1) * 512],
                        psum[bank][:, 0:512], AF.Identity,
                        scale=CN1W[:, c:c + 1], bias=CN1B[:, c:c + 1])
                    mk = pr.inc(ins, "act", 1)
                    bank_free[bank] = mk
                    xn_marks.append(mk)
            act_sec(fev_xn)
            reg_free["T2"] = [pr.mark("pe")]

        def emit_A_full(b, xn_bysub):
            emit_A_chain(b, 0, defer_x1gen=False)
            emit_A_tail_sub(b, 0, xn_bysub[0])
            emit_A_chain(b, 1, defer_x1gen=False,
                         x1_gate=[CM["wms"], CM["wsum"]])
            emit_A_tail_sub(b, 1, xn_bysub[1])

        pending_xn = [[], []]
        emit_A_full(0, pending_xn)

        for b in range(n_blocks):
            xn_by_sub = pending_xn
            xn_marks = xn_by_sub[0] + xn_by_sub[1]
            pending_xn = [[], []]

            load_w(f"gate{b}", W_MIX[:, 0:16384],
                   s_gate.rearrange("p m k f -> p (m k f)"),
                   CM["cast_gate"], wmix_free)

            def gate_dst(s, m):
                if m < 8:
                    return B1[:, m * 1024 + s * 512: m * 1024 + (s + 1) * 512]
                return B2[:, (m - 8) * 1024 + s * 512: (m - 8) * 1024 + (s + 1) * 512]

            g1s_marks = mm_stage(
                W_MIX, lambda m, k: m * ND + k, NM_G, ND,
                lambda s, k: FMB[:, k * 1024 + s * 512: k * 1024 + (s + 1) * 512],
                "act", gate_dst, CM[f"gate{b}"], func=AF.Silu,
                pre_waits=list(reg_free["B1"]) + list(reg_free["B2"]),
                group_waits=lambda s, m: xn_by_sub[s], m_order=list(range(8)))
            gate_lo_done = pr.mark("pe")

            load_w(f"state{b}", W_MIX[:, 0:8192],
                   s_state.rearrange("p m k f -> p (m k f)"),
                   CM["cast_state"], gate_lo_done)

            g2s_marks = mm_stage(
                W_MIX, lambda m, k: m * ND + k, NM_G, ND,
                lambda s, k: FMB[:, k * 1024 + s * 512: k * 1024 + (s + 1) * 512],
                "act", gate_dst, CM[f"gate{b}"], func=AF.Silu,
                group_waits=lambda s, m: xn_by_sub[s],
                m_order=list(range(8, 16)))
            gate_last_mm = pr.mark("pe")

            def fv1(dv):
                pr.waits(dv, "dve", g1s_marks)
                pr.waits(dv, "dve", xn_marks)
                for mk in reg_free["B3"]:
                    pr.wait(dv, "dve", mk)
                CM[f"v1{b}"] = pr.inc(dv.tensor_mul(B3, FMB[:], B1), "dve", 1)
            dve_sec(fv1)

            dpre_base = pr.c["dve"]

            def fdpre_pe(t):
                pr.wait(t, "pe", CM["wms"])
                pr.waits(t, "pe", xn_marks)
                for s in range(NSUB):
                    if s > 0:
                        pr.wait(t, "pe", ("dve", dpre_base + s))
                    else:
                        pr.wait(t, "pe", bank_free[6])
                    ins = None
                    for k in range(ND):
                        ins = t.matmul(
                            psum[6][0:1, 0:512], WMS[:, k:k + 1],
                            FMB[:, k * 1024 + s * 512: k * 1024 + (s + 1) * 512],
                            start=(k == 0), stop=(k == ND - 1))
                    CM[f"dpref{b}_{s}"] = pr.inc(ins, "pe", 1)
            pe_sec(fdpre_pe)

            def fdpre_dv(dv):
                pr.wait(dv, "dve", CM["bsb"])
                for s in range(NSUB):
                    pr.wait(dv, "dve", CM[f"dpref{b}_{s}"])
                    mk = pr.inc(dv.tensor_scalar(R0[0:1, s * 512:(s + 1) * 512],
                                                 psum[6][0:1, 0:512], BSB[0:1, 1:2],
                                                 None, ALU.add), "dve", 1)
                    bank_free[6] = mk
                CM[f"dpre{b}"] = pr.mark("dve")
            dve_sec(fdpre_dv)
            assert pr.c["dve"] == dpre_base + NSUB

            def fda(ac):
                pr.wait(ac, "act", CM[f"dpre{b}"])
                m0 = pr.inc(ac.activation(R1[0:1, :], R0[0:1, :], AF.Abs), "act", 1)
                pr.wait(ac, "act", m0)
                m1_ = pr.inc(ac.activation(R2[0:1, :], R1[0:1, :], AF.Exp, scale=-1.0),
                             "act", 1)
                pr.wait(ac, "act", m1_)
                CM[f"sp{b}"] = pr.inc(
                    ac.activation(R1[0:1, :], R2[0:1, :], AF.Ln, bias=1.0), "act", 1)
                pr.wait(ac, "act", ("dve", pr.c["dve"] + 2))
                CM[f"arow{b}"] = pr.inc(
                    ac.activation(R1[0:1, :], R0[0:1, :], AF.Exp, scale=-1.0), "act", 1)
            act_sec(fda)

            def fdv2(dv):
                pr.wait(dv, "dve", CM[f"sp{b}"])
                m0 = pr.inc(dv.tensor_scalar_max(R2[0:1, :], R0[0:1, :], 0.0), "dve", 1)
                pr.wait(dv, "dve", m0)
                CM[f"delta{b}"] = pr.inc(
                    dv.tensor_add(R0[0:1, :], R2[0:1, :], R1[0:1, :]), "dve", 1)
            dve_sec(fdv2)

            bc_base = pr.c["dve"]

            def fbc_pe(t):
                pr.wait(t, "pe", CM["ones"])
                i = 0
                for bank, rm, row in [(6, f"delta{b}", R0), (7, f"arow{b}", R1)]:
                    pr.wait(t, "pe", CM[rm])
                    for s in range(NSUB):
                        if i % 2 == 1:
                            pr.wait(t, "pe", ("dve", bc_base + i))
                        else:
                            pr.wait(t, "pe", bank_free[bank])
                        ins = t.matmul(psum[bank][:, 0:512], ONES[:],
                                       row[0:1, s * 512:(s + 1) * 512],
                                       start=True, stop=True)
                        CM[f"bcf{b}_{i}"] = pr.inc(ins, "pe", 1)
                        i += 1
            pe_sec(fbc_pe)

            bc_marks = []

            def fbc_dv(dv):
                for mk in reg_free["SCBC"]:
                    pr.wait(dv, "dve", mk)
                i = 0
                for bank, dstv in [(6, D_BC), (7, A_BC)]:
                    for s in range(NSUB):
                        pr.wait(dv, "dve", CM[f"bcf{b}_{i}"])
                        mk = pr.inc(dv.tensor_copy(dstv[:, s * 512:(s + 1) * 512],
                                                   psum[bank][:, 0:512]), "dve", 1)
                        bank_free[bank] = mk
                        bc_marks.append(mk)
                        i += 1
            dve_sec(fbc_dv)
            assert pr.c["dve"] == bc_base + 4
            reg_free["SCBC"] = []

            u_marks = mm_stage(
                W_MIX, lambda m, k: m * ND + k, ND, ND,
                lambda s, k: B3[:, k * 1024 + s * 512: k * 1024 + (s + 1) * 512],
                "dve",
                lambda s, m: B1[:, m * 1024 + s * 512: m * 1024 + (s + 1) * 512],
                CM[f"state{b}"], bias_col=lambda m: CB_S[:, m:m + 1],
                pre_waits=[CM[f"v1{b}"]],
                extra_pe_waits=[CM[f"v1{b}"]] + g1s_marks)

            g2s_by_sub = [g2s_marks[0:8], g2s_marks[8:16]]
            bm_marks = mm_stage(
                W_MIX, lambda m, k: m * ND + k, ND, ND,
                lambda s, k: B2[:, k * 1024 + s * 512: k * 1024 + (s + 1) * 512],
                "act",
                lambda s, m: B3[:, m * 1024 + s * 512: m * 1024 + (s + 1) * 512],
                CM[f"state{b}"], bias_col=lambda m: CB_S[:, m:m + 1],
                pre_waits=[CM[f"v1{b}"]],
                extra_pe_waits=[CM[f"v1{b}"]],
                group_waits=lambda s, m: g2s_by_sub[s])
            state_last_mm = pr.mark("pe")

            load_w(f"outp{b}", W_MIX[:, 0:8192],
                   s_outp.rearrange("p m k f -> p (m k f)"),
                   CM["cast_outp"], state_last_mm)

            load_w(f"mlp1a{b}", W_BIG[:, 16384:24576],
                   s_mlp1[:, 8:16].rearrange("p m k f -> p (m k f)"),
                   CM["cast_mlp1"], state_last_mm)

            scan_marks = []

            def fscan(dv):
                pr.waits(dv, "dve", bc_marks)
                pr.waits(dv, "dve", u_marks)
                udm = pr.inc(dv.tensor_tensor(
                    B1[:, :].rearrange("p (c f) -> p c f", c=ND),
                    B1[:, :].rearrange("p (c f) -> p c f", c=ND),
                    D_BC[:, None, :].broadcast_to([P, ND, 1024]), ALU.mult),
                    "dve", 1)
                pr.wait(dv, "dve", udm)
                for c in range(ND):
                    pr.wait(dv, "dve", bm_marks[8 + c])
                    m0 = pr.inc(dv.tensor_tensor(BCH[c % 2],
                                                 B1[:, c * 1024:(c + 1) * 1024],
                                                 B3[:, c * 1024:(c + 1) * 1024],
                                                 ALU.mult), "dve", 1)
                    pr.wait(dv, "dve", m0)
                    init = 0.0 if b % 2 == 0 else HCAR[:, c:c + 1]
                    ins = dv.tensor_tensor_scan(B1[:, c * 1024:(c + 1) * 1024],
                                                A_BC, BCH[c % 2], init,
                                                ALU.mult, ALU.add)
                    scan_marks.append(pr.inc(ins, "dve", 1))
                if b % 2 == 0:
                    pr.waits(dv, "dve", scan_marks)
                    ins = None
                    for c in range(ND):
                        ins = dv.tensor_copy(
                            HCAR[:, c:c + 1],
                            B1[:, (c + 1) * 1024 - 1:(c + 1) * 1024])
                    CM[f"hcar{b}"] = pr.inc(ins, "dve", 1)
            dve_sec(fscan)
            if b % 2 == 0:
                scan_h_mark = CM[f"hcar{b}"]

            scan_done_dve = pr.mark("dve")
            load_w(f"mlp1b{b}", W_BIG[:, 24576:40960],
                   s_mlp1[:, 16:32].rearrange("p m k f -> p (m k f)"),
                   CM["cast_mlp1"], scan_done_dve)

            x2n_by_sub = [[], []]
            xb_last = None
            t1_free = list(reg_free["T1"])

            def emit_out_mm(s, pre):
                return mm_stage(
                    W_MIX, lambda m, k: m * ND + k, ND, ND,
                    lambda s_, k: B1[:, k * 1024 + s * 512: k * 1024 + (s + 1) * 512],
                    "act", lambda s_, m: T1[:, m * 512:(m + 1) * 512],
                    CM[f"outp{b}"], bias_col=lambda m: CB_O[:, m:m + 1],
                    pre_waits=pre, extra_pe_waits=list(scan_marks), subs=(s,))

            def emit_fcons(s, fills):
                xb_marks = []

                def fcons(dv, s=s, fills=fills, xb_marks=xb_marks):
                    pr.wait(dv, "dve", CM[f"x1g{b}_{s}"])
                    pr.wait(dv, "dve", CM[f"zx{b}_{s}"])
                    for i in range(NTT * 2):
                        tau, cg = divmod(i, 2)
                        bank = (4, 5)[cg]
                        g = s * NTT + tau
                        pr.wait(dv, "dve", fills[i])
                        sl = slice(g * 1024 + cg * 512, g * 1024 + (cg + 1) * 512)
                        sl2 = slice(tau * 1024 + cg * 512, tau * 1024 + (cg + 1) * 512)
                        pr.inc(dv.tensor_add(X1[:, sl], X1[:, sl],
                                             psum[bank][:, 0:512]), "dve", 1)
                        ins = dv.tensor_add(T2[:, sl2], Z[:, sl], psum[bank][:, 0:512])
                        mk = pr.inc(ins, "dve", 1)
                        bank_free[bank] = mk
                        xb_marks.append(mk)
                dve_sec(fcons)
                return xb_marks

            def emit_ln2_t2(s, xb_marks):
                def fv5(dv, s=s, xb_marks=xb_marks):
                    pr.waits(dv, "dve", xb_marks)
                    ins = None
                    for tau in range(NTT):
                        g = s * NTT + tau
                        dv.bn_stats(BN6[:, g * 12:g * 12 + 6],
                                    T2[:, tau * 1024:tau * 1024 + 512])
                        ins = dv.bn_stats(BN6[:, g * 12 + 6:g * 12 + 12],
                                          T2[:, tau * 1024 + 512:(tau + 1) * 1024])
                    m0 = pr.inc(ins, "dve", 1)
                    pr.wait(dv, "dve", m0)
                    for tau in range(NTT):
                        g = s * NTT + tau
                        ins = dv.bn_aggr(BN2[:, g * 2:g * 2 + 2],
                                         BN6[:, g * 12:(g + 1) * 12])
                    m1_ = pr.inc(ins, "dve", 1)
                    pr.wait(dv, "dve", m1_)
                    m2_ = pr.inc(dv.tensor_scalar(MV[:, 24 + s * 4:24 + s * 4 + 4],
                                                  BN2[:, s * 8 + 1:s * 8 + 8:2],
                                                  EPS, None, ALU.add), "dve", 1)
                    pr.wait(dv, "dve", m2_)
                    rsqrt_newton(dv, MV[:, s * 4:s * 4 + 4],
                                 MV[:, 24 + s * 4:24 + s * 4 + 4])
                    ins = None
                    for tau in range(NTT):
                        g = s * NTT + tau
                        ins = dv.tensor_scalar(
                            T2[:, tau * 1024:(tau + 1) * 1024],
                            T2[:, tau * 1024:(tau + 1) * 1024],
                            BN2[:, g * 2:g * 2 + 1],
                            MV[:, s * 4 + tau:s * 4 + tau + 1],
                            ALU.subtract, ALU.mult)
                    CM[f"t2{b}_{s}"] = pr.inc(ins, "dve", 1)
                dve_sec(fv5)

            def emit_x2n(s, fills2):
                def fev_x2n(ac, s=s, fills2=fills2):
                    pr.waits(ac, "act", CONST_MARKS)
                    for c in range(ND):
                        bank = (4, 5)[c % 2]
                        pr.wait(ac, "act", fills2[c])
                        ins = ac.activation(
                            FMB[:, c * 1024 + s * 512: c * 1024 + (s + 1) * 512],
                            psum[bank][:, 0:512], AF.Identity,
                            scale=CN2W[:, c:c + 1], bias=CN2B[:, c:c + 1])
                        mk = pr.inc(ins, "act", 1)
                        bank_free[bank] = mk
                        x2n_by_sub[s].append(mk)
                act_sec(fev_x2n)

            ssm0 = emit_out_mm(0, t1_free)
            fills_s0 = fm_to_tm_pe(T1, ssm0, "dve", 2)
            ssmT0_done = pr.mark("pe")
            xb0 = emit_fcons(0, fills_s0)
            emit_ln2_t2(0, xb0)
            xb_last = xb0[-1]

            ssm1 = emit_out_mm(1, [ssmT0_done])
            outmm1_pe = pr.mark("pe")
            load_w(f"mlp1c{b}", W_BIG[:, 8192:16384],
                   s_mlp1[:, 0:8].rearrange("p m k f -> p (m k f)"),
                   CM["cast_mlp1"], outmm1_pe)

            def mlp1_ready(m, b=b):
                if m < 8:
                    return CM[f"mlp1c{b}"]
                if m < 16:
                    return CM[f"mlp1a{b}"]
                return CM[f"mlp1b{b}"]
            mlp1_m_order = list(range(8, 32)) + list(range(0, 8))
            fills_t20 = tm_to_fm_pe(T2, [CM[f"t2{b}_0"]], "act")
            emit_x2n(0, fills_t20)
            t2T0_done = pr.mark("pe")
            fills_s1 = fm_to_tm_pe(T1, ssm1, "dve", 2)
            ssmT1_done = pr.mark("pe")
            xb1 = emit_fcons(1, fills_s1)
            emit_ln2_t2(1, xb1)
            xb_last = xb1[-1]

            m1_0 = mm_stage(
                W_BIG, lambda m, k: 64 + m * ND + k, NM_MLP, ND,
                lambda s_, k: FMB[:, k * 1024 + 0 * 512: k * 1024 + 1 * 512],
                "act", lambda s_, m: M1[:, m * 512:(m + 1) * 512],
                mlp1_ready, func=AF.Gelu,
                bias_col=lambda m: CB_M1[:, m:m + 1],
                pre_waits=list(reg_free["M1"]),
                extra_pe_waits=list(x2n_by_sub[0]) + list(reg_free["M1"]),
                subs=(0,), m_order=mlp1_m_order)

            fills_t21 = tm_to_fm_pe(T2, [CM[f"t2{b}_1"]], "act")
            emit_x2n(1, fills_t21)
            t2T1_done = pr.mark("pe")

            prev_ring = [None, None]
            ringslots = [0, 4096]

            def emit_mlp2(s, ring_free_mark, m1_marks):
                pe_base2 = pr.c["pe"]
                dve_base2 = pr.c["dve"]
                ring_marks = {}

                def fr(sy, s=s):
                    pr.wait(sy, "sync", CM["cast_mlp2"])
                    pr.wait(sy, "sync", ring_free_mark)
                    for m in range(ND):
                        slot = ringslots[m % 2]
                        if m < 2:
                            pr.wait(sy, "sync", prev_ring[m % 2])
                        else:
                            pr.wait(sy, "sync", ("pe", pe_base2 + m - 1))
                        ring_marks[m] = pr.dma(
                            sy, "sync", "ql", A_RING[:, slot:slot + 4096],
                            s_mlp2[:, m].rearrange("p k f -> p (k f)"))
                sync_sec(fr)

                def fpe2(t, s=s):
                    pr.waits(t, "pe", m1_marks)
                    for m in range(ND):
                        slot = ringslots[m % 2]
                        bank = m % 4
                        pr.wait(t, "pe", ring_marks[m])
                        if m >= 4:
                            pr.wait(t, "pe", ("dve", dve_base2 + m - 3))
                        else:
                            pr.wait(t, "pe", bank_free[bank])
                        ins = None
                        for k in range(NM_MLP):
                            ins = t.matmul(psum[bank][:, 0:512],
                                           A_RING[:, slot + k * P:slot + (k + 1) * P],
                                           M1[:, k * 512:(k + 1) * 512],
                                           start=(k == 0), stop=(k == NM_MLP - 1))
                        CM[f"m2f{b}_{s}_{m}"] = pr.inc(ins, "pe", 1)
                        prev_ring[m % 2] = CM[f"m2f{b}_{s}_{m}"]
                pe_sec(fpe2)

                mlpo_marks = []

                def fev2(dv, s=s):
                    for mk in reg_free["T2"]:
                        pr.wait(dv, "dve", mk)
                    for m in range(ND):
                        bank = m % 4
                        pr.wait(dv, "dve", CM[f"m2f{b}_{s}_{m}"])
                        ins = dv.tensor_scalar(T2[:, m * 512:(m + 1) * 512],
                                               psum[bank][:, 0:512],
                                               CB_M2[:, m:m + 1], None, ALU.add)
                        mk = pr.inc(ins, "dve", 1)
                        bank_free[bank] = mk
                        mlpo_marks.append(mk)
                dve_sec(fev2)
                assert pr.c["dve"] == dve_base2 + ND

                fills3 = fm_to_tm_pe(T2, mlpo_marks, "dve", 1)

                def fcons2(dv, s=s, fills3=fills3):
                    for i in range(NTT * 2):
                        tau, cg = divmod(i, 2)
                        bank = (4, 5)[cg]
                        g = s * NTT + tau
                        pr.wait(dv, "dve", fills3[i])
                        sl = slice(g * 1024 + cg * 512, g * 1024 + (cg + 1) * 512)
                        ins = dv.tensor_add(X1[:, sl], X1[:, sl], psum[bank][:, 0:512])
                        mk = pr.inc(ins, "dve", 1)
                        bank_free[bank] = mk
                dve_sec(fcons2)
                reg_free["T2"] = [pr.mark("pe")]
                CM[f"xmid{b}_{s}"] = pr.mark("dve")

            reg_free["T2"] = [t2T1_done]
            emit_mlp2(0, ssmT1_done, m1_0)
            mlp2_0_pe = pr.mark("pe")

            m1_1 = mm_stage(
                W_BIG, lambda m, k: 64 + m * ND + k, NM_MLP, ND,
                lambda s_, k: FMB[:, k * 1024 + 1 * 512: k * 1024 + 2 * 512],
                "act", lambda s_, m: M1[:, m * 512:(m + 1) * 512],
                mlp1_ready, func=AF.Gelu,
                bias_col=lambda m: CB_M1[:, m:m + 1],
                pre_waits=[mlp2_0_pe],
                extra_pe_waits=list(x2n_by_sub[1]) + [mlp2_0_pe],
                subs=(1,), m_order=mlp1_m_order)

            def emit_ln3(s, affno_free):
                def fv7(dv, s=s):
                    pr.wait(dv, "dve", CM[f"xmid{b}_{s}"])
                    ins = None
                    for tau in range(NTT):
                        g = s * NTT + tau
                        dv.bn_stats(BN6[:, g * 12:g * 12 + 6],
                                    X1[:, g * 1024:g * 1024 + 512])
                        ins = dv.bn_stats(BN6[:, g * 12 + 6:g * 12 + 12],
                                          X1[:, g * 1024 + 512:(g + 1) * 1024])
                    m0 = pr.inc(ins, "dve", 1)
                    pr.wait(dv, "dve", m0)
                    for tau in range(NTT):
                        g = s * NTT + tau
                        ins = dv.bn_aggr(BN2[:, g * 2:g * 2 + 2],
                                         BN6[:, g * 12:(g + 1) * 12])
                    m1_ = pr.inc(ins, "dve", 1)
                    pr.wait(dv, "dve", m1_)
                    m2_ = pr.inc(dv.tensor_scalar(MV[:, 24 + s * 4:24 + s * 4 + 4],
                                                  BN2[:, s * 8 + 1:s * 8 + 8:2],
                                                  EPS, None, ALU.add), "dve", 1)
                    pr.wait(dv, "dve", m2_)
                    rsqrt_newton(dv, MV[:, s * 4:s * 4 + 4],
                                 MV[:, 24 + s * 4:24 + s * 4 + 4])
                    CM[f"ln3c{b}_{s}"] = pr.inc(dv.scalar_tensor_tensor(
                        MV[:, 8 + s * 4:8 + s * 4 + 4], BN2[:, s * 8:s * 8 + 8:2],
                        -1.0, MV[:, s * 4:s * 4 + 4], ALU.mult, ALU.mult), "dve", 1)
                dve_sec(fv7)

                def fa8(ac, s=s, affno_free=affno_free):
                    pr.wait(ac, "act", CM[f"ln3c{b}_{s}"])
                    for mk in affno_free:
                        pr.wait(ac, "act", mk)
                    ins = None
                    for tau in range(NTT):
                        g = s * NTT + tau
                        ins = ac.activation(
                            A_FFNO[:, tau * 1024:(tau + 1) * 1024],
                            X1[:, g * 1024:(g + 1) * 1024], AF.Identity,
                            scale=MV[:, s * 4 + tau:s * 4 + tau + 1],
                            bias=MV[:, 8 + s * 4 + tau:8 + s * 4 + tau + 1])
                    CM[f"t3{b}_{s}"] = pr.inc(ins, "act", 1)
                act_sec(fa8)

            m11_pe = pr.mark("pe")
            emit_ln3(0, [m11_pe])

            emit_mlp2(1, ssmT1_done, m1_1)
            reg_free["M1"] = [pr.mark("pe")]
            reg_free["Z"] = [xb_last]
            mlp_last_pe = pr.mark("pe")
            reg_free["T1"] = [CM[f"m2f{b}_1_{ND - 1}"]]

            prev_g = [None, None]
            prev_u = [None, None]
            prev_d = [None, None]
            rgs = [8192, 9216]
            rus = [10240, 11264]
            rds = [12288, 15360]
            ringslots_d = rds

            def emit_zn(s):
                fills4 = tm_to_fm_pe(A_FFNO, [CM[f"t3{b}_{s}"]], "dve")
                zn_marks = []

                def fev_zn(dv, s=s, fills4=fills4):
                    for c in range(ND):
                        bank = (4, 5)[c % 2]
                        pr.wait(dv, "dve", fills4[c])
                        ins = dv.tensor_copy(
                            FMB[:, c * 1024 + s * 512: c * 1024 + (s + 1) * 512],
                            psum[bank][:, 0:512])
                        mk = pr.inc(ins, "dve", 1)
                        bank_free[bank] = mk
                        zn_marks.append(mk)
                dve_sec(fev_zn)
                return zn_marks, pr.mark("pe")

            def emit_gu(s, zn_marks, m1_free):
                pe_base3 = pr.c["pe"]
                act_base3 = pr.c["act"]
                dve_base3 = pr.c["dve"]

                def frg(sy, s=s):
                    pr.wait(sy, "sync", CM["cast_fg"])
                    pr.wait(sy, "sync", CM["cast_fu"])
                    pr.wait(sy, "sync", mlp_last_pe)
                    for m in range(NM_FFN):
                        sg_slot = rgs[m % 2]
                        su_slot = rus[m % 2]
                        if m < 2:
                            pr.wait(sy, "sync", prev_g[m % 2])
                        else:
                            pr.wait(sy, "sync", ("pe", pe_base3 + (m - 2) * 2 + 1))
                        CM[f"rg{b}_{s}_{m}"] = pr.dma(
                            sy, "sync", "ql", W_BIG[:, sg_slot:sg_slot + 1024],
                            s_fg[:, m].rearrange("p k f -> p (k f)"))
                        if m < 2:
                            pr.wait(sy, "sync", prev_u[m % 2])
                        else:
                            pr.wait(sy, "sync", ("pe", pe_base3 + (m - 2) * 2 + 2))
                        CM[f"ru{b}_{s}_{m}"] = pr.dma(
                            sy, "sync", "ql", W_BIG[:, su_slot:su_slot + 1024],
                            s_fu[:, m].rearrange("p k f -> p (k f)"))
                sync_sec(frg)

                def fpe3(t, s=s, act_base3=act_base3, dve_base3=dve_base3):
                    pr.waits(t, "pe", m1_free)
                    for m in range(NM_FFN):
                        sg_slot = rgs[m % 2]
                        su_slot = rus[m % 2]
                        bg = m % 2
                        bu = 2 + m % 2
                        pr.wait(t, "pe", CM[f"rg{b}_{s}_{m}"])
                        if m < 2:
                            pr.wait(t, "pe", bank_free[bg])
                            pr.waits(t, "pe", zn_marks)
                        else:
                            pr.wait(t, "pe", ("act", act_base3 + m - 1))
                        ins = None
                        for k in range(ND):
                            ins = t.matmul(
                                psum[bg][:, 0:512],
                                W_BIG[:, sg_slot + k * P:sg_slot + (k + 1) * P],
                                FMB[:, k * 1024 + s * 512: k * 1024 + (s + 1) * 512],
                                start=(k == 0), stop=(k == ND - 1))
                        CM[f"fgf{b}_{s}_{m}"] = pr.inc(ins, "pe", 1)
                        pr.wait(t, "pe", CM[f"ru{b}_{s}_{m}"])
                        if m < 2:
                            pr.wait(t, "pe", bank_free[bu])
                        else:
                            pr.wait(t, "pe", ("dve", dve_base3 + (m - 2) * 2 + 1))
                        for k in range(ND):
                            ins = t.matmul(
                                psum[bu][:, 0:512],
                                W_BIG[:, su_slot + k * P:su_slot + (k + 1) * P],
                                FMB[:, k * 1024 + s * 512: k * 1024 + (s + 1) * 512],
                                start=(k == 0), stop=(k == ND - 1))
                        CM[f"fuf{b}_{s}_{m}"] = pr.inc(ins, "pe", 1)
                pe_sec(fpe3)

                def fact3(ac, s=s, dve_base3=dve_base3):
                    for m in range(NM_FFN):
                        bg = m % 2
                        pr.wait(ac, "act", CM[f"fgf{b}_{s}_{m}"])
                        if m > 0:
                            pr.wait(ac, "act", ("dve", dve_base3 + (m - 1) * 2 + 2))
                        elif CM.get(f"hlast{b}_{1 - s}") is not None and s == 1:
                            pr.wait(ac, "act", CM[f"hlast{b}_0"])
                        ins = ac.activation(A_FS[:, 0:512], psum[bg][:, 0:512],
                                            AF.Silu, scale=QS)
                        CM[f"fsg{b}_{s}_{m}"] = pr.inc(ins, "act", 1)
                act_sec(fact3)
                assert pr.c["act"] == act_base3 + NM_FFN

                h_marks = []

                def fdve3(dv, s=s):
                    for m in range(NM_FFN):
                        bu = 2 + m % 2
                        pr.wait(dv, "dve", CM[f"fuf{b}_{s}_{m}"])
                        if s == 1 and m == 0:
                            pr.wait(dv, "dve", CM[f"fdf{b}_0_{ND - 1}"])
                        if h_marks:
                            pr.wait(dv, "dve", h_marks[-1])
                        m0 = pr.inc(dv.tensor_scalar(A_FS[:, 512:1024],
                                                     psum[bu][:, 0:512], QS, None,
                                                     ALU.mult), "dve", 1)
                        pr.wait(dv, "dve", m0)
                        pr.wait(dv, "dve", CM[f"fsg{b}_{s}_{m}"])
                        ins = dv.tensor_mul(M1[:, m * 512:(m + 1) * 512],
                                            A_FS[:, 0:512], A_FS[:, 512:1024])
                        mk = pr.inc(ins, "dve", 1)
                        h_marks.append(mk)
                dve_sec(fdve3)
                assert pr.c["dve"] == dve_base3 + 2 * NM_FFN
                CM[f"hlast{b}_{s}"] = h_marks[-1]
                prev_g[0] = CM[f"fgf{b}_{s}_{NM_FFN - 2}"]
                prev_g[1] = CM[f"fgf{b}_{s}_{NM_FFN - 1}"]
                prev_u[0] = CM[f"fuf{b}_{s}_{NM_FFN - 2}"]
                prev_u[1] = CM[f"fuf{b}_{s}_{NM_FFN - 1}"]
                return h_marks

            def emit_down(s, h_marks, extra_pe=()):
                pe_base4 = pr.c["pe"]

                def frd(sy, s=s):
                    pr.wait(sy, "sync", CM["cast_fd"])
                    pr.wait(sy, "sync", mlp_last_pe)
                    for m in range(ND):
                        slot = rds[m % 2]
                        if m < 2:
                            pr.wait(sy, "sync", prev_d[m % 2])
                        else:
                            pr.wait(sy, "sync", ("pe", pe_base4 + m - 1))
                        CM[f"rd{b}_{s}_{m}"] = pr.dma(
                            sy, "sync", "ql", W_BIG[:, slot:slot + 3072],
                            s_fd[:, m].rearrange("p k f -> p (k f)"))
                sync_sec(frd)

                act_base4 = pr.c["act"]

                def fpe4(t, s=s, act_base4=act_base4):
                    pr.waits(t, "pe", h_marks)
                    for mk in extra_pe:
                        pr.wait(t, "pe", mk)
                    for m in range(ND):
                        slot = rds[m % 2]
                        bank = m % 4
                        pr.wait(t, "pe", CM[f"rd{b}_{s}_{m}"])
                        if m >= 4:
                            pr.wait(t, "pe", ("act", act_base4 + m - 3))
                        else:
                            pr.wait(t, "pe", bank_free[bank])
                        ins = None
                        for k in range(NM_FFN):
                            ins = t.matmul(psum[bank][:, 0:512],
                                           W_BIG[:, slot + k * P:slot + (k + 1) * P],
                                           M1[:, k * 512:(k + 1) * 512],
                                           start=(k == 0), stop=(k == NM_FFN - 1))
                        CM[f"fdf{b}_{s}_{m}"] = pr.inc(ins, "pe", 1)
                pe_sec(fpe4)
                prev_d[0] = CM[f"fdf{b}_{s}_{ND - 2}"]
                prev_d[1] = CM[f"fdf{b}_{s}_{ND - 1}"]
                return act_base4

            def emit_fev4(s, act_base4):
                ffno_marks = []

                def fev4(ac, s=s):
                    for m in range(ND):
                        bank = m % 4
                        pr.wait(ac, "act", CM[f"fdf{b}_{s}_{m}"])
                        ins = ac.activation(A_FFNO[:, m * 512:(m + 1) * 512],
                                            psum[bank][:, 0:512], AF.Identity,
                                            scale=QS)
                        mk = pr.inc(ins, "act", 1)
                        bank_free[bank] = mk
                        ffno_marks.append(mk)
                act_sec(fev4)
                assert pr.c["act"] == act_base4 + ND
                return ffno_marks

            def emit_ffn_tail(s, ffno_marks):
                fills5 = fm_to_tm_pe(A_FFNO, ffno_marks, "dve", 1)
                xo_marks = []

                def fcons3(dv, s=s, fills5=fills5, xo_marks=xo_marks):
                    for i in range(NTT * 2):
                        tau, cg = divmod(i, 2)
                        bank = (4, 5)[cg]
                        g = s * NTT + tau
                        pr.wait(dv, "dve", fills5[i])
                        sl = slice(g * 1024 + cg * 512, g * 1024 + (cg + 1) * 512)
                        ins = dv.tensor_add(X1[:, sl], X1[:, sl], psum[bank][:, 0:512])
                        mk = pr.inc(ins, "dve", 1)
                        bank_free[bank] = mk
                        xo_marks.append(mk)
                dve_sec(fcons3)

                def fst(gp, s=s, xo_marks=xo_marks):
                    pr.waits(gp, "pool", xo_marks)
                    x1_store_marks[s] = pr.dma(
                        gp, "pool", "qs",
                        y_out[b, s].rearrange("p tt d -> p (tt d)"),
                        X1[:, s * 4096:(s + 1) * 4096])
                gp_sec(fst)

            zn0, zn0_pe = emit_zn(0)
            h0 = emit_gu(0, zn0, reg_free["M1"])
            emit_ln3(1, [zn0_pe])
            zn1, _ = emit_zn(1)
            d0base = emit_down(0, h0)
            ffno0 = emit_fev4(0, d0base)
            h1 = emit_gu(1, zn1, [])
            ffn_gu_done = pr.mark("pe")
            emit_ffn_tail(0, ffno0)
            if b + 1 < n_blocks:
                emit_A_stats(b + 1, 0)
                emit_A_zt(b + 1, 0)
                emit_A_stats(b + 1, 1, x1_gate=[m11_pe])
                emit_X1gen(b + 1, 0)
            d1base = emit_down(1, h1)
            ffno1 = emit_fev4(1, d1base)
            if b + 1 < n_blocks:
                wmix_free_next = pr.mark("pe")
                reg_free["FMB"] = [ffn_gu_done]
                emit_A_tail_sub(b + 1, 0, pending_xn[0])
                emit_A_zt(b + 1, 1)
            emit_ffn_tail(1, ffno1)
            reg_free["M1"] = [pr.mark("pe")]
            if b + 1 < n_blocks:
                emit_X1gen(b + 1, 1)
                emit_A_tail_sub(b + 1, 1, pending_xn[1])

            endpe = pr.mark("pe")
            enddve = pr.mark("dve")
            reg_free["B1"] = [endpe, enddve]
            reg_free["B2"] = [endpe, enddve]
            reg_free["B3"] = [endpe, enddve]
            reg_free["SCBC"] = [endpe, enddve]
            if b + 1 < n_blocks:
                reg_free["SCBC"].append(CM[f"zx{b + 1}_1"])
            if b + 1 < n_blocks:
                wmix_free = wmix_free_next
            else:
                wmix_free = endpe

        def fin(gp):
            for i in range(4):
                gp.wait_ge(pr.h[f"qs{i}"], pr.c[f"qs{i}"])
        gp_sec(fin)

    return nc



def _tile_pm(w, mc, kc):
    K, M = w.shape
    assert K == kc * P and M == mc * P
    return np.ascontiguousarray(w.reshape(kc, P, mc, P).transpose(1, 2, 0, 3))


def _cols(v):
    n = v.shape[0] // P
    return np.ascontiguousarray(v.reshape(n, P).T)


def _x_layout(x):
    return np.ascontiguousarray(
        x.reshape(NB, NSUB, NTT, P, D).transpose(0, 1, 3, 2, 4))


def _y_unlayout(y):
    return np.asarray(y).transpose(0, 1, 3, 2, 4).reshape(NSEQ, TSEQ, D)


def make_inputs(L0_attn_x, L0_attn_norm1_w, L0_attn_norm1_b, L0_attn_gate_w,
                L0_attn_gate_b, L0_attn_state_w, L0_attn_state_b, L0_attn_out_w,
                L0_attn_out_b, L0_attn_norm2_w, L0_attn_norm2_b, L0_attn_mlp_w1,
                L0_attn_mlp_b1, L0_attn_mlp_w2, L0_attn_mlp_b2, L0_ffn_w_gate,
                L0_ffn_w_up, L0_ffn_w_down, n_cores=8):
    x = np.asarray(L0_attn_x, dtype=np.float32)
    common = {
        "w_gate": _tile_pm(np.asarray(L0_attn_gate_w, np.float32), NM_G, ND),
        "w_state": _tile_pm(np.asarray(L0_attn_state_w, np.float32), ND, ND),
        "w_outp": _tile_pm(np.asarray(L0_attn_out_w, np.float32), ND, ND),
        "w_mlp1": _tile_pm(np.asarray(L0_attn_mlp_w1, np.float32), NM_MLP, ND),
        "w_mlp2": _tile_pm(np.asarray(L0_attn_mlp_w2, np.float32), ND, NM_MLP),
        "w_fg": _tile_pm(np.ascontiguousarray(np.asarray(L0_ffn_w_gate, np.int32).T),
                         NM_FFN, ND),
        "w_fu": _tile_pm(np.ascontiguousarray(np.asarray(L0_ffn_w_up, np.int32).T),
                         NM_FFN, ND),
        "w_fd": _tile_pm(np.ascontiguousarray(np.asarray(L0_ffn_w_down, np.int32).T),
                         ND, NM_FFN),
        "c_gate_b": _cols(np.asarray(L0_attn_gate_b, np.float32)),
        "c_state_b": _cols(np.asarray(L0_attn_state_b, np.float32)),
        "c_out_b": _cols(np.asarray(L0_attn_out_b, np.float32)),
        "c_mlp_b1": _cols(np.asarray(L0_attn_mlp_b1, np.float32)),
        "c_mlp_b2": _cols(np.asarray(L0_attn_mlp_b2, np.float32)),
        "c_n1w": _cols(np.asarray(L0_attn_norm1_w, np.float32)),
        "c_n1b": _cols(np.asarray(L0_attn_norm1_b, np.float32)),
        "c_n2w": _cols(np.asarray(L0_attn_norm2_w, np.float32)),
        "c_n2b": _cols(np.asarray(L0_attn_norm2_b, np.float32)),
        "r_state_b": np.ascontiguousarray(
            np.asarray(L0_attn_state_b, np.float32).reshape(1, D)),
        "c_ident": np.eye(P, dtype=np.float32),
        "c_ones": np.ones((1, P), dtype=np.float32),
    }
    in_maps = []
    for c in range(n_cores):
        m = dict(common)
        m["x_in"] = _x_layout(x[c * NSEQ:(c + 1) * NSEQ])
        in_maps.append(m)
    return in_maps


def kernel(**inputs):
    in_maps = make_inputs(**inputs)
    nc = build_program()
    res = run_bass_kernel_spmd(nc, in_maps, core_ids=list(range(8)))
    return np.concatenate([_y_unlayout(r["y_out"]) for r in res.results], axis=0)


# version 19 (final)
# speedup vs baseline: 1.2232x; 1.0374x over previous
import contextlib
import numpy as np
import concourse.bass as bass
import concourse.mybir as mybir
from concourse.bass_utils import run_bass_kernel_spmd

F32 = mybir.dt.float32
BF16 = mybir.dt.bfloat16
I32 = mybir.dt.int32
AF = mybir.ActivationFunctionType
ALU = mybir.AluOpType
AXX = mybir.AxisListType.X

P = 128
D = 1024
ND = 8
NSEQ = 4
TSEQ = 2048
TB = 1024
NB = 8
NSUB = 2
SUB = 512
NTT = 4
NM_G = 16
NM_MLP = 32
NM_FFN = 24
EPS = 1e-5
QS = 1.0 / 127.0
INV_D = 1.0 / 1024.0


class Prog:
    def __init__(self):
        self.h = {}
        self.c = {}
        self.waited = {}

    def add_sem(self, name, handle):
        self.h[name] = handle
        self.c[name] = 0

    def inc(self, instr, sem, n=1):
        instr.then_inc(self.h[sem], n)
        self.c[sem] += n
        return (sem, self.c[sem])

    def mark(self, sem):
        return (sem, self.c[sem])

    def wait(self, eng, engname, m):
        if m is None:
            return
        sem, v = m
        if v <= 0:
            return
        key = (engname, sem)
        if self.waited.get(key, -1) >= v:
            return
        eng.wait_ge(self.h[sem], v)
        self.waited[key] = v

    def waits(self, eng, engname, marks):
        for m in marks:
            self.wait(eng, engname, m)

    def init_dmaq(self, qname, semnames):
        if not hasattr(self, "q"):
            self.q = {}
        self.q[qname] = {"sems": list(semnames), "rr": 0}

    def dma(self, eng, engname, qname, out, in_):
        q = self.q[qname]
        i = q["rr"]
        q["rr"] = (i + 1) % len(q["sems"])
        s = q["sems"][i]
        self.wait(eng, engname, (s, self.c[s]))
        return self.inc(eng.dma_start(out=out, in_=in_), s, 16)


def build_program(n_blocks=NB):
    nc = bass.Bass("TRN2", target_bir_lowering=False, debug=False)

    x_in = nc.dram_tensor("x_in", [NB, NSUB, P, NTT, D], F32, kind="ExternalInput").ap()
    y_out = nc.dram_tensor("y_out", [NB, NSUB, P, NTT, D], F32, kind="ExternalOutput").ap()

    w_gate = nc.dram_tensor("w_gate", [P, NM_G, ND, P], F32, kind="ExternalInput").ap()
    w_state = nc.dram_tensor("w_state", [P, ND, ND, P], F32, kind="ExternalInput").ap()
    w_outp = nc.dram_tensor("w_outp", [P, ND, ND, P], F32, kind="ExternalInput").ap()
    w_mlp1 = nc.dram_tensor("w_mlp1", [P, NM_MLP, ND, P], F32, kind="ExternalInput").ap()
    w_mlp2 = nc.dram_tensor("w_mlp2", [P, ND, NM_MLP, P], F32, kind="ExternalInput").ap()
    w_fg = nc.dram_tensor("w_fg", [P, NM_FFN, ND, P], I32, kind="ExternalInput").ap()
    w_fu = nc.dram_tensor("w_fu", [P, NM_FFN, ND, P], I32, kind="ExternalInput").ap()
    w_fd = nc.dram_tensor("w_fd", [P, ND, NM_FFN, P], I32, kind="ExternalInput").ap()

    c_gate_b = nc.dram_tensor("c_gate_b", [P, NM_G], F32, kind="ExternalInput").ap()
    c_state_b = nc.dram_tensor("c_state_b", [P, ND], F32, kind="ExternalInput").ap()
    c_out_b = nc.dram_tensor("c_out_b", [P, ND], F32, kind="ExternalInput").ap()
    c_mlp_b1 = nc.dram_tensor("c_mlp_b1", [P, NM_MLP], F32, kind="ExternalInput").ap()
    c_mlp_b2 = nc.dram_tensor("c_mlp_b2", [P, ND], F32, kind="ExternalInput").ap()
    c_n1w = nc.dram_tensor("c_n1w", [P, ND], F32, kind="ExternalInput").ap()
    c_n1b = nc.dram_tensor("c_n1b", [P, ND], F32, kind="ExternalInput").ap()
    c_n2w = nc.dram_tensor("c_n2w", [P, ND], F32, kind="ExternalInput").ap()
    c_n2b = nc.dram_tensor("c_n2b", [P, ND], F32, kind="ExternalInput").ap()
    r_state_b = nc.dram_tensor("r_state_b", [1, D], F32, kind="ExternalInput").ap()
    c_ident = nc.dram_tensor("c_ident", [P, P], F32, kind="ExternalInput").ap()
    c_ones = nc.dram_tensor("c_ones", [1, P], F32, kind="ExternalInput").ap()

    s_gate = nc.dram_tensor("s_gate", [P, NM_G, ND, P], BF16, kind="Internal").ap()
    s_state = nc.dram_tensor("s_state", [P, ND, ND, P], BF16, kind="Internal").ap()
    s_outp = nc.dram_tensor("s_outp", [P, ND, ND, P], BF16, kind="Internal").ap()
    s_mlp1 = nc.dram_tensor("s_mlp1", [P, NM_MLP, ND, P], BF16, kind="Internal").ap()
    s_mlp2 = nc.dram_tensor("s_mlp2", [P, ND, NM_MLP, P], BF16, kind="Internal").ap()
    s_fg = nc.dram_tensor("s_fg", [P, NM_FFN, ND, P], BF16, kind="Internal").ap()
    s_fu = nc.dram_tensor("s_fu", [P, NM_FFN, ND, P], BF16, kind="Internal").ap()
    s_fd = nc.dram_tensor("s_fd", [P, ND, NM_FFN, P], BF16, kind="Internal").ap()

    pr = Prog()

    with contextlib.ExitStack() as es:
        E = es.enter_context

        X1 = E(nc.sbuf_tensor("X1", [P, 8192], F32))
        FMB = E(nc.sbuf_tensor("FMB", [P, 8192], BF16))
        TTR = E(nc.sbuf_tensor("TTR", [P, 8192], F32))
        T1 = TTR[:, 0:4096]
        T2 = TTR[:, 4096:8192]
        M1R = E(nc.sbuf_tensor("M1R", [P, 16384], BF16))
        M1 = M1R
        W_MIX = M1R
        REG = E(nc.sbuf_tensor("REG", [P, 20480], F32))
        Z = REG[:, 0:4096].bitcast(BF16)
        B1 = REG[:, 4096:8192].bitcast(BF16)
        B2 = REG[:, 8192:12288].bitcast(BF16)
        B3 = REG[:, 12288:16384].bitcast(BF16)
        SCBC = REG[:, 16384:20480]
        W_BIG = REG[:, :].bitcast(BF16)
        A_PRO = REG[:, 0:1024]
        A_FS = REG[:, 9216:10240]
        A_FFNO = REG[:, 10240:14336]
        A_RING = TTR[:, 0:4096].bitcast(BF16)

        def small(name, shape, dtype):
            return E(nc.sbuf_tensor(name, shape, dtype))

        IDF = small("IDF", [P, P], F32)
        ONES = small("ONES", [1, P], F32)
        CB_G = small("CB_G", [P, NM_G], F32)
        CB_S = small("CB_S", [P, ND], F32)
        CB_O = small("CB_O", [P, ND], F32)
        CB_M1 = small("CB_M1", [P, NM_MLP], F32)
        CB_M2 = small("CB_M2", [P, ND], F32)
        CN1W = small("CN1W", [P, ND], F32)
        CN1B = small("CN1B", [P, ND], F32)
        CN2W = small("CN2W", [P, ND], F32)
        CN2B = small("CN2B", [P, ND], F32)
        WMS = small("WMS", [P, ND], BF16)
        HCAR = small("HCAR", [P, ND], BF16)
        BN6 = small("BN6", [P, 96], F32)
        BN2 = small("BN2", [P, 16], F32)
        MV = small("MV", [P, 32], F32)
        ZS = small("ZS", [P, 16], F32)
        RS = small("RS", [P, 8], F32)
        AC = small("AC", [P, 16], F32)
        WSA = small("WSA", [P, 8], F32)
        BSB = small("BSB", [1, 4], F32)
        MAGIC = small("MAGIC", [P, 4], I32)
        NEWT = small("NEWT", [P, 8], F32)
        BN6A = small("BN6A", [P, 96], F32)
        BN2A = small("BN2A", [P, 16], F32)
        MVA = small("MVA", [P, 32], F32)
        R0 = small("R0", [1, 1024], F32)
        R1 = small("R1", [1, 1024], F32)
        R2 = small("R2", [1, 1024], F32)

        A_BC = SCBC[:, 0:1024]
        D_BC = SCBC[:, 2048:3072]
        BCH = [SCBC[:, 1024:2048], SCBC[:, 3072:4096]]

        psum = [E(nc.psum_tensor(f"pb{i}", [P, 512], F32)) for i in range(8)]
        bank_fill = [None] * 8
        bank_free = [None] * 8

        for s_ in (["pe", "act", "dve"]
                   + [f"ql{i}" for i in range(8)] + [f"qs{i}" for i in range(4)]):
            pr.add_sem(s_, E(nc.semaphore(s_)))
        pr.init_dmaq("ql", [f"ql{i}" for i in range(8)])
        pr.init_dmaq("qs", [f"qs{i}" for i in range(4)])
        block = E(nc.Block())

        def sync_sec(f):
            @block.sync
            def _(e):
                f(e)

        def gp_sec(f):
            @block.gpsimd
            def _(e):
                f(e)

        def pe_sec(f):
            @block.tensor
            def _(e):
                f(e)

        def act_sec(f):
            @block.scalar
            def _(e):
                f(e)

        def dve_sec(f):
            @block.vector
            def _(e):
                f(e)

        EV = {"act": act_sec, "dve": dve_sec}
        CM = {}

        def prologue():
            def f(sy):
                for name, dst, src in [
                    ("ident", IDF[:], c_ident[:, :]), ("ones", ONES[:], c_ones[:, :]),
                    ("gb", CB_G[:], c_gate_b[:, :]), ("sb", CB_S[:], c_state_b[:, :]),
                    ("ob", CB_O[:], c_out_b[:, :]), ("m1b", CB_M1[:], c_mlp_b1[:, :]),
                    ("m2b", CB_M2[:], c_mlp_b2[:, :]),
                    ("n1w", CN1W[:], c_n1w[:, :]), ("n1b", CN1B[:], c_n1b[:, :]),
                    ("n2w", CN2W[:], c_n2w[:, :]), ("n2b", CN2B[:], c_n2b[:, :]),
                    ("sbr", R2[0:1, :], r_state_b[:, :]),
                ]:
                    CM[name] = pr.dma(sy, "sync", "ql", dst, src)
            sync_sec(f)

            def g(gp):
                prev = None
                for name, dst, src in [
                    ("gate", s_gate, w_gate), ("state", s_state, w_state),
                    ("outp", s_outp, w_outp), ("mlp1", s_mlp1, w_mlp1),
                    ("mlp2", s_mlp2, w_mlp2), ("fg", s_fg, w_fg),
                    ("fu", s_fu, w_fu), ("fd", s_fd, w_fd),
                ]:
                    if prev is not None:
                        pr.wait(gp, "pool", prev)
                    CM["cast_" + name] = pr.dma(
                        gp, "pool", "qs",
                        dst.rearrange("p m k f -> p (m k f)"),
                        src.rearrange("p m k f -> p (m k f)"))
                    prev = CM["cast_" + name]
            gp_sec(g)

            SW = REG[:, 8192:16384]
            SWR = REG[:, 16384:16448]

            def fl(sy):
                CM["wsm"] = pr.dma(sy, "sync", "ql", SW,
                                   w_state.rearrange("p m k f -> p (m k f)"))
            sync_sec(fl)

            def vl(dv):
                pr.wait(dv, "dve", CM["wsm"])
                rm = pr.inc(dv.tensor_reduce(
                    SWR[:, 0:64],
                    SW.rearrange("p (mk f) -> p mk f", f=P),
                    AXX, ALU.add), "dve", 1)
                pr.wait(dv, "dve", rm)
                mm = pr.inc(dv.tensor_copy(WSA[:], SWR[:, 0:ND]), "dve", 1)
                for m in range(1, ND):
                    pr.wait(dv, "dve", mm)
                    mm = pr.inc(dv.tensor_add(WSA[:], WSA[:],
                                              SWR[:, m * ND:(m + 1) * ND]),
                                "dve", 1)
                pr.wait(dv, "dve", mm)
                CM["wsum"] = mm
            dve_sec(vl)

            def vf(dv):
                mgm = pr.inc(dv.memset(MAGIC[:], 0x5f3759df), "dve", 1)
                pr.wait(dv, "dve", mgm)
                CM["wms"] = pr.inc(
                    dv.tensor_scalar(WMS[:], WSA[:], INV_D, None, ALU.mult), "dve", 1)
                pr.wait(dv, "dve", CM["sbr"])
                mm = pr.inc(dv.tensor_reduce(BSB[0:1, 0:1], R2[0:1, :], AXX, ALU.add),
                            "dve", 1)
                pr.wait(dv, "dve", mm)
                CM["bsb"] = pr.inc(
                    dv.tensor_scalar(BSB[0:1, 1:2], BSB[0:1, 0:1], INV_D, None, ALU.mult),
                    "dve", 1)
            dve_sec(vf)

        prologue()

        reg_free = {
            "T1": [], "T2": [], "Z": [], "B1": [],
            "B2": [CM["wsum"]], "B3": [CM["wsum"]],
            "FMB": [], "M1": [], "SCBC": [CM["wms"]], "X1": [],
        }
        CONST_MARKS = [CM[n] for n in (
            "ident", "ones", "gb", "sb", "ob", "m1b", "m2b",
            "n1w", "n1b", "n2w", "n2b", "sbr")]

        def rsqrt_newton(dv, dst, src):
            dsti = dst.bitcast(I32)
            srci = src.bitcast(I32)
            n = dst.shape[1]
            sh = NEWT[:, 0:n].bitcast(I32)
            tq = NEWT[:, 4:4 + n]
            m0 = pr.inc(dv.tensor_scalar(sh, srci, 1, None,
                                         ALU.logical_shift_right), "dve", 1)
            pr.wait(dv, "dve", m0)
            m0 = pr.inc(dv.tensor_tensor(dsti, MAGIC[:, 0:n], sh, ALU.subtract),
                        "dve", 1)
            pr.wait(dv, "dve", m0)
            for _ in range(3):
                m0 = pr.inc(dv.tensor_mul(tq, dst, dst), "dve", 1)
                pr.wait(dv, "dve", m0)
                m0 = pr.inc(dv.tensor_mul(tq, tq, src), "dve", 1)
                pr.wait(dv, "dve", m0)
                m0 = pr.inc(dv.tensor_scalar(tq, tq, -0.5, 1.5, ALU.mult, ALU.add),
                            "dve", 1)
                pr.wait(dv, "dve", m0)
                m0 = pr.inc(dv.tensor_mul(dst, dst, tq), "dve", 1)
                pr.wait(dv, "dve", m0)
            return m0

        def mm_stage(wsb, w_tile, n_m, n_k, rhs_fn, eng, dst_fn, w_ready,
                     func=AF.Identity, bias_col=None, scale=1.0, pre_waits=(),
                     extra_pe_waits=(), banks=(0, 1, 2, 3), subs=(0, 1),
                     group_waits=None, m_order=None):
            sem = eng
            evac_base = pr.c[sem]
            nbk = len(banks)
            idx = [(s, m) for s in subs
                   for m in (m_order if m_order is not None else range(n_m))]
            fills = []

            def fpe(t):
                if not callable(w_ready):
                    pr.wait(t, "pe", w_ready)
                for mk in extra_pe_waits:
                    pr.wait(t, "pe", mk)
                for i, (s, m) in enumerate(idx):
                    if callable(w_ready):
                        pr.wait(t, "pe", w_ready(m))
                    if group_waits is not None:
                        pr.waits(t, "pe", group_waits(s, m))
                    bank = banks[i % nbk]
                    if i >= nbk:
                        pr.wait(t, "pe", (sem, evac_base + i - nbk + 1))
                    else:
                        pr.wait(t, "pe", bank_free[bank])
                    ins = None
                    for k in range(n_k):
                        ins = t.matmul(
                            psum[bank][:, 0:512],
                            wsb[:, w_tile(m, k) * P:(w_tile(m, k) + 1) * P],
                            rhs_fn(s, k), start=(k == 0), stop=(k == n_k - 1))
                    fills.append(pr.inc(ins, "pe", 1))
            pe_sec(fpe)

            marks = []

            def fev(e):
                pr.waits(e, eng, CONST_MARKS)
                for mk in pre_waits:
                    pr.wait(e, eng, mk)
                for i, (s, m) in enumerate(idx):
                    bank = banks[i % nbk]
                    pr.wait(e, eng, fills[i])
                    if eng == "act":
                        ins = e.activation(dst_fn(s, m), psum[bank][:, 0:512], func,
                                           bias=(bias_col(m) if bias_col else 0.0),
                                           scale=scale)
                    else:
                        if bias_col is not None:
                            ins = e.tensor_scalar(dst_fn(s, m), psum[bank][:, 0:512],
                                                  bias_col(m), None, ALU.add)
                        elif scale != 1.0:
                            ins = e.tensor_scalar(dst_fn(s, m), psum[bank][:, 0:512],
                                                  scale, None, ALU.mult)
                        else:
                            ins = e.tensor_copy(dst_fn(s, m), psum[bank][:, 0:512])
                    mk = pr.inc(ins, eng, 1)
                    bank_free[bank] = mk
                    marks.append(mk)
            EV[eng](fev)
            assert pr.c[sem] == evac_base + len(idx)
            return marks

        def tm_to_fm_pe(src, src_ready, eng, banks=(4, 5)):
            sem = eng
            base = pr.c[sem]
            fills = []

            def fpe(t):
                for mk in src_ready:
                    pr.wait(t, "pe", mk)
                pr.wait(t, "pe", CM["ident"])
                for c in range(ND):
                    bank = banks[c % 2]
                    if c >= 2:
                        pr.wait(t, "pe", (sem, base + c - 1))
                    else:
                        pr.wait(t, "pe", bank_free[bank])
                    ins = None
                    for tau in range(NTT):
                        ins = t.transpose(
                            psum[bank][:, tau * P:(tau + 1) * P],
                            src[:, tau * 1024 + c * P: tau * 1024 + (c + 1) * P],
                            IDF[:])
                    fills.append(pr.inc(ins, "pe", 1))
            pe_sec(fpe)
            return fills

        def fm_to_tm_pe(src, src_ready, eng, incs_per_group, banks=(4, 5)):
            sem = eng
            base = pr.c[sem]
            fills = []

            def fpe(t):
                for mk in src_ready:
                    pr.wait(t, "pe", mk)
                pr.wait(t, "pe", CM["ident"])
                for i in range(NTT * 2):
                    tau, cg = divmod(i, 2)
                    bank = banks[cg]
                    if i >= 2:
                        pr.wait(t, "pe", (sem, base + (i - 1) * incs_per_group))
                    else:
                        pr.wait(t, "pe", bank_free[bank])
                    ins = None
                    for j in range(4):
                        c = cg * 4 + j
                        ins = t.transpose(
                            psum[bank][:, j * P:(j + 1) * P],
                            src[:, c * 512 + tau * P: c * 512 + (tau + 1) * P],
                            IDF[:])
                    fills.append(pr.inc(ins, "pe", 1))
            pe_sec(fpe)
            return fills

        x1_store_marks = [None] * NSUB
        wmix_free = None
        scan_h_mark = None

        def load_w(name, dst2, src2, cast_mark, prior_use):
            def f(sy):
                pr.wait(sy, "sync", cast_mark)
                pr.wait(sy, "sync", prior_use)
                CM[name] = pr.dma(sy, "sync", "ql", dst2, src2)
            sync_sec(f)

        def emit_A_chain(b, s, defer_x1gen):
            def fx(sy):
                if s == 0:
                    for mk in reg_free["T1"]:
                        pr.wait(sy, "sync", mk)
                else:
                    pr.wait(sy, "sync", CM[f"x1g{b}_0"])
                    pr.wait(sy, "sync", CM[f"zx{b}_0"])
                CM[f"x{b}_{s}"] = pr.dma(
                    sy, "sync", "ql",
                    T1[:, :], x_in[b, s].rearrange("p tt d -> p (tt d)"))
            sync_sec(fx)

            def fv(dv):
                pr.wait(dv, "dve", CM[f"x{b}_{s}"])
                ins = None
                for tau in range(NTT):
                    g = s * NTT + tau
                    dv.bn_stats(BN6A[:, g * 12:g * 12 + 6],
                                T1[:, tau * 1024:tau * 1024 + 512])
                    ins = dv.bn_stats(BN6A[:, g * 12 + 6:g * 12 + 12],
                                      T1[:, tau * 1024 + 512:(tau + 1) * 1024])
                m0 = pr.inc(ins, "dve", 1)
                pr.wait(dv, "dve", m0)
                for tau in range(NTT):
                    g = s * NTT + tau
                    ins = dv.bn_aggr(BN2A[:, g * 2:g * 2 + 2],
                                     BN6A[:, g * 12:(g + 1) * 12])
                m1_ = pr.inc(ins, "dve", 1)
                pr.wait(dv, "dve", m1_)
                m2_ = pr.inc(dv.tensor_scalar(MVA[:, 24 + s * 4:24 + s * 4 + 4],
                                              BN2A[:, s * 8 + 1:s * 8 + 8:2],
                                              EPS, None, ALU.add), "dve", 1)
                pr.wait(dv, "dve", m2_)
                rsqrt_newton(dv, MVA[:, s * 4:s * 4 + 4],
                             MVA[:, 24 + s * 4:24 + s * 4 + 4])
                m3_ = pr.inc(dv.scalar_tensor_tensor(
                    MVA[:, 8 + s * 4:8 + s * 4 + 4], BN2A[:, s * 8:s * 8 + 8:2],
                    -1.0, MVA[:, s * 4:s * 4 + 4], ALU.mult, ALU.mult), "dve", 1)
                CM[f"ln1c{b}_{s}"] = pr.inc(
                    dv.tensor_scalar(MVA[:, 16 + s * 4:16 + s * 4 + 4],
                                     MVA[:, s * 4:s * 4 + 4], 1.0, None, ALU.add),
                    "dve", 1)
                pr.wait(dv, "dve", CM[f"ln1c{b}_{s}"])
                m4_ = pr.inc(dv.tensor_mul(RS[:, 0:4], MVA[:, s * 4:s * 4 + 4],
                                           MVA[:, s * 4:s * 4 + 4]), "dve", 1)
                pr.wait(dv, "dve", m4_)
                m5_ = pr.inc(dv.tensor_mul(RS[:, 0:4], RS[:, 0:4],
                                           BN2A[:, s * 8 + 1:s * 8 + 8:2]), "dve", 1)
                pr.wait(dv, "dve", m5_)
                m6_ = pr.inc(dv.tensor_scalar(RS[:, 0:4], RS[:, 0:4], EPS, None,
                                              ALU.add), "dve", 1)
                pr.wait(dv, "dve", m6_)
                rsqrt_newton(dv, RS[:, 4:8], RS[:, 0:4])
                pr.inc(dv.tensor_mul(AC[:, s * 4:s * 4 + 4],
                                     MVA[:, s * 4:s * 4 + 4], RS[:, 4:8]),
                       "dve", 1)
                CM[f"ln1f{b}_{s}"] = pr.inc(
                    dv.tensor_mul(AC[:, 8 + s * 4:8 + s * 4 + 4],
                                  MVA[:, 8 + s * 4:8 + s * 4 + 4], RS[:, 4:8]),
                    "dve", 1)
                pr.wait(dv, "dve", CM[f"ln1f{b}_{s}"])
            dve_sec(fv)

            def fzt(ac):
                pr.wait(ac, "act", CM[f"ln1f{b}_{s}"])
                for mk in reg_free["T2"]:
                    pr.wait(ac, "act", mk)
                ins = None
                for tau in range(NTT):
                    ins = ac.activation(
                        T2[:, tau * 1024:(tau + 1) * 1024],
                        T1[:, tau * 1024:(tau + 1) * 1024], AF.Identity,
                        scale=AC[:, s * 4 + tau:s * 4 + tau + 1],
                        bias=AC[:, 8 + s * 4 + tau:8 + s * 4 + tau + 1])
                CM[f"t1{b}_{s}"] = pr.inc(ins, "act", 1)
                for mk in reg_free["Z"]:
                    pr.wait(ac, "act", mk)
                ins = None
                for tau in range(NTT):
                    g = s * NTT + tau
                    ins = ac.activation(
                        Z[:, g * 1024:(g + 1) * 1024],
                        T1[:, tau * 1024:(tau + 1) * 1024], AF.Identity,
                        scale=MVA[:, s * 4 + tau:s * 4 + tau + 1],
                        bias=MVA[:, 8 + s * 4 + tau:8 + s * 4 + tau + 1])
                CM[f"zx{b}_{s}"] = pr.inc(ins, "act", 1)
            act_sec(fzt)
            reg_free["T2"] = [CM[f"t1{b}_{s}"]]
            if not defer_x1gen:
                emit_X1gen(b, s)

        def emit_X1gen(b, s):
            def fg(dv):
                if x1_store_marks[s] is not None:
                    pr.wait(dv, "dve", x1_store_marks[s])
                pr.wait(dv, "dve", CM[f"ln1c{b}_{s}"])
                ins = None
                for tau in range(NTT):
                    g = s * NTT + tau
                    nb = MVA[:, 8 + s * 4 + tau:8 + s * 4 + tau + 1]
                    rp1 = MVA[:, 16 + s * 4 + tau:16 + s * 4 + tau + 1]
                    ins = dv.tensor_scalar(X1[:, g * 1024:(g + 1) * 1024],
                                           T1[:, tau * 1024:(tau + 1) * 1024],
                                           rp1, nb, ALU.mult, ALU.add)
                CM[f"x1g{b}_{s}"] = pr.inc(ins, "dve", 1)
            dve_sec(fg)
            if s == 1:
                reg_free["T1"] = [CM[f"x1g{b}_{s}"], CM[f"zx{b}_{s}"]]

        def emit_A_tail_sub(b, s, xn_marks):
            fills = tm_to_fm_pe(T2, [CM[f"t1{b}_{s}"]], "act")

            def fev_xn(ac):
                pr.waits(ac, "act", CONST_MARKS)
                for mk in reg_free["FMB"]:
                    pr.wait(ac, "act", mk)
                for c in range(ND):
                    bank = (4, 5)[c % 2]
                    pr.wait(ac, "act", fills[c])
                    ins = ac.activation(
                        FMB[:, c * 1024 + s * 512: c * 1024 + (s + 1) * 512],
                        psum[bank][:, 0:512], AF.Identity,
                        scale=CN1W[:, c:c + 1], bias=CN1B[:, c:c + 1])
                    mk = pr.inc(ins, "act", 1)
                    bank_free[bank] = mk
                    xn_marks.append(mk)
            act_sec(fev_xn)
            reg_free["T2"] = [pr.mark("pe")]

        def emit_A_full(b, xn_bysub):
            emit_A_chain(b, 0, defer_x1gen=False)
            emit_A_tail_sub(b, 0, xn_bysub[0])
            emit_A_chain(b, 1, defer_x1gen=False)
            emit_A_tail_sub(b, 1, xn_bysub[1])

        pending_xn = [[], []]
        emit_A_full(0, pending_xn)

        for b in range(n_blocks):
            xn_by_sub = pending_xn
            xn_marks = xn_by_sub[0] + xn_by_sub[1]
            pending_xn = [[], []]

            load_w(f"gate{b}", W_MIX[:, 0:16384],
                   s_gate.rearrange("p m k f -> p (m k f)"),
                   CM["cast_gate"], wmix_free)

            def gate_dst(s, m):
                if m < 8:
                    return B1[:, m * 1024 + s * 512: m * 1024 + (s + 1) * 512]
                return B2[:, (m - 8) * 1024 + s * 512: (m - 8) * 1024 + (s + 1) * 512]

            g1s_marks = mm_stage(
                W_MIX, lambda m, k: m * ND + k, NM_G, ND,
                lambda s, k: FMB[:, k * 1024 + s * 512: k * 1024 + (s + 1) * 512],
                "act", gate_dst, CM[f"gate{b}"], func=AF.Silu,
                pre_waits=list(reg_free["B1"]) + list(reg_free["B2"]),
                group_waits=lambda s, m: xn_by_sub[s], m_order=list(range(8)))
            gate_lo_done = pr.mark("pe")

            load_w(f"state{b}", W_MIX[:, 0:8192],
                   s_state.rearrange("p m k f -> p (m k f)"),
                   CM["cast_state"], gate_lo_done)

            g2s_marks = mm_stage(
                W_MIX, lambda m, k: m * ND + k, NM_G, ND,
                lambda s, k: FMB[:, k * 1024 + s * 512: k * 1024 + (s + 1) * 512],
                "act", gate_dst, CM[f"gate{b}"], func=AF.Silu,
                group_waits=lambda s, m: xn_by_sub[s],
                m_order=list(range(8, 16)))
            gate_last_mm = pr.mark("pe")

            def fv1(dv):
                pr.waits(dv, "dve", g1s_marks)
                pr.waits(dv, "dve", xn_marks)
                for mk in reg_free["B3"]:
                    pr.wait(dv, "dve", mk)
                CM[f"v1{b}"] = pr.inc(dv.tensor_mul(B3, FMB[:], B1), "dve", 1)
            dve_sec(fv1)

            dpre_base = pr.c["dve"]

            def fdpre_pe(t):
                pr.wait(t, "pe", CM["wms"])
                pr.waits(t, "pe", xn_marks)
                for s in range(NSUB):
                    if s > 0:
                        pr.wait(t, "pe", ("dve", dpre_base + s))
                    else:
                        pr.wait(t, "pe", bank_free[6])
                    ins = None
                    for k in range(ND):
                        ins = t.matmul(
                            psum[6][0:1, 0:512], WMS[:, k:k + 1],
                            FMB[:, k * 1024 + s * 512: k * 1024 + (s + 1) * 512],
                            start=(k == 0), stop=(k == ND - 1))
                    CM[f"dpref{b}_{s}"] = pr.inc(ins, "pe", 1)
            pe_sec(fdpre_pe)

            def fdpre_dv(dv):
                pr.wait(dv, "dve", CM["bsb"])
                for s in range(NSUB):
                    pr.wait(dv, "dve", CM[f"dpref{b}_{s}"])
                    mk = pr.inc(dv.tensor_scalar(R0[0:1, s * 512:(s + 1) * 512],
                                                 psum[6][0:1, 0:512], BSB[0:1, 1:2],
                                                 None, ALU.add), "dve", 1)
                    bank_free[6] = mk
                CM[f"dpre{b}"] = pr.mark("dve")
            dve_sec(fdpre_dv)
            assert pr.c["dve"] == dpre_base + NSUB

            def fda(ac):
                pr.wait(ac, "act", CM[f"dpre{b}"])
                m0 = pr.inc(ac.activation(R1[0:1, :], R0[0:1, :], AF.Abs), "act", 1)
                pr.wait(ac, "act", m0)
                m1_ = pr.inc(ac.activation(R2[0:1, :], R1[0:1, :], AF.Exp, scale=-1.0),
                             "act", 1)
                pr.wait(ac, "act", m1_)
                CM[f"sp{b}"] = pr.inc(
                    ac.activation(R1[0:1, :], R2[0:1, :], AF.Ln, bias=1.0), "act", 1)
                pr.wait(ac, "act", ("dve", pr.c["dve"] + 2))
                CM[f"arow{b}"] = pr.inc(
                    ac.activation(R1[0:1, :], R0[0:1, :], AF.Exp, scale=-1.0), "act", 1)
            act_sec(fda)

            def fdv2(dv):
                pr.wait(dv, "dve", CM[f"sp{b}"])
                m0 = pr.inc(dv.tensor_scalar_max(R2[0:1, :], R0[0:1, :], 0.0), "dve", 1)
                pr.wait(dv, "dve", m0)
                CM[f"delta{b}"] = pr.inc(
                    dv.tensor_add(R0[0:1, :], R2[0:1, :], R1[0:1, :]), "dve", 1)
            dve_sec(fdv2)

            bc_base = pr.c["dve"]

            def fbc_pe(t):
                pr.wait(t, "pe", CM["ones"])
                i = 0
                for bank, rm, row in [(6, f"delta{b}", R0), (7, f"arow{b}", R1)]:
                    pr.wait(t, "pe", CM[rm])
                    for s in range(NSUB):
                        if i % 2 == 1:
                            pr.wait(t, "pe", ("dve", bc_base + i))
                        else:
                            pr.wait(t, "pe", bank_free[bank])
                        ins = t.matmul(psum[bank][:, 0:512], ONES[:],
                                       row[0:1, s * 512:(s + 1) * 512],
                                       start=True, stop=True)
                        CM[f"bcf{b}_{i}"] = pr.inc(ins, "pe", 1)
                        i += 1
            pe_sec(fbc_pe)

            bc_marks = []

            def fbc_dv(dv):
                for mk in reg_free["SCBC"]:
                    pr.wait(dv, "dve", mk)
                i = 0
                for bank, dstv in [(6, D_BC), (7, A_BC)]:
                    for s in range(NSUB):
                        pr.wait(dv, "dve", CM[f"bcf{b}_{i}"])
                        mk = pr.inc(dv.tensor_copy(dstv[:, s * 512:(s + 1) * 512],
                                                   psum[bank][:, 0:512]), "dve", 1)
                        bank_free[bank] = mk
                        bc_marks.append(mk)
                        i += 1
            dve_sec(fbc_dv)
            assert pr.c["dve"] == bc_base + 4
            reg_free["SCBC"] = []

            u_marks = mm_stage(
                W_MIX, lambda m, k: m * ND + k, ND, ND,
                lambda s, k: B3[:, k * 1024 + s * 512: k * 1024 + (s + 1) * 512],
                "dve",
                lambda s, m: B1[:, m * 1024 + s * 512: m * 1024 + (s + 1) * 512],
                CM[f"state{b}"], bias_col=lambda m: CB_S[:, m:m + 1],
                pre_waits=[CM[f"v1{b}"]],
                extra_pe_waits=[CM[f"v1{b}"]] + g1s_marks)

            g2s_by_sub = [g2s_marks[0:8], g2s_marks[8:16]]
            bm_marks = mm_stage(
                W_MIX, lambda m, k: m * ND + k, ND, ND,
                lambda s, k: B2[:, k * 1024 + s * 512: k * 1024 + (s + 1) * 512],
                "act",
                lambda s, m: B3[:, m * 1024 + s * 512: m * 1024 + (s + 1) * 512],
                CM[f"state{b}"], bias_col=lambda m: CB_S[:, m:m + 1],
                pre_waits=[CM[f"v1{b}"]],
                extra_pe_waits=[CM[f"v1{b}"]],
                group_waits=lambda s, m: g2s_by_sub[s])
            state_last_mm = pr.mark("pe")

            load_w(f"outp{b}", W_MIX[:, 0:8192],
                   s_outp.rearrange("p m k f -> p (m k f)"),
                   CM["cast_outp"], state_last_mm)

            load_w(f"mlp1a{b}", W_BIG[:, 16384:24576],
                   s_mlp1[:, 8:16].rearrange("p m k f -> p (m k f)"),
                   CM["cast_mlp1"], state_last_mm)

            scan_marks = []

            def fscan(dv):
                pr.waits(dv, "dve", bc_marks)
                pr.waits(dv, "dve", u_marks)
                udm = pr.inc(dv.tensor_tensor(
                    B1[:, :].rearrange("p (c f) -> p c f", c=ND),
                    B1[:, :].rearrange("p (c f) -> p c f", c=ND),
                    D_BC[:, None, :].broadcast_to([P, ND, 1024]), ALU.mult),
                    "dve", 1)
                pr.wait(dv, "dve", udm)
                for c in range(ND):
                    pr.wait(dv, "dve", bm_marks[8 + c])
                    m0 = pr.inc(dv.tensor_tensor(BCH[c % 2],
                                                 B1[:, c * 1024:(c + 1) * 1024],
                                                 B3[:, c * 1024:(c + 1) * 1024],
                                                 ALU.mult), "dve", 1)
                    pr.wait(dv, "dve", m0)
                    init = 0.0 if b % 2 == 0 else HCAR[:, c:c + 1]
                    ins = dv.tensor_tensor_scan(B1[:, c * 1024:(c + 1) * 1024],
                                                A_BC, BCH[c % 2], init,
                                                ALU.mult, ALU.add)
                    scan_marks.append(pr.inc(ins, "dve", 1))
                if b % 2 == 0:
                    pr.waits(dv, "dve", scan_marks)
                    ins = None
                    for c in range(ND):
                        ins = dv.tensor_copy(
                            HCAR[:, c:c + 1],
                            B1[:, (c + 1) * 1024 - 1:(c + 1) * 1024])
                    CM[f"hcar{b}"] = pr.inc(ins, "dve", 1)
            dve_sec(fscan)
            if b % 2 == 0:
                scan_h_mark = CM[f"hcar{b}"]

            scan_done_dve = pr.mark("dve")
            load_w(f"mlp1b{b}", W_BIG[:, 24576:40960],
                   s_mlp1[:, 16:32].rearrange("p m k f -> p (m k f)"),
                   CM["cast_mlp1"], scan_done_dve)

            x2n_by_sub = [[], []]
            xb_last = None
            t1_free = list(reg_free["T1"])

            def emit_out_mm(s, pre):
                return mm_stage(
                    W_MIX, lambda m, k: m * ND + k, ND, ND,
                    lambda s_, k: B1[:, k * 1024 + s * 512: k * 1024 + (s + 1) * 512],
                    "act", lambda s_, m: T1[:, m * 512:(m + 1) * 512],
                    CM[f"outp{b}"], bias_col=lambda m: CB_O[:, m:m + 1],
                    pre_waits=pre, extra_pe_waits=list(scan_marks), subs=(s,))

            def emit_fcons(s, fills):
                xb_marks = []

                def fcons(dv, s=s, fills=fills, xb_marks=xb_marks):
                    pr.wait(dv, "dve", CM[f"x1g{b}_{s}"])
                    pr.wait(dv, "dve", CM[f"zx{b}_{s}"])
                    for i in range(NTT * 2):
                        tau, cg = divmod(i, 2)
                        bank = (4, 5)[cg]
                        g = s * NTT + tau
                        pr.wait(dv, "dve", fills[i])
                        sl = slice(g * 1024 + cg * 512, g * 1024 + (cg + 1) * 512)
                        sl2 = slice(tau * 1024 + cg * 512, tau * 1024 + (cg + 1) * 512)
                        pr.inc(dv.tensor_add(X1[:, sl], X1[:, sl],
                                             psum[bank][:, 0:512]), "dve", 1)
                        ins = dv.tensor_add(T2[:, sl2], Z[:, sl], psum[bank][:, 0:512])
                        mk = pr.inc(ins, "dve", 1)
                        bank_free[bank] = mk
                        xb_marks.append(mk)
                dve_sec(fcons)
                return xb_marks

            def emit_ln2_t2(s, xb_marks):
                def fv5(dv, s=s, xb_marks=xb_marks):
                    pr.waits(dv, "dve", xb_marks)
                    ins = None
                    for tau in range(NTT):
                        g = s * NTT + tau
                        dv.bn_stats(BN6[:, g * 12:g * 12 + 6],
                                    T2[:, tau * 1024:tau * 1024 + 512])
                        ins = dv.bn_stats(BN6[:, g * 12 + 6:g * 12 + 12],
                                          T2[:, tau * 1024 + 512:(tau + 1) * 1024])
                    m0 = pr.inc(ins, "dve", 1)
                    pr.wait(dv, "dve", m0)
                    for tau in range(NTT):
                        g = s * NTT + tau
                        ins = dv.bn_aggr(BN2[:, g * 2:g * 2 + 2],
                                         BN6[:, g * 12:(g + 1) * 12])
                    m1_ = pr.inc(ins, "dve", 1)
                    pr.wait(dv, "dve", m1_)
                    m2_ = pr.inc(dv.tensor_scalar(MV[:, 24 + s * 4:24 + s * 4 + 4],
                                                  BN2[:, s * 8 + 1:s * 8 + 8:2],
                                                  EPS, None, ALU.add), "dve", 1)
                    pr.wait(dv, "dve", m2_)
                    rsqrt_newton(dv, MV[:, s * 4:s * 4 + 4],
                                 MV[:, 24 + s * 4:24 + s * 4 + 4])
                    ins = None
                    for tau in range(NTT):
                        g = s * NTT + tau
                        ins = dv.tensor_scalar(
                            T2[:, tau * 1024:(tau + 1) * 1024],
                            T2[:, tau * 1024:(tau + 1) * 1024],
                            BN2[:, g * 2:g * 2 + 1],
                            MV[:, s * 4 + tau:s * 4 + tau + 1],
                            ALU.subtract, ALU.mult)
                    CM[f"t2{b}_{s}"] = pr.inc(ins, "dve", 1)
                dve_sec(fv5)

            def emit_x2n(s, fills2):
                def fev_x2n(ac, s=s, fills2=fills2):
                    pr.waits(ac, "act", CONST_MARKS)
                    for c in range(ND):
                        bank = (4, 5)[c % 2]
                        pr.wait(ac, "act", fills2[c])
                        ins = ac.activation(
                            FMB[:, c * 1024 + s * 512: c * 1024 + (s + 1) * 512],
                            psum[bank][:, 0:512], AF.Identity,
                            scale=CN2W[:, c:c + 1], bias=CN2B[:, c:c + 1])
                        mk = pr.inc(ins, "act", 1)
                        bank_free[bank] = mk
                        x2n_by_sub[s].append(mk)
                act_sec(fev_x2n)

            ssm0 = emit_out_mm(0, t1_free)
            fills_s0 = fm_to_tm_pe(T1, ssm0, "dve", 2)
            ssmT0_done = pr.mark("pe")
            xb0 = emit_fcons(0, fills_s0)
            emit_ln2_t2(0, xb0)
            xb_last = xb0[-1]

            ssm1 = emit_out_mm(1, [ssmT0_done])
            outmm1_pe = pr.mark("pe")
            load_w(f"mlp1c{b}", W_BIG[:, 8192:16384],
                   s_mlp1[:, 0:8].rearrange("p m k f -> p (m k f)"),
                   CM["cast_mlp1"], outmm1_pe)

            def mlp1_ready(m, b=b):
                if m < 8:
                    return CM[f"mlp1c{b}"]
                if m < 16:
                    return CM[f"mlp1a{b}"]
                return CM[f"mlp1b{b}"]
            mlp1_m_order = list(range(8, 32)) + list(range(0, 8))
            fills_t20 = tm_to_fm_pe(T2, [CM[f"t2{b}_0"]], "act")
            emit_x2n(0, fills_t20)
            t2T0_done = pr.mark("pe")
            fills_s1 = fm_to_tm_pe(T1, ssm1, "dve", 2)
            ssmT1_done = pr.mark("pe")
            xb1 = emit_fcons(1, fills_s1)
            emit_ln2_t2(1, xb1)
            xb_last = xb1[-1]

            m1_0 = mm_stage(
                W_BIG, lambda m, k: 64 + m * ND + k, NM_MLP, ND,
                lambda s_, k: FMB[:, k * 1024 + 0 * 512: k * 1024 + 1 * 512],
                "act", lambda s_, m: M1[:, m * 512:(m + 1) * 512],
                mlp1_ready, func=AF.Gelu,
                bias_col=lambda m: CB_M1[:, m:m + 1],
                pre_waits=list(reg_free["M1"]),
                extra_pe_waits=list(x2n_by_sub[0]) + list(reg_free["M1"]),
                subs=(0,), m_order=mlp1_m_order)

            fills_t21 = tm_to_fm_pe(T2, [CM[f"t2{b}_1"]], "act")
            emit_x2n(1, fills_t21)
            t2T1_done = pr.mark("pe")

            prev_ring = [None, None]
            ringslots = [0, 4096]

            def emit_mlp2(s, ring_free_mark, m1_marks):
                pe_base2 = pr.c["pe"]
                dve_base2 = pr.c["dve"]
                ring_marks = {}

                def fr(sy, s=s):
                    pr.wait(sy, "sync", CM["cast_mlp2"])
                    pr.wait(sy, "sync", ring_free_mark)
                    for m in range(ND):
                        slot = ringslots[m % 2]
                        if m < 2:
                            pr.wait(sy, "sync", prev_ring[m % 2])
                        else:
                            pr.wait(sy, "sync", ("pe", pe_base2 + m - 1))
                        ring_marks[m] = pr.dma(
                            sy, "sync", "ql", A_RING[:, slot:slot + 4096],
                            s_mlp2[:, m].rearrange("p k f -> p (k f)"))
                sync_sec(fr)

                def fpe2(t, s=s):
                    pr.waits(t, "pe", m1_marks)
                    for m in range(ND):
                        slot = ringslots[m % 2]
                        bank = m % 4
                        pr.wait(t, "pe", ring_marks[m])
                        if m >= 4:
                            pr.wait(t, "pe", ("dve", dve_base2 + m - 3))
                        else:
                            pr.wait(t, "pe", bank_free[bank])
                        ins = None
                        for k in range(NM_MLP):
                            ins = t.matmul(psum[bank][:, 0:512],
                                           A_RING[:, slot + k * P:slot + (k + 1) * P],
                                           M1[:, k * 512:(k + 1) * 512],
                                           start=(k == 0), stop=(k == NM_MLP - 1))
                        CM[f"m2f{b}_{s}_{m}"] = pr.inc(ins, "pe", 1)
                        prev_ring[m % 2] = CM[f"m2f{b}_{s}_{m}"]
                pe_sec(fpe2)

                mlpo_marks = []

                def fev2(dv, s=s):
                    for mk in reg_free["T2"]:
                        pr.wait(dv, "dve", mk)
                    for m in range(ND):
                        bank = m % 4
                        pr.wait(dv, "dve", CM[f"m2f{b}_{s}_{m}"])
                        ins = dv.tensor_scalar(T2[:, m * 512:(m + 1) * 512],
                                               psum[bank][:, 0:512],
                                               CB_M2[:, m:m + 1], None, ALU.add)
                        mk = pr.inc(ins, "dve", 1)
                        bank_free[bank] = mk
                        mlpo_marks.append(mk)
                dve_sec(fev2)
                assert pr.c["dve"] == dve_base2 + ND

                fills3 = fm_to_tm_pe(T2, mlpo_marks, "dve", 1)

                def fcons2(dv, s=s, fills3=fills3):
                    for i in range(NTT * 2):
                        tau, cg = divmod(i, 2)
                        bank = (4, 5)[cg]
                        g = s * NTT + tau
                        pr.wait(dv, "dve", fills3[i])
                        sl = slice(g * 1024 + cg * 512, g * 1024 + (cg + 1) * 512)
                        ins = dv.tensor_add(X1[:, sl], X1[:, sl], psum[bank][:, 0:512])
                        mk = pr.inc(ins, "dve", 1)
                        bank_free[bank] = mk
                dve_sec(fcons2)
                reg_free["T2"] = [pr.mark("pe")]
                CM[f"xmid{b}_{s}"] = pr.mark("dve")

            reg_free["T2"] = [t2T1_done]
            emit_mlp2(0, ssmT1_done, m1_0)
            mlp2_0_pe = pr.mark("pe")

            m1_1 = mm_stage(
                W_BIG, lambda m, k: 64 + m * ND + k, NM_MLP, ND,
                lambda s_, k: FMB[:, k * 1024 + 1 * 512: k * 1024 + 2 * 512],
                "act", lambda s_, m: M1[:, m * 512:(m + 1) * 512],
                mlp1_ready, func=AF.Gelu,
                bias_col=lambda m: CB_M1[:, m:m + 1],
                pre_waits=[mlp2_0_pe],
                extra_pe_waits=list(x2n_by_sub[1]) + [mlp2_0_pe],
                subs=(1,), m_order=mlp1_m_order)

            def emit_ln3(s, affno_free):
                def fv7(dv, s=s):
                    pr.wait(dv, "dve", CM[f"xmid{b}_{s}"])
                    ins = None
                    for tau in range(NTT):
                        g = s * NTT + tau
                        dv.bn_stats(BN6[:, g * 12:g * 12 + 6],
                                    X1[:, g * 1024:g * 1024 + 512])
                        ins = dv.bn_stats(BN6[:, g * 12 + 6:g * 12 + 12],
                                          X1[:, g * 1024 + 512:(g + 1) * 1024])
                    m0 = pr.inc(ins, "dve", 1)
                    pr.wait(dv, "dve", m0)
                    for tau in range(NTT):
                        g = s * NTT + tau
                        ins = dv.bn_aggr(BN2[:, g * 2:g * 2 + 2],
                                         BN6[:, g * 12:(g + 1) * 12])
                    m1_ = pr.inc(ins, "dve", 1)
                    pr.wait(dv, "dve", m1_)
                    m2_ = pr.inc(dv.tensor_scalar(MV[:, 24 + s * 4:24 + s * 4 + 4],
                                                  BN2[:, s * 8 + 1:s * 8 + 8:2],
                                                  EPS, None, ALU.add), "dve", 1)
                    pr.wait(dv, "dve", m2_)
                    rsqrt_newton(dv, MV[:, s * 4:s * 4 + 4],
                                 MV[:, 24 + s * 4:24 + s * 4 + 4])
                    CM[f"ln3c{b}_{s}"] = pr.inc(dv.scalar_tensor_tensor(
                        MV[:, 8 + s * 4:8 + s * 4 + 4], BN2[:, s * 8:s * 8 + 8:2],
                        -1.0, MV[:, s * 4:s * 4 + 4], ALU.mult, ALU.mult), "dve", 1)
                dve_sec(fv7)

                def fa8(ac, s=s, affno_free=affno_free):
                    pr.wait(ac, "act", CM[f"ln3c{b}_{s}"])
                    for mk in affno_free:
                        pr.wait(ac, "act", mk)
                    ins = None
                    for tau in range(NTT):
                        g = s * NTT + tau
                        ins = ac.activation(
                            A_FFNO[:, tau * 1024:(tau + 1) * 1024],
                            X1[:, g * 1024:(g + 1) * 1024], AF.Identity,
                            scale=MV[:, s * 4 + tau:s * 4 + tau + 1],
                            bias=MV[:, 8 + s * 4 + tau:8 + s * 4 + tau + 1])
                    CM[f"t3{b}_{s}"] = pr.inc(ins, "act", 1)
                act_sec(fa8)

            m11_pe = pr.mark("pe")
            emit_ln3(0, [m11_pe])

            emit_mlp2(1, ssmT1_done, m1_1)
            reg_free["M1"] = [pr.mark("pe")]
            reg_free["Z"] = [xb_last]
            mlp_last_pe = pr.mark("pe")
            reg_free["T1"] = [CM[f"m2f{b}_1_{ND - 1}"]]

            prev_g = [None, None]
            prev_u = [None, None]
            prev_d = [None, None]
            rgs = [8192, 9216]
            rus = [10240, 11264]
            rds = [12288, 15360]
            ringslots_d = rds

            def emit_zn(s):
                fills4 = tm_to_fm_pe(A_FFNO, [CM[f"t3{b}_{s}"]], "dve")
                zn_marks = []

                def fev_zn(dv, s=s, fills4=fills4):
                    for c in range(ND):
                        bank = (4, 5)[c % 2]
                        pr.wait(dv, "dve", fills4[c])
                        ins = dv.tensor_copy(
                            FMB[:, c * 1024 + s * 512: c * 1024 + (s + 1) * 512],
                            psum[bank][:, 0:512])
                        mk = pr.inc(ins, "dve", 1)
                        bank_free[bank] = mk
                        zn_marks.append(mk)
                dve_sec(fev_zn)
                return zn_marks, pr.mark("pe")

            def emit_gu(s, zn_marks, m1_free):
                pe_base3 = pr.c["pe"]
                act_base3 = pr.c["act"]
                dve_base3 = pr.c["dve"]

                def frg(sy, s=s):
                    pr.wait(sy, "sync", CM["cast_fg"])
                    pr.wait(sy, "sync", CM["cast_fu"])
                    pr.wait(sy, "sync", mlp_last_pe)
                    for m in range(NM_FFN):
                        sg_slot = rgs[m % 2]
                        su_slot = rus[m % 2]
                        if m < 2:
                            pr.wait(sy, "sync", prev_g[m % 2])
                        else:
                            pr.wait(sy, "sync", ("pe", pe_base3 + (m - 2) * 2 + 1))
                        CM[f"rg{b}_{s}_{m}"] = pr.dma(
                            sy, "sync", "ql", W_BIG[:, sg_slot:sg_slot + 1024],
                            s_fg[:, m].rearrange("p k f -> p (k f)"))
                        if m < 2:
                            pr.wait(sy, "sync", prev_u[m % 2])
                        else:
                            pr.wait(sy, "sync", ("pe", pe_base3 + (m - 2) * 2 + 2))
                        CM[f"ru{b}_{s}_{m}"] = pr.dma(
                            sy, "sync", "ql", W_BIG[:, su_slot:su_slot + 1024],
                            s_fu[:, m].rearrange("p k f -> p (k f)"))
                sync_sec(frg)

                def fpe3(t, s=s, act_base3=act_base3, dve_base3=dve_base3):
                    pr.waits(t, "pe", m1_free)
                    for m in range(NM_FFN):
                        sg_slot = rgs[m % 2]
                        su_slot = rus[m % 2]
                        bg = m % 2
                        bu = 2 + m % 2
                        pr.wait(t, "pe", CM[f"rg{b}_{s}_{m}"])
                        if m < 2:
                            pr.wait(t, "pe", bank_free[bg])
                            pr.waits(t, "pe", zn_marks)
                        else:
                            pr.wait(t, "pe", ("act", act_base3 + m - 1))
                        ins = None
                        for k in range(ND):
                            ins = t.matmul(
                                psum[bg][:, 0:512],
                                W_BIG[:, sg_slot + k * P:sg_slot + (k + 1) * P],
                                FMB[:, k * 1024 + s * 512: k * 1024 + (s + 1) * 512],
                                start=(k == 0), stop=(k == ND - 1))
                        CM[f"fgf{b}_{s}_{m}"] = pr.inc(ins, "pe", 1)
                        pr.wait(t, "pe", CM[f"ru{b}_{s}_{m}"])
                        if m < 2:
                            pr.wait(t, "pe", bank_free[bu])
                        else:
                            pr.wait(t, "pe", ("dve", dve_base3 + (m - 2) * 2 + 1))
                        for k in range(ND):
                            ins = t.matmul(
                                psum[bu][:, 0:512],
                                W_BIG[:, su_slot + k * P:su_slot + (k + 1) * P],
                                FMB[:, k * 1024 + s * 512: k * 1024 + (s + 1) * 512],
                                start=(k == 0), stop=(k == ND - 1))
                        CM[f"fuf{b}_{s}_{m}"] = pr.inc(ins, "pe", 1)
                pe_sec(fpe3)

                def fact3(ac, s=s, dve_base3=dve_base3):
                    for m in range(NM_FFN):
                        bg = m % 2
                        pr.wait(ac, "act", CM[f"fgf{b}_{s}_{m}"])
                        if m > 0:
                            pr.wait(ac, "act", ("dve", dve_base3 + (m - 1) * 2 + 2))
                        elif CM.get(f"hlast{b}_{1 - s}") is not None and s == 1:
                            pr.wait(ac, "act", CM[f"hlast{b}_0"])
                        ins = ac.activation(A_FS[:, 0:512], psum[bg][:, 0:512],
                                            AF.Silu, scale=QS)
                        CM[f"fsg{b}_{s}_{m}"] = pr.inc(ins, "act", 1)
                act_sec(fact3)
                assert pr.c["act"] == act_base3 + NM_FFN

                h_marks = []

                def fdve3(dv, s=s):
                    for m in range(NM_FFN):
                        bu = 2 + m % 2
                        pr.wait(dv, "dve", CM[f"fuf{b}_{s}_{m}"])
                        if s == 1 and m == 0:
                            pr.wait(dv, "dve", CM[f"fdf{b}_0_{ND - 1}"])
                        if h_marks:
                            pr.wait(dv, "dve", h_marks[-1])
                        m0 = pr.inc(dv.tensor_scalar(A_FS[:, 512:1024],
                                                     psum[bu][:, 0:512], QS, None,
                                                     ALU.mult), "dve", 1)
                        pr.wait(dv, "dve", m0)
                        pr.wait(dv, "dve", CM[f"fsg{b}_{s}_{m}"])
                        ins = dv.tensor_mul(M1[:, m * 512:(m + 1) * 512],
                                            A_FS[:, 0:512], A_FS[:, 512:1024])
                        mk = pr.inc(ins, "dve", 1)
                        h_marks.append(mk)
                dve_sec(fdve3)
                assert pr.c["dve"] == dve_base3 + 2 * NM_FFN
                CM[f"hlast{b}_{s}"] = h_marks[-1]
                prev_g[0] = CM[f"fgf{b}_{s}_{NM_FFN - 2}"]
                prev_g[1] = CM[f"fgf{b}_{s}_{NM_FFN - 1}"]
                prev_u[0] = CM[f"fuf{b}_{s}_{NM_FFN - 2}"]
                prev_u[1] = CM[f"fuf{b}_{s}_{NM_FFN - 1}"]
                return h_marks

            def emit_down(s, h_marks, extra_pe=()):
                pe_base4 = pr.c["pe"]

                def frd(sy, s=s):
                    pr.wait(sy, "sync", CM["cast_fd"])
                    pr.wait(sy, "sync", mlp_last_pe)
                    for m in range(ND):
                        slot = rds[m % 2]
                        if m < 2:
                            pr.wait(sy, "sync", prev_d[m % 2])
                        else:
                            pr.wait(sy, "sync", ("pe", pe_base4 + m - 1))
                        CM[f"rd{b}_{s}_{m}"] = pr.dma(
                            sy, "sync", "ql", W_BIG[:, slot:slot + 3072],
                            s_fd[:, m].rearrange("p k f -> p (k f)"))
                sync_sec(frd)

                act_base4 = pr.c["act"]

                def fpe4(t, s=s, act_base4=act_base4):
                    pr.waits(t, "pe", h_marks)
                    for mk in extra_pe:
                        pr.wait(t, "pe", mk)
                    for m in range(ND):
                        slot = rds[m % 2]
                        bank = m % 4
                        pr.wait(t, "pe", CM[f"rd{b}_{s}_{m}"])
                        if m >= 4:
                            pr.wait(t, "pe", ("act", act_base4 + m - 3))
                        else:
                            pr.wait(t, "pe", bank_free[bank])
                        ins = None
                        for k in range(NM_FFN):
                            ins = t.matmul(psum[bank][:, 0:512],
                                           W_BIG[:, slot + k * P:slot + (k + 1) * P],
                                           M1[:, k * 512:(k + 1) * 512],
                                           start=(k == 0), stop=(k == NM_FFN - 1))
                        CM[f"fdf{b}_{s}_{m}"] = pr.inc(ins, "pe", 1)
                pe_sec(fpe4)
                prev_d[0] = CM[f"fdf{b}_{s}_{ND - 2}"]
                prev_d[1] = CM[f"fdf{b}_{s}_{ND - 1}"]
                return act_base4

            def emit_fev4(s, act_base4):
                ffno_marks = []

                def fev4(ac, s=s):
                    for m in range(ND):
                        bank = m % 4
                        pr.wait(ac, "act", CM[f"fdf{b}_{s}_{m}"])
                        ins = ac.activation(A_FFNO[:, m * 512:(m + 1) * 512],
                                            psum[bank][:, 0:512], AF.Identity,
                                            scale=QS)
                        mk = pr.inc(ins, "act", 1)
                        bank_free[bank] = mk
                        ffno_marks.append(mk)
                act_sec(fev4)
                assert pr.c["act"] == act_base4 + ND
                return ffno_marks

            def emit_ffn_tail(s, ffno_marks):
                fills5 = fm_to_tm_pe(A_FFNO, ffno_marks, "dve", 1)
                xo_marks = []

                def fcons3(dv, s=s, fills5=fills5, xo_marks=xo_marks):
                    for i in range(NTT * 2):
                        tau, cg = divmod(i, 2)
                        bank = (4, 5)[cg]
                        g = s * NTT + tau
                        pr.wait(dv, "dve", fills5[i])
                        sl = slice(g * 1024 + cg * 512, g * 1024 + (cg + 1) * 512)
                        ins = dv.tensor_add(X1[:, sl], X1[:, sl], psum[bank][:, 0:512])
                        mk = pr.inc(ins, "dve", 1)
                        bank_free[bank] = mk
                        xo_marks.append(mk)
                dve_sec(fcons3)

                def fst(gp, s=s, xo_marks=xo_marks):
                    pr.waits(gp, "pool", xo_marks)
                    x1_store_marks[s] = pr.dma(
                        gp, "pool", "qs",
                        y_out[b, s].rearrange("p tt d -> p (tt d)"),
                        X1[:, s * 4096:(s + 1) * 4096])
                gp_sec(fst)

            zn0, zn0_pe = emit_zn(0)
            emit_ln3(1, [zn0_pe])
            h0 = emit_gu(0, zn0, reg_free["M1"])
            zn1, _ = emit_zn(1)
            d0base = emit_down(0, h0)
            ffno0 = emit_fev4(0, d0base)
            h1 = emit_gu(1, zn1, [])
            ffn_gu_done = pr.mark("pe")
            emit_ffn_tail(0, ffno0)
            if b + 1 < n_blocks:
                emit_A_chain(b + 1, 0, defer_x1gen=True)
                emit_X1gen(b + 1, 0)
            d1base = emit_down(1, h1)
            ffno1 = emit_fev4(1, d1base)
            if b + 1 < n_blocks:
                wmix_free_next = pr.mark("pe")
                reg_free["FMB"] = [ffn_gu_done]
                emit_A_tail_sub(b + 1, 0, pending_xn[0])
            emit_ffn_tail(1, ffno1)
            reg_free["M1"] = [pr.mark("pe")]
            if b + 1 < n_blocks:
                emit_A_chain(b + 1, 1, defer_x1gen=True)
                emit_X1gen(b + 1, 1)
                emit_A_tail_sub(b + 1, 1, pending_xn[1])

            endpe = pr.mark("pe")
            enddve = pr.mark("dve")
            reg_free["B1"] = [endpe, enddve]
            reg_free["B2"] = [endpe, enddve]
            reg_free["B3"] = [endpe, enddve]
            reg_free["SCBC"] = [endpe, enddve]
            if b + 1 < n_blocks:
                wmix_free = wmix_free_next
            else:
                wmix_free = endpe

        def fin(gp):
            for i in range(4):
                gp.wait_ge(pr.h[f"qs{i}"], pr.c[f"qs{i}"])
        gp_sec(fin)

    return nc



def _tile_pm(w, mc, kc):
    K, M = w.shape
    assert K == kc * P and M == mc * P
    return np.ascontiguousarray(w.reshape(kc, P, mc, P).transpose(1, 2, 0, 3))


def _cols(v):
    n = v.shape[0] // P
    return np.ascontiguousarray(v.reshape(n, P).T)


def _x_layout(x):
    return np.ascontiguousarray(
        x.reshape(NB, NSUB, NTT, P, D).transpose(0, 1, 3, 2, 4))


def _y_unlayout(y):
    return np.asarray(y).transpose(0, 1, 3, 2, 4).reshape(NSEQ, TSEQ, D)


def make_inputs(L0_attn_x, L0_attn_norm1_w, L0_attn_norm1_b, L0_attn_gate_w,
                L0_attn_gate_b, L0_attn_state_w, L0_attn_state_b, L0_attn_out_w,
                L0_attn_out_b, L0_attn_norm2_w, L0_attn_norm2_b, L0_attn_mlp_w1,
                L0_attn_mlp_b1, L0_attn_mlp_w2, L0_attn_mlp_b2, L0_ffn_w_gate,
                L0_ffn_w_up, L0_ffn_w_down, n_cores=8):
    x = np.asarray(L0_attn_x, dtype=np.float32)
    common = {
        "w_gate": _tile_pm(np.asarray(L0_attn_gate_w, np.float32), NM_G, ND),
        "w_state": _tile_pm(np.asarray(L0_attn_state_w, np.float32), ND, ND),
        "w_outp": _tile_pm(np.asarray(L0_attn_out_w, np.float32), ND, ND),
        "w_mlp1": _tile_pm(np.asarray(L0_attn_mlp_w1, np.float32), NM_MLP, ND),
        "w_mlp2": _tile_pm(np.asarray(L0_attn_mlp_w2, np.float32), ND, NM_MLP),
        "w_fg": _tile_pm(np.ascontiguousarray(np.asarray(L0_ffn_w_gate, np.int32).T),
                         NM_FFN, ND),
        "w_fu": _tile_pm(np.ascontiguousarray(np.asarray(L0_ffn_w_up, np.int32).T),
                         NM_FFN, ND),
        "w_fd": _tile_pm(np.ascontiguousarray(np.asarray(L0_ffn_w_down, np.int32).T),
                         ND, NM_FFN),
        "c_gate_b": _cols(np.asarray(L0_attn_gate_b, np.float32)),
        "c_state_b": _cols(np.asarray(L0_attn_state_b, np.float32)),
        "c_out_b": _cols(np.asarray(L0_attn_out_b, np.float32)),
        "c_mlp_b1": _cols(np.asarray(L0_attn_mlp_b1, np.float32)),
        "c_mlp_b2": _cols(np.asarray(L0_attn_mlp_b2, np.float32)),
        "c_n1w": _cols(np.asarray(L0_attn_norm1_w, np.float32)),
        "c_n1b": _cols(np.asarray(L0_attn_norm1_b, np.float32)),
        "c_n2w": _cols(np.asarray(L0_attn_norm2_w, np.float32)),
        "c_n2b": _cols(np.asarray(L0_attn_norm2_b, np.float32)),
        "r_state_b": np.ascontiguousarray(
            np.asarray(L0_attn_state_b, np.float32).reshape(1, D)),
        "c_ident": np.eye(P, dtype=np.float32),
        "c_ones": np.ones((1, P), dtype=np.float32),
    }
    in_maps = []
    for c in range(n_cores):
        m = dict(common)
        m["x_in"] = _x_layout(x[c * NSEQ:(c + 1) * NSEQ])
        in_maps.append(m)
    return in_maps


def kernel(**inputs):
    in_maps = make_inputs(**inputs)
    nc = build_program()
    res = run_bass_kernel_spmd(nc, in_maps, core_ids=list(range(8)))
    return np.concatenate([_y_unlayout(r["y_out"]) for r in res.results], axis=0)
